# version 07
# baseline (speedup 1.0000x reference)
import numpy as np
import ml_dtypes
from contextlib import ExitStack

import concourse.bass as bass
import concourse.mybir as mybir
from concourse.bass_utils import run_bass_kernel_spmd
from concourse.bass_core_rust import IndirectOffsetOnAxis

F32 = mybir.dt.float32
BF16 = mybir.dt.bfloat16
I32 = mybir.dt.int32
AF = mybir.ActivationFunctionType
ALU = mybir.AluOpType
AX = mybir.AxisListType

B, T, D = 2, 8192, 1024
WIN = 2048
C = 512
NCH = WIN // C
NT = WIN // 128
G, NQ, NK, H, DH = 4, 1024, 512, 8, 64
DI = H * DH
QCAP = 352
QS = G * QCAP
KCAP = 256
KS = 4 * KCAP
PADW = 512
BIS_IT = 40
AUGW = 1032
SHIFT = -24.0
RG = [[0, 1, 2, 3], [4, 5, 6, 7]]
BIGIDX = float(1 << 22)

PHASE = 3


class Prog:
    ENGINES = ("tensor", "vector", "scalar", "gpsimd", "sync")

    def __init__(self, nc, sems):
        self.nc = nc
        self.sems = sems
        self.counts = {k: 0 for k in sems}
        self.active = None
        self.marks = {}

    def reset(self):
        self.counts = {k: 0 for k in self.sems}
        self.marks = {}

    def _do(self, eng, opname, *args, inc=(), **kw):
        if self.active == eng:
            instr = getattr(getattr(self.nc, eng), opname)(*args, **kw)
            for (s, n) in inc:
                instr = instr.then_inc(self.sems[s], n)
        for (s, n) in inc:
            self.counts[s] += n

    def wait(self, eng, sem, val=None):
        v = self.counts[sem] if val is None else val
        if self.active == eng:
            getattr(self.nc, eng).wait_ge(self.sems[sem], v)

    def mark(self, name):
        self.marks[name] = dict(self.counts)

    def wmark(self, eng, name, *semlist):
        for s in semlist:
            if self.active == eng:
                getattr(self.nc, eng).wait_ge(self.sems[s], self.marks[name][s])

    def __getattr__(self, name):
        if name in Prog.ENGINES:
            prog = self

            class _E:
                def __getattr__(self, op):
                    def f(*a, inc=(), **kw):
                        prog._do(name, op, *a, inc=inc, **kw)
                    return f
            return _E()
        raise AttributeError(name)


def build_nc():
    nc = bass.Bass("TRN2", target_bir_lowering=False)
    dt = nc.dram_tensor

    x_win = dt("x_win", [WIN + 1, D], F32, kind="ExternalInput")
    WwT = dt("WwT", [D, D], F32, kind="ExternalInput")
    WrT = dt("WrT", [D, D], F32, kind="ExternalInput")
    decay = dt("decay", [1, 1], F32, kind="ExternalInput")
    log_alpha = dt("log_alpha", [1, 1], F32, kind="ExternalInput")
    w8in = dt("w8in", [D, 8], F32, kind="ExternalInput")
    wqT = dt("wqT", [D, DI], F32, kind="ExternalInput")
    wkvT = dt("wkvT", [D, 2 * DI], F32, kind="ExternalInput")
    woutT = dt("woutT", [DI, D], F32, kind="ExternalInput")
    nk_in = dt("nk_in", [128, 4], F32, kind="ExternalInput")
    nv_in = dt("nv_in", [1, 8 * 65], F32, kind="ExternalInput")
    null_in = dt("null_in", [128, D], F32, kind="ExternalInput")
    DIFFS_T = dt("DIFFS_T", [C, C], F32, kind="ExternalInput")
    CAUSAL_T = dt("CAUSAL_T", [C, C], F32, kind="ExternalInput")
    IMODBC = dt("IMODBC", [128, WIN], F32, kind="ExternalInput")
    IWINBC = dt("IWINBC", [128, WIN], F32, kind="ExternalInput")
    REV_COL = dt("REV_COL", [128, 4], F32, kind="ExternalInput")
    EXPW = dt("EXPW", [1, 4], F32, kind="ExternalInput")
    MASKW = dt("MASKW", [1, 4], F32, kind="ExternalInput")
    TRIS_F = dt("TRIS_F", [128, 128], F32, kind="ExternalInput")
    ONES_ROW_F = dt("ONES_ROW_F", [1, 128], F32, kind="ExternalInput")
    ONES_COL_B = dt("ONES_COL_B", [128, 1], BF16, kind="ExternalInput")
    ONES_COL_F = dt("ONES_COL_F", [128, 1], F32, kind="ExternalInput")
    SEL16 = dt("SEL16", [128, 8], F32, kind="ExternalInput")
    SEL16T = dt("SEL16T", [8, 128], F32, kind="ExternalInput")
    SELW32 = dt("SELW32", [128, 32], F32, kind="ExternalInput")
    PREVW = dt("PREVW", [32, 8], F32, kind="ExternalInput")
    NQCAPC = dt("NQCAPC", [128, 8], F32, kind="ExternalInput")
    TGT = dt("TGT", [128, 1], F32, kind="ExternalInput")
    IBLK4 = dt("IBLK4", [128, 32], F32, kind="ExternalInput")
    IBLK128 = dt("IBLK128", [128, 1024], BF16, kind="ExternalInput")
    IOTA_COL = dt("IOTA_COL", [128, 1], F32, kind="ExternalInput")
    IDENT8 = dt("IDENT8", [8, 8], F32, kind="ExternalInput")
    IDENT32 = dt("IDENT32", [32, 32], F32, kind="ExternalInput")
    WOFF = dt("WOFF", [1, 1], F32, kind="ExternalInput")
    POSSEL = dt("POSSEL", [32, 20], F32, kind="ExternalInput")
    GSELQ = dt("GSELQ", [128, 8], F32, kind="ExternalInput")
    GSELK = dt("GSELK", [128, 8], F32, kind="ExternalInput")
    KBASE20 = dt("KBASE20", [128, 20], F32, kind="ExternalInput")
    COMBSEL = dt("COMBSEL", [32, 12], F32, kind="ExternalInput")
    COMBSEL2 = dt("COMBSEL2", [32, 12], F32, kind="ExternalInput")
    KBASE12 = dt("KBASE12", [128, 12], F32, kind="ExternalInput")
    PREFIXW = dt("PREFIXW", [32, 32], F32, kind="ExternalInput")
    CNTSEL = dt("CNTSEL", [32, 20], F32, kind="ExternalInput")
    NQ32 = dt("NQ32", [32, 1], F32, kind="ExternalInput")

    out_y = dt("out_y", [WIN, D], F32, kind="ExternalOutput")
    dbg_y1 = dt("dbg_y1", [WIN, D], F32, kind="ExternalOutput")
    dbg_z = dt("dbg_z", [WIN + 1, D], BF16, kind="ExternalOutput")
    dbg_reads = dt("dbg_reads", [WIN, D], BF16, kind="ExternalOutput")
    dbg_wst = dt("dbg_wst", [D, D], BF16, kind="ExternalOutput")
    dbg_aseg = dt("dbg_aseg", [D, D], BF16, kind="ExternalOutput")
    dbg_mix = dt("dbg_mix", [128, 6144], BF16, kind="ExternalOutput")
    dbg_s = dt("dbg_s", [128, 32], F32, kind="ExternalOutput")

    z_t_dram = dt("z_t_dram", [WIN + 1, D], BF16)
    reads_dram = dt("reads_dram", [WIN, D], BF16)
    y1_dram = dt("y1_dram", [WIN, D], F32)
    z2lo_dram = dt("z2lo_dram", [WIN, D], BF16)
    aseg_bi = dt("aseg_bi", [D, D], BF16)
    aseg_bo = dt("aseg_bo", [4 * D, D], BF16)
    wst_dram = dt("wst_dram", [D, D], BF16)
    logit_bi = dt("logit_bi", [8, WIN], F32)
    logit_bo = dt("logit_bo", [32, WIN], F32)
    z2aug_bi = dt("z2aug_bi", [WIN, AUGW], BF16)
    z2aug_bo = dt("z2aug_bo", [4 * WIN, AUGW], BF16)
    pos_bi = dt("pos_bi", [8, PADW], F32)
    pos_bo = dt("pos_bo", [32, PADW], F32)
    qcomp_dram = dt("qcomp_dram", [QS, D], BF16)
    kvcomp_dram = dt("kvcomp_dram", [KS, D], BF16)
    qlog_dram = dt("qlog_dram", [QS, 1], F32)
    o_bi = dt("o_bi", [QS, D], BF16)
    o_bo = dt("o_bo", [4 * QS, D], BF16)
    winbuf = dt("winbuf", [WIN, D], F32)

    al = ExitStack()
    sbf = lambda n, s, d: al.enter_context(nc.sbuf_tensor(n, s, d))
    pst = lambda n, s: al.enter_context(nc.psum_tensor(n, s, F32))

    with al:
        z_f = [sbf(f"z_f{i}", [128, WIN + 16], BF16) for i in range(8)]
        z_fs = [sbf(f"z_fs{i}", [128, WIN + 16], BF16) for i in range(8)]
        gpatw = sbf("gpatw", [128, WIN], BF16)

        AR2N = 47104
        AR2 = sbf("AR2", [128, AR2N], BF16)

        def a2(off, w):
            return AR2[:, off:off + w]

        WwT_b = [a2(1024 * i, 1024) for i in range(8)]
        Wst = [[a2(8192 + 8192 * p + 1024 * i, 1024) for i in range(8)] for p in range(2)]
        wk_sc = [a2(24576 + 1024 * i, 1024) for i in range(4)]
        v_sb = [a2(28672 + 1024 * i, 1024) for i in range(4)]
        SMt = [a2(32768 + 512 * i, 512) for i in range(4)]
        M_T = [a2(34816 + 512 * i, 512) for i in range(4)]
        rd_t = [a2(36864 + 1024 * i, 1024) for i in range(2)]
        WrT_b = [a2(38912 + 1024 * i, 1024) for i in range(8)]
        readsf = [a2(1024 * i, 1024) for i in range(8)]
        wstT_b = [a2(8192 + 1024 * i, 1024) for i in range(8)]
        m2t = [a2(24576 + 1024 * i, 1024) for i in range(8)]
        zfg = [a2(32768 + 128 * i, 128) for i in range(8)]
        wq_b = [a2(512 * i, 512) for i in range(8)]
        wkv_b = [a2(4096 + 1024 * i, 1024) for i in range(8)]
        wout_b = [a2(12288 + 1024 * i, 1024) for i in range(4)]
        vtok = [a2(16384 + 520 * i, 520) for i in range(8)]
        expT = [a2(20544 + 1408 * i, 1408) for i in range(8)]
        oT = [a2(31808 + 1408 * i, 1408) for i in range(4)]
        recb = a2(37440, 1408)[0:64, :]
        grow = [a2(38848 + 1032 * i, AUGW) for i in range(3)]
        kvf = [a2(41944 + 1024 * i, 1024) for i in range(4)]

        xs = [sbf(f"xs{i}", [128, D], F32) for i in range(2)]
        f1 = sbf("f1", [128, D], F32)
        gen_tok = sbf("gen_tok", [128, D], BF16)
        stat = sbf("stat", [128, 16], F32)
        scal = sbf("scal", [128, 16], F32)
        tiny = sbf("tiny", [1, 64], F32)
        rowb = sbf("rowb", [1, 64], F32)
        gw4 = sbf("gw4", [128, 4], F32)
        logit_tok = sbf("logit_tok", [128, 8 * NT], F32)
        bw = sbf("bw", [128, 16], F32)
        cnt32 = sbf("cnt32", [32, 1], F32)
        cntb32 = sbf("cntb32", [128, 32], F32)
        selr = sbf("selr", [128, 16 * NT], F32)
        offrow = sbf("offrow", [1, 8], F32)
        offib = sbf("offib", [128, 8], F32)
        taub = sbf("taub", [128, 8], F32)
        pose = sbf("pose", [128, 48], F32)
        ehi = sbf("ehi", [128, 32], F32)
        idx_f = sbf("idx_f", [128, 24], F32)
        idx_i = sbf("idx_i", [128, 24], I32)

        AR4N = 6016
        AR4 = sbf("AR4", [128, AR4N], F32)

        def a4(off, w):
            return AR4[:, off:off + w]

        logits_sb = a4(0, 2048)[0:8, :]
        ocomb = [a4(1024, 1024), a4(2048, 1024)]
        bis = a4(2048, 512)
        elo = a4(2560, 1024)
        possb = a4(3584, 1024)[0:32, :]
        sumrow = a4(4608, 1408)[0:1, :]
        blk16 = a4(3072, 128)

        tris = sbf("tris", [128, 128], F32)
        onesr = sbf("onesr", [1, 128], F32)
        onescb = sbf("onescb", [128, 1], BF16)
        onescf = sbf("onescf", [128, 1], F32)
        sel16 = sbf("sel16", [128, 8], F32)
        sel16t = sbf("sel16t", [8, 128], F32)
        selw32 = sbf("selw32", [128, 32], F32)
        prevw = sbf("prevw", [32, 8], F32)
        nqcap = sbf("nqcap", [128, 8], F32)
        tgt = sbf("tgt", [128, 1], F32)
        iblk4 = sbf("iblk4", [128, 32], F32)
        iblk128 = sbf("iblk128", [128, 1024], BF16)
        iota_c = sbf("iota_c", [128, 1], F32)
        ident8 = sbf("ident8", [8, 8], F32)
        ident32 = sbf("ident32", [32, 32], F32)
        rev4 = sbf("rev4", [128, 4], F32)
        nullb = a4(0, 1024)
        nkb = sbf("nkb", [128, 4], BF16)
        nvb = sbf("nvb", [1, 8 * 65], BF16)
        possel = sbf("possel", [32, 20], F32)
        gselq = sbf("gselq", [128, 8], F32)
        gselk = sbf("gselk", [128, 8], F32)
        kbase20 = sbf("kbase20", [128, 20], F32)
        combsel = sbf("combsel", [32, 12], F32)
        combsel2 = sbf("combsel2", [32, 12], F32)
        kbase12 = sbf("kbase12", [128, 12], F32)
        prefixw = sbf("prefixw", [32, 32], F32)
        cntsel = sbf("cntsel", [32, 20], F32)
        nq32 = sbf("nq32", [32, 1], F32)
        cnt20b = sbf("cnt20b", [128, 20], F32)
        valid20 = sbf("valid20", [128, 20], F32)
        idxq_i = sbf("idxq_i", [128, 20], I32)
        cm_f = sbf("cm_f", [128, 12], F32)
        cm_i = sbf("cm_i", [128, 12], I32)
        cm2_i = sbf("cm2_i", [128, 12], I32)
        snapw = sbf("snapw", [1, 16], F32)

        w8hi = sbf("w8hi", [128, 64], BF16)
        w8lo = sbf("w8lo", [128, 64], BF16)
        expN = a2(38848, QS)[0:1, :]
        sig_kv = sbf("sig_kv", [128, 8], F32)
        sig_q = sbf("sig_q", [128, 12], F32)
        kbias = sbf("kbias", [128, 8], F32)
        zt_eg = [a2(34816 + 1024 * i, 1024) for i in range(2)]
        kf = [z_fs[i].ap()[:, 0:KS + 1] for i in range(4)]
        qh = [z_fs[4 + i].ap()[:, 0:QS] for i in range(4)]
        qf = [z_f[i].ap()[:, 0:QS] for i in range(8)]

        ps0 = pst("ps0", [128, 1024])
        ps1 = pst("ps1", [128, 1024])
        ps2 = pst("ps2", [128, 1024])
        ps3 = pst("ps3", [128, 1024])

        sems = {k: al.enter_context(nc.semaphore(k))
                for k in ["ld", "lt", "lx", "stz", "str", "ste", "stw", "gp", "gs", "gz",
                          "lr", "lw", "gz1", "gq", "gqb", "gqc", "sw", "pool", "so0", "so1", "pe", "dve", "act", "cc", "dbg", "dbg2"]}
        P = Prog(nc, sems)

        def dve_edge():
            P.wait("vector", "dve")

        def ln_tile(src_f32, rn, out_bf):
            P.vector.tensor_reduce(stat[0:rn, 0:1], src_f32[0:rn, :], AX.X, ALU.add,
                                   inc=[("dve", 1)])
            P.wait("scalar", "dve")
            P.scalar.activation(f1[0:rn, :], src_f32[0:rn, :], AF.Square, inc=[("act", 1)])
            P.wait("vector", "act")
            P.vector.tensor_reduce(stat[0:rn, 1:2], f1[0:rn, :], AX.X, ALU.add, inc=[("dve", 1)])
            dve_edge()
            P.vector.tensor_scalar(stat[0:rn, 2:3], stat[0:rn, 0:1], 1.0 / D, None, ALU.mult,
                                   inc=[("dve", 1)])
            dve_edge()
            P.vector.tensor_scalar(stat[0:rn, 3:4], stat[0:rn, 1:2], 1.0 / D, None, ALU.mult,
                                   inc=[("dve", 1)])
            dve_edge()
            P.vector.tensor_tensor(stat[0:rn, 4:5], stat[0:rn, 2:3], stat[0:rn, 2:3], ALU.mult,
                                   inc=[("dve", 1)])
            dve_edge()
            P.vector.tensor_tensor(stat[0:rn, 5:6], stat[0:rn, 3:4], stat[0:rn, 4:5],
                                   ALU.subtract, inc=[("dve", 1)])
            P.wait("scalar", "dve")
            P.scalar.activation(stat[0:rn, 6:7], stat[0:rn, 5:6], AF.Ln, bias=scal[0:rn, 8:9],
                                inc=[("act", 1)])
            P.wait("scalar", "act")
            P.scalar.activation(stat[0:rn, 7:8], stat[0:rn, 6:7], AF.Exp, scale=-0.5,
                                inc=[("act", 1)])
            P.wait("vector", "act")
            P.vector.tensor_scalar(out_bf[0:rn, :], src_f32[0:rn, :], stat[0:rn, 2:3],
                                   stat[0:rn, 7:8], ALU.subtract, op1=ALU.mult, inc=[("dve", 1)])

        def program():
            LG = scal[:, 5:6]
            ALPHA = scal[:, 6:7]
            GC = scal[:, 7:8]
            WOFFB = scal[:, 14:15]
            WJ = [stat[:, 8 + j:9 + j] for j in range(4)]

            P.vector.memset(tiny[:], 0.0, inc=[("dve", 1)])
            P.vector.memset(rowb[:], 0.0, inc=[("dve", 1)])
            P.vector.memset(bw[:], 0.0, inc=[("dve", 1)])
            P.vector.memset(stat[:], 0.0, inc=[("dve", 1)])
            P.vector.memset(scal[:], 0.0, inc=[("dve", 1)])
            P.wait("sync", "dve")
            for (dst, src) in [(tris, TRIS_F), (onesr, ONES_ROW_F), (onescb, ONES_COL_B),
                               (onescf, ONES_COL_F), (sel16, SEL16), (sel16t, SEL16T),
                               (selw32, SELW32), (prevw, PREVW), (nqcap, NQCAPC),
                               (tgt, TGT), (iblk4, IBLK4), (iblk128, IBLK128),
                               (iota_c, IOTA_COL), (ident8, IDENT8), (ident32, IDENT32),
                               (rev4, REV_COL), (possel, POSSEL), (gselq, GSELQ),
                               (gselk, GSELK), (kbase20, KBASE20), (combsel, COMBSEL),
                               (combsel2, COMBSEL2), (kbase12, KBASE12),
                               (prefixw, PREFIXW), (nq32, NQ32),
                               (cntsel, CNTSEL)]:
                P.sync.dma_start(out=dst[:], in_=src[:], inc=[("ld", 16)])
            P.sync.dma_start(out=tiny[0:1, 0:1], in_=decay[:], inc=[("ld", 16)])
            P.sync.dma_start(out=tiny[0:1, 1:2], in_=log_alpha[:], inc=[("ld", 16)])
            P.sync.dma_start(out=tiny[0:1, 8:12], in_=EXPW[:], inc=[("ld", 16)])
            P.sync.dma_start(out=tiny[0:1, 12:16], in_=MASKW[:], inc=[("ld", 16)])
            P.sync.dma_start(out=tiny[0:1, 30:31], in_=WOFF[:], inc=[("ld", 16)])
            for i in range(8):
                for (dsts, srcs, w) in [(WwT_b, WwT, D), (WrT_b, WrT, D)]:
                    P.wait("sync", "dve")
                    P.sync.dma_start(out=xs[0][:, 0:w], in_=srcs[128 * i:128 * (i + 1), :],
                                     inc=[("ld", 16)])
                    P.wait("vector", "ld")
                    P.vector.tensor_copy(dsts[i][:], xs[0][:, 0:w], inc=[("dve", 1)])
            for i in range(8):
                P.wait("sync", "dve")
                P.sync.dma_start(out=xs[0][:, 0:8], in_=w8in[128 * i:128 * (i + 1), :],
                                 inc=[("ld", 16)])
                P.wait("vector", "ld")
                P.vector.tensor_copy(w8hi[:, 8 * i:8 * (i + 1)], xs[0][:, 0:8], inc=[("dve", 1)])
                dve_edge()
                P.vector.tensor_tensor(w8lo[:, 8 * i:8 * (i + 1)], xs[0][:, 0:8],
                                       w8hi[:, 8 * i:8 * (i + 1)], ALU.subtract, inc=[("dve", 1)])
            for et in range(8):
                P.vector.memset(Wst[0][et][:], 0.0, inc=[("dve", 1)])
            P.wait("scalar", "ld")
            P.scalar.activation(tiny[0:1, 2:3], tiny[0:1, 0:1], AF.Exp, scale=-1.0,
                                inc=[("act", 1)])
            P.wait("vector", "act")
            P.vector.tensor_scalar(tiny[0:1, 3:4], tiny[0:1, 2:3], 1.0, None, ALU.add,
                                   inc=[("dve", 1)])
            dve_edge()
            P.vector.reciprocal(tiny[0:1, 4:5], tiny[0:1, 3:4], inc=[("dve", 1)])
            P.wait("scalar", "dve")
            P.scalar.activation(tiny[0:1, 5:6], tiny[0:1, 4:5], AF.Ln, inc=[("act", 1)])
            P.scalar.activation(tiny[0:1, 6:7], tiny[0:1, 1:2], AF.Exp, inc=[("act", 1)])
            P.wait("vector", "act")
            P.vector.tensor_scalar(tiny[0:1, 16:20], tiny[0:1, 8:12], tiny[0:1, 5:6], None,
                                   ALU.mult, inc=[("dve", 1)])
            P.wait("scalar", "dve")
            P.scalar.activation(tiny[0:1, 20:24], tiny[0:1, 16:20], AF.Exp, inc=[("act", 1)])
            P.wait("vector", "act")
            P.vector.tensor_tensor(tiny[0:1, 24:28], tiny[0:1, 20:24], tiny[0:1, 12:16],
                                   ALU.mult, inc=[("dve", 1)])
            P.wait("tensor", "dve")
            P.wait("tensor", "act")
            P.wait("tensor", "ld")
            P.tensor.matmul(ps0[:, 0:16], onesr[:], tiny[0:1, 0:16], start=True, stop=True,
                            inc=[("pe", 1)])
            P.tensor.matmul(ps0[:, 16:20], onesr[:], tiny[0:1, 24:28], start=True, stop=True,
                            inc=[("pe", 1)])
            P.tensor.matmul(ps0[:, 20:21], onesr[:], tiny[0:1, 30:31], start=True, stop=True,
                            inc=[("pe", 1)])
            P.wait("vector", "pe")
            P.vector.tensor_copy(scal[:, 0:14], ps0[:, 0:14], inc=[("dve", 1)])
            dve_edge()
            P.vector.tensor_copy(stat[:, 8:12], ps0[:, 16:20], inc=[("dve", 1)])
            dve_edge()
            P.vector.tensor_copy(scal[:, 14:15], ps0[:, 20:21], inc=[("dve", 1)])
            P.wait("scalar", "dve")
            P.scalar.activation(scal[:, 7:8], LG, AF.Exp, scale=float(C), inc=[("act", 1)])
            P.scalar.activation(gw4[:], rev4[:], AF.Exp, scale=LG, inc=[("act", 1)])
            P.vector.memset(scal[:, 8:9], 1e-5, inc=[("dve", 1)])
            dve_edge()
            P.vector.memset(scal[:, 10:11], SHIFT, inc=[("dve", 1)])
            for o in range(0, WIN, D):
                P.wait("sync", "act")
                P.wait("sync", "dve")
                P.sync.dma_start(out=xs[0][:], in_=IWINBC[:, o:o + D], inc=[("ld", 16)])
                P.wait("scalar", "ld")
                P.scalar.activation(gpatw[:, o:o + D], xs[0][:], AF.Exp, scale=LG,
                                    inc=[("act", 1)])
            for kt in range(4):
                P.wait("sync", "act")
                P.wait("sync", "dve")
                P.sync.dma_start(out=xs[0][:, 0:C], in_=DIFFS_T[128 * kt:128 * (kt + 1), :],
                                 inc=[("ld", 16)])
                P.wait("scalar", "ld")
                P.scalar.activation(f1[:, 0:C], xs[0][:, 0:C], AF.Exp, scale=LG, inc=[("act", 1)])
                P.wait("sync", "act")
                P.sync.dma_start(out=xs[0][:, 0:C], in_=CAUSAL_T[128 * kt:128 * (kt + 1), :],
                                 inc=[("ld", 16)])
                P.wait("vector", "ld")
                P.wait("vector", "act")
                P.vector.tensor_tensor(M_T[kt][:], f1[:, 0:C], xs[0][:, 0:C], ALU.mult,
                                       inc=[("dve", 1)])
            P.mark("setup")

            nrow = WIN + 1
            ntile = (nrow + 127) // 128

            def ln_emit(i):
                r0, rn = 128 * i, min(128, nrow - 128 * i)
                buf = xs[i % 2]
                if i >= 2:
                    P.wmark("sync", f"ln{i - 2}", "dve")
                else:
                    P.wmark("sync", "setup", "dve", "act")
                P.sync.dma_start(out=buf[0:rn, :], in_=x_win[r0:r0 + rn, :], inc=[("ld", 16)])
                P.wait("vector", "ld")
                P.wait("scalar", "ld")
                if i < 2:
                    P.wmark("vector", "setup", "act")
                    P.wmark("scalar", "setup", "dve")
                if i >= 1:
                    P.wait("vector", "stz")
                ln_tile(buf, rn, gen_tok)
                P.wait("sync", "dve")
                P.sync.dma_start(out=z_t_dram[r0:r0 + rn, :], in_=gen_tok[0:rn, :],
                                 inc=[("stz", 16)])
                P.mark(f"ln{i}")

            def ztrans_chunk(cc):
                P.wmark("sync", f"ln{4 * cc + 4}", "stz")
                for dl in range(8):
                    P.sync.dma_start(
                        out=z_f[dl][:, 16 + 512 * cc:16 + 512 * (cc + 1)],
                        in_=z_t_dram[1 + 512 * cc:1 + 512 * (cc + 1),
                                     128 * dl:128 * (dl + 1)],
                        transpose=True,
                        inc=[("lt" if cc % 2 == 0 else "lx", 16)])

            def zfs_chunk(cc):
                o = 512 * cc
                P.wmark("sync", f"ln{4 * cc + 4}", "dve")
                P.wait("sync", "act")
                P.sync.dma_start(out=f1[:, 512:1024], in_=IMODBC[:, o:o + 512],
                                 inc=[("lw", 16)])
                P.wait("scalar", "lw")
                P.wait("scalar", "dve")
                P.scalar.activation(f1[:, 0:512], f1[:, 512:1024], AF.Exp, scale=LG,
                                    inc=[("act", 1)])
                P.wait("vector", "act")
                P.wait("vector", "lt" if cc % 2 == 0 else "lx")
                for dl in range(8):
                    P.vector.tensor_tensor(z_fs[dl][:, 16 + o:16 + o + 512],
                                           z_f[dl][:, 16 + o:16 + o + 512],
                                           f1[:, 0:512], ALU.mult, inc=[("dve", 1)])
                P.mark(f"zfc{cc}")

            for i in range(5):
                ln_emit(i)
            P.wmark("sync", "ln0", "stz")
            for dl in range(8):
                P.sync.dma_start(out=z_fs[dl][:, 0:16],
                                 in_=z_t_dram[0:16, 128 * dl:128 * (dl + 1)],
                                 transpose=True, inc=[("lx", 16)])
            P.wait("vector", "lx")
            for dl in range(8):
                P.vector.tensor_copy(z_f[dl][:, 15:16], z_fs[dl][:, 0:1],
                                     inc=[("dve", 1)])
            ztrans_chunk(0)
            zfs_chunk(0)

            def scan_chunk(c):
                cw = C * c
                pcur, pnxt = c % 2, (c + 1) % 2
                P.wmark("gpsimd", f"ln{4 * c + 4}", "stz")
                for tt in range(4):
                    if c > 0:
                        P.wmark("gpsimd", f"Adone{c - 1}", "pe")
                    P.gpsimd.dma_start(out=wk_sc[tt][:],
                                       in_=z_t_dram[cw + 128 * tt:cw + 128 * (tt + 1), :],
                                       inc=[("gp", 16)])
                P.wait("scalar", "gp")
                for tt in range(4):
                    P.scalar.mul(wk_sc[tt][:], wk_sc[tt][:], gw4[:, tt:tt + 1], inc=[("act", 1)])
                P.mark(f"wk{c}")
                P.wmark("tensor", f"zfc{c}", "dve")
                P.wait("tensor", "lt" if c % 2 == 0 else "lx")
                if c > 0:
                    P.wmark("tensor", f"Adone{c - 1}", "dve")
                for tt in range(4):
                    psv = ps0 if tt % 2 == 0 else ps1
                    if tt >= 2:
                        P.wmark("tensor", f"vev{c}_{tt - 2}", "dve")
                    for hf in range(2):
                        for kt in range(8):
                            P.tensor.matmul(
                                psv[:, 512 * hf:512 * (hf + 1)],
                                z_f[kt][:, 16 + cw + 128 * tt:16 + cw + 128 * (tt + 1)],
                                WwT_b[kt][:, 512 * hf:512 * (hf + 1)],
                                start=(kt == 0), stop=(kt == 7),
                                inc=[("pe", 1)] if kt == 7 else ())
                    P.wait("vector", "pe")
                    P.vector.tensor_copy(v_sb[tt][:], psv[:], inc=[("dve", 1)])
                    P.mark(f"vev{c}_{tt}")
                for kt in range(4):
                    pss = ps2[:, 0:512] if kt % 2 == 0 else ps2[:, 512:1024]
                    if kt >= 2:
                        P.wmark("tensor", f"sev{c}_{kt - 2}", "dve")
                    for jt in range(8):
                        P.tensor.matmul(
                            pss,
                            z_f[jt][:, 15 + cw + 128 * kt:15 + cw + 128 * (kt + 1)],
                            z_f[jt][:, 16 + cw:16 + cw + C],
                            start=(jt == 0), stop=(jt == 7),
                            inc=[("pe", 1)] if jt == 7 else ())
                    P.wait("vector", "pe")
                    P.vector.tensor_tensor(SMt[kt][:], pss, M_T[kt][:], ALU.mult,
                                           inc=[("dve", 1)])
                    P.mark(f"sev{c}_{kt}")
                P.wmark("tensor", f"wk{c}", "act")
                P.wmark("tensor", f"sev{c}_3", "dve")
                P.wmark("tensor", f"vev{c}_3", "dve")
                for ct in range(4):
                    psr = ps3 if ct % 2 == 0 else ps0
                    if ct == 1:
                        P.wmark("tensor", f"vev{c}_2", "dve")
                    if ct >= 2:
                        P.wmark("tensor", f"rev{c}_{ct - 2}", "dve")
                    for hf in range(2):
                        for kt in range(4):
                            P.tensor.matmul(
                                psr[:, 512 * hf:512 * (hf + 1)],
                                SMt[kt][:, 128 * ct:128 * (ct + 1)],
                                v_sb[kt][:, 512 * hf:512 * (hf + 1)],
                                start=(kt == 0), stop=False)
                        for jt in range(8):
                            P.tensor.matmul(
                                psr[:, 512 * hf:512 * (hf + 1)],
                                z_fs[jt][:, 16 + cw + 128 * ct:16 + cw + 128 * (ct + 1)],
                                Wst[pcur][jt][:, 512 * hf:512 * (hf + 1)],
                                start=False, stop=(jt == 7),
                                inc=[("pe", 1)] if jt == 7 else ())
                    P.wait("vector", "pe")
                    bufr = rd_t[ct % 2]
                    P.wait("vector", "str")
                    P.vector.tensor_copy(bufr[:], psr[:], inc=[("dve", 1)])
                    P.wait("gpsimd", "dve")
                    P.gpsimd.dma_start(out=reads_dram[cw + 128 * ct:cw + 128 * (ct + 1), :],
                                       in_=bufr[:], inc=[("str", 16)])
                    P.mark(f"rst{c}_{ct}")
                    P.mark(f"rev{c}_{ct}")
                for et in range(8):
                    for hf in range(2):
                        idx = 2 * et + hf
                        psa = ps1[:, 0:512] if idx % 2 == 0 else ps1[:, 512:1024]
                        if idx >= 2:
                            P.wmark("tensor", f"fold{c}_{idx - 2}", "dve")
                        if idx < 2:
                            P.wmark("tensor", f"vev{c}_3", "dve")
                        for tt in range(4):
                            P.tensor.matmul(
                                psa,
                                wk_sc[tt][:, 128 * et:128 * (et + 1)],
                                v_sb[tt][:, 512 * hf:512 * (hf + 1)],
                                start=(tt == 0), stop=(tt == 3),
                                inc=[("pe", 1)] if tt == 3 else ())
                        P.wait("vector", "pe")
                        P.vector.scalar_tensor_tensor(
                            Wst[pnxt][et][:, 512 * hf:512 * (hf + 1)],
                            Wst[pcur][et][:, 512 * hf:512 * (hf + 1)],
                            GC, psa, ALU.mult, ALU.add, inc=[("dve", 1)])
                        P.mark(f"fold{c}_{idx}")
                P.mark(f"fold{c}")
                P.mark(f"Adone{c}")

            scan_chunk(0)
            for c in range(1, NCH):
                for i in range(4 * c + 1, 4 * c + 5):
                    ln_emit(i)
                ztrans_chunk(c)
                zfs_chunk(c)
                scan_chunk(c)

            pfin = NCH % 2
            P.wmark("sync", f"fold{NCH - 1}", "dve")
            for et in range(8):
                P.sync.dma_start(out=aseg_bi[128 * et:128 * (et + 1), :],
                                 in_=Wst[pfin][et][:], inc=[("stw", 16)])
            P.wait("gpsimd", "stw")
            P.gpsimd.collective_compute("AllGather", ALU.bypass, replica_groups=RG,
                                        ins=[aseg_bi.ap().opt()], outs=[aseg_bo.ap().opt()],
                                        inc=[("cc", 1)])
            P.wait("gpsimd", "cc")
            pw = (NCH + 1) % 2
            for et in range(8):
                P.wait("gpsimd", "dve")
                for j in range(4):
                    P.wmark("gpsimd", f"Adone{NCH - 1}", "pe")
                    P.gpsimd.dma_start(out=v_sb[j][:],
                                       in_=aseg_bo[D * j + 128 * et:D * j + 128 * (et + 1), :],
                                       inc=[("gs", 16)])
                P.wait("vector", "gs")
                P.vector.tensor_scalar(Wst[pw][et][:], v_sb[0][:], WJ[0], None, ALU.mult,
                                       inc=[("dve", 1)])
                for j in range(1, 4):
                    dve_edge()
                    P.vector.scalar_tensor_tensor(Wst[pw][et][:], v_sb[j][:], WJ[j],
                                                  Wst[pw][et][:], ALU.mult, ALU.add,
                                                  inc=[("dve", 1)])
                P.wait("sync", "dve")
                P.sync.dma_start(out=wst_dram[128 * et:128 * (et + 1), :], in_=Wst[pw][et][:],
                                 inc=[("stw", 16)])
            P.mark("wstdone")
            P.wait("sync", "stw")
            for il in range(8):
                P.sync.dma_start(out=wstT_b[il][:], in_=wst_dram[:, 128 * il:128 * (il + 1)],
                                 transpose=True, inc=[("lw", 16)])
            P.wait("tensor", "lw")
            for jt in range(8):
                psm = ps2 if jt % 2 == 0 else ps3
                if jt >= 2:
                    P.wmark("tensor", f"m2ev{jt - 2}", "dve")
                for hf in range(2):
                    for il in range(8):
                        P.tensor.matmul(psm[:, 512 * hf:512 * (hf + 1)],
                                        wstT_b[il][:, 128 * jt:128 * (jt + 1)],
                                        WrT_b[il][:, 512 * hf:512 * (hf + 1)],
                                        start=(il == 0), stop=(il == 7),
                                        inc=[("pe", 1)] if il == 7 else ())
                P.wait("vector", "pe")
                P.vector.tensor_copy(m2t[jt][:], psm[:], inc=[("dve", 1)])
                P.mark(f"m2ev{jt}")
            P.mark("m2done")

            P.wait("sync", "str")
            for dl in range(8):
                P.sync.dma_start(out=readsf[dl][:],
                                 in_=reads_dram[0:1024, 128 * dl:128 * (dl + 1)],
                                 transpose=True, inc=[("lr", 16)])
            P.mark("rtr0")

            for i in range(NT):
                r0 = 128 * i
                hf_sc = i // 8
                if i == 8:
                    P.wmark("sync", "eg7", "pe")
                    for dl in range(8):
                        P.sync.dma_start(out=readsf[dl][:],
                                         in_=reads_dram[1024:2048, 128 * dl:128 * (dl + 1)],
                                         transpose=True, inc=[("lr", 16)])
                    P.mark("rtr1")
                col = r0 - 1024 * hf_sc
                if i == 0:
                    P.wmark("vector", "m2done", "dve")
                for dl in range(8):
                    if i > 0:
                        P.wmark("vector", f"egmm{i - 1}", "pe")
                    P.vector.tensor_tensor(zfg[dl][:], z_f[dl][:, 16 + r0:16 + r0 + 128],
                                           gpatw[:, r0:r0 + 128], ALU.mult, inc=[("dve", 1)])
                P.mark(f"zfg{i}")
                P.wmark("sync", f"eg{i - 2}", "dve") if i >= 2 else None
                P.sync.dma_start(out=xs[i % 2][:], in_=x_win[1 + r0:1 + r0 + 128, :],
                                 inc=[("lx", 16)])
                if i >= 2:
                    P.wmark("gpsimd", f"eg{i - 2}", "dve")
                P.gpsimd.dma_start(out=zt_eg[i % 2][:], in_=z_t_dram[1 + r0:1 + r0 + 128, :],
                                   inc=[("gz" if i % 2 == 0 else "gz1", 16)])
                psr = ps0 if i % 2 == 0 else ps1
                P.wmark("tensor", "rtr0" if hf_sc == 0 else "rtr1", "lr")
                P.wmark("tensor", f"zfg{i}", "dve")
                if i >= 2:
                    P.wmark("tensor", f"y1v{i - 2}", "dve")
                if i == 0:
                    P.wmark("tensor", "m2done", "dve")
                for hf in range(2):
                    for kt in range(8):
                        P.tensor.matmul(psr[:, 512 * hf:512 * (hf + 1)],
                                        readsf[kt][:, col:col + 128],
                                        WrT_b[kt][:, 512 * hf:512 * (hf + 1)],
                                        start=(kt == 0), stop=False)
                    for jt in range(8):
                        P.tensor.matmul(psr[:, 512 * hf:512 * (hf + 1)],
                                        zfg[jt][:],
                                        m2t[jt][:, 512 * hf:512 * (hf + 1)],
                                        start=False, stop=(jt == 7),
                                        inc=[("pe", 1)] if jt == 7 else ())
                P.mark(f"egmm{i}")
                P.wait("vector", "pe")
                P.wait("vector", "lx")
                P.wait("vector", "gz" if i % 2 == 0 else "gz1")
                P.vector.scalar_tensor_tensor(f1[:], psr[:], ALPHA, xs[i % 2][:],
                                              ALU.mult, ALU.add, inc=[("dve", 1)])
                dve_edge()
                P.vector.tensor_tensor(f1[:], f1[:], zt_eg[i % 2][:], ALU.add,
                                       inc=[("dve", 1)])
                P.mark(f"y1v{i}")
                P.wait("sync", "dve")
                P.sync.dma_start(out=y1_dram[r0:r0 + 128, :], in_=f1[:], inc=[("ste", 16)])
                P.sync.dma_start(out=dbg_y1[r0:r0 + 128, :], in_=f1[:], inc=[("dbg", 16)])
                if PHASE == 1:
                    P.sync.dma_start(out=out_y[r0:r0 + 128, :], in_=f1[:], inc=[("ste", 16)])
                dve_edge()
                P.vector.tensor_copy(xs[i % 2][:], f1[:], inc=[("dve", 1)])
                dve_edge()
                P.wait("scalar", "ste")
                P.wait("scalar", "dbg")
                P.wait("scalar", "dve")
                P.wait("vector", "ste")
                P.wait("vector", "dbg")
                ln_tile(xs[i % 2], 128, gen_tok)
                dve_edge()
                P.vector.tensor_scalar(f1[:], xs[i % 2][:], stat[:, 2:3], stat[:, 7:8],
                                       ALU.subtract, op1=ALU.mult, inc=[("dve", 1)])
                dve_edge()
                P.vector.tensor_tensor(rd_t[0][:], f1[:], gen_tok[:], ALU.subtract,
                                       inc=[("dve", 1)])
                P.wait("sync", "dve")
                P.sync.dma_start(out=z2aug_bi[r0:r0 + 128, 0:D], in_=gen_tok[:],
                                 inc=[("ste", 16)])
                P.sync.dma_start(out=z2lo_dram[r0:r0 + 128, :], in_=rd_t[0][:],
                                 inc=[("ste", 16)])
                P.mark(f"eg{i}")
            if PHASE == 1:
                P.gpsimd.dma_start(out=dbg_z[:], in_=z_t_dram[:], inc=[("dbg2", 16)])
                P.gpsimd.dma_start(out=dbg_reads[:], in_=reads_dram[:], inc=[("dbg2", 16)])
                P.gpsimd.dma_start(out=dbg_wst[:], in_=wst_dram[:], inc=[("dbg2", 16)])
                P.gpsimd.dma_start(out=dbg_aseg[:], in_=aseg_bi[:], inc=[("dbg2", 16)])
                P.wait("sync", "ste")
                P.wait("sync", "str")
                P.wait("sync", "stz")
                P.wait("sync", "stw")
                P.wait("gpsimd", "gp")
                P.wait("gpsimd", "ld")
                P.wait("gpsimd", "lt")
                P.wait("gpsimd", "lx")
                P.wait("gpsimd", "lr")
                P.wait("gpsimd", "lw")
                P.wait("gpsimd", "gs")
                P.wait("gpsimd", "gz")
                P.wait("gpsimd", "gz1")
                P.wait("gpsimd", "dbg")
                P.wait("gpsimd", "dbg2")
                P.wait("scalar", "act")
                P.wait("vector", "dve")
                P.wait("tensor", "pe")
                return


            P.wait("sync", "ste")
            P.wmark("sync", "eg15", "dve", "pe")
            for hf in range(2):
                for dl in range(8):
                    P.sync.dma_start(out=z_f[dl][:, 1024 * hf:1024 * (hf + 1)],
                                     in_=z2aug_bi[1024 * hf:1024 * (hf + 1),
                                                  128 * dl:128 * (dl + 1)],
                                     transpose=True, inc=[("lt", 16)])
                    P.sync.dma_start(out=z_fs[dl][:, 1024 * hf:1024 * (hf + 1)],
                                     in_=z2lo_dram[1024 * hf:1024 * (hf + 1),
                                                   128 * dl:128 * (dl + 1)],
                                     transpose=True, inc=[("lt", 16)])
            P.wait("tensor", "lt")
            for hf in range(2):
                if hf == 1:
                    P.wmark("tensor", "lgev0", "act")
                combos = [(w8hi, z_f), (w8hi, z_fs), (w8lo, z_f)]
                for ci, (wmat, zmat) in enumerate(combos):
                    for kt in range(8):
                        for nh in range(2):
                            P.tensor.matmul(ps2[0:8, 512 * nh:512 * (nh + 1)],
                                            wmat[:, 8 * kt:8 * (kt + 1)],
                                            zmat[kt][:, 1024 * hf + 512 * nh:
                                                      1024 * hf + 512 * (nh + 1)],
                                            start=(ci == 0 and kt == 0),
                                            stop=(ci == 2 and kt == 7),
                                            inc=[("pe", 1)] if (ci == 2 and kt == 7
                                                                and nh == 1) else ())
                P.wait("scalar", "pe")
                P.scalar.copy(logits_sb[0:8, 1024 * hf:1024 * (hf + 1)], ps2[0:8, 0:1024],
                              inc=[("act", 1)])
                P.mark(f"lgev{hf}")
            P.wait("sync", "act")
            P.sync.dma_start(out=logit_bi[:], in_=logits_sb[:], inc=[("ste", 16)])
            P.wait("tensor", "act")
            for i in range(NT):
                psl = ps0 if i % 2 == 0 else ps1
                if i >= 2:
                    P.wmark("tensor", f"ltok{i - 2}", "dve")
                P.tensor.matmul(psl[:, 0:8], logits_sb[0:8, 128 * i:128 * (i + 1)],
                                ident8[:], start=True, stop=True, inc=[("pe", 1)])
                P.wait("vector", "pe")
                P.wait("vector", "ste")
                P.vector.tensor_copy(logit_tok[:, 8 * i:8 * (i + 1)], psl[:, 0:8],
                                     inc=[("dve", 1)])
                dve_edge()
                P.vector.tensor_copy(gen_tok[:, 0:8], psl[:, 0:8], inc=[("dve", 1)])
                P.wait("sync", "dve")
                P.sync.dma_start(out=z2aug_bi[128 * i:128 * (i + 1), 1024:1032],
                                 in_=gen_tok[:, 0:8], inc=[("ste", 16)])
                P.mark(f"ltok{i}")
            P.wait("gpsimd", "ste")
            P.gpsimd.collective_compute("AllGather", ALU.bypass, replica_groups=RG,
                                        ins=[logit_bi.ap().opt()], outs=[logit_bo.ap().opt()],
                                        inc=[("cc", 1)])
            P.gpsimd.collective_compute("AllGather", ALU.bypass, replica_groups=RG,
                                        ins=[z2aug_bi.ap().opt()], outs=[z2aug_bo.ap().opt()],
                                        inc=[("cc", 1)])

            P.wait("sync", "cc", 2)
            for inst in range(8):
                for w in range(4):
                    P.sync.dma_start(
                        out=bis[16 * inst + 4 * w:16 * inst + 4 * w + 4, :],
                        in_=logit_bo[8 * w + inst:8 * w + inst + 1, :].rearrange(
                            "1 (a f) -> a f", a=4),
                        inc=[("ld", 16)])
            P.wait("tensor", "ld")
            P.tensor.matmul(ps2[:, 384:512], sel16t[:], sel16t[:], start=True, stop=True,
                            inc=[("pe", 1)])
            P.wait("vector", "pe")
            P.vector.tensor_copy(blk16[:], ps2[:, 384:512], inc=[("dve", 1)])
            P.wait("vector", "ld")
            P.vector.memset(bw[:, 0:1], -40.0, inc=[("dve", 1)])
            P.vector.memset(bw[:, 1:2], 40.0, inc=[("dve", 1)])
            P.vector.memset(bw[:, 2:3], 0.0, inc=[("dve", 1)])
            LO, HI, TAU = bw[:, 0:1], bw[:, 1:2], bw[:, 2:3]
            for it in range(BIS_IT):
                dve_edge()
                P.vector.tensor_scalar(f1[:, 0:512], bis[:], TAU, None, ALU.is_ge,
                                       inc=[("dve", 1)])
                dve_edge()
                P.vector.tensor_reduce(bw[:, 3:4], f1[:, 0:512], AX.X, ALU.add,
                                       inc=[("dve", 1)])
                P.wait("tensor", "dve")
                P.tensor.matmul(ps2[:, 2:3], blk16[:], bw[:, 3:4], start=True, stop=True,
                                inc=[("pe", 1)])
                P.wait("vector", "pe")
                P.vector.tensor_copy(bw[:, 5:6], ps2[:, 2:3], inc=[("dve", 1)])
                dve_edge()
                P.vector.tensor_tensor(bw[:, 6:7], bw[:, 5:6], tgt[:], ALU.is_ge,
                                       inc=[("dve", 1)])
                dve_edge()
                P.vector.tensor_tensor(bw[:, 7:8], TAU, LO, ALU.subtract, inc=[("dve", 1)])
                dve_edge()
                P.vector.scalar_tensor_tensor(LO, bw[:, 7:8], bw[:, 6:7], LO,
                                              ALU.mult, ALU.add, inc=[("dve", 1)])
                dve_edge()
                P.vector.tensor_tensor(bw[:, 9:10], HI, TAU, ALU.subtract, inc=[("dve", 1)])
                dve_edge()
                P.vector.scalar_tensor_tensor(HI, bw[:, 9:10], bw[:, 6:7], TAU,
                                              ALU.mult, ALU.add, inc=[("dve", 1)])
                dve_edge()
                P.vector.tensor_scalar(TAU, LO, HI, 0.5, ALU.add, op1=ALU.mult,
                                       inc=[("dve", 1)])
            dve_edge()
            P.vector.tensor_scalar(f1[:, 0:512], bis[:], LO, None, ALU.is_ge,
                                   inc=[("dve", 1)])
            dve_edge()
            P.vector.tensor_scalar(f1[:, 512:1024], bis[:], HI, None, ALU.is_ge,
                                   inc=[("dve", 1)])
            dve_edge()
            P.vector.tensor_tensor(f1[:, 0:512], f1[:, 0:512], f1[:, 512:1024],
                                   ALU.subtract, inc=[("dve", 1)])
            dve_edge()
            P.vector.tensor_scalar(f1[:, 512:1024], f1[:, 0:512], 1e30, -1e30, ALU.mult,
                                   op1=ALU.add, inc=[("dve", 1)])
            dve_edge()
            P.vector.tensor_tensor(f1[:, 0:512], f1[:, 0:512], bis[:], ALU.mult,
                                   inc=[("dve", 1)])
            dve_edge()
            P.vector.tensor_tensor(f1[:, 0:512], f1[:, 0:512], f1[:, 512:1024],
                                   ALU.add, inc=[("dve", 1)])
            dve_edge()
            P.vector.tensor_reduce(bw[:, 4:5], f1[:, 0:512], AX.X, ALU.max,
                                   inc=[("dve", 1)])
            dve_edge()
            P.vector.tensor_copy(f1[:, 640:768], iblk128[:, 0:128], inc=[("dve", 1)])
            dve_edge()
            P.vector.tensor_scalar(f1[:, 512:640], f1[:, 640:768], iota_c[:], None,
                                   ALU.is_equal, inc=[("dve", 1)])
            P.wait("tensor", "dve")
            P.tensor.matmul(ps2[0:1, 128:256], bw[:, 4:5], f1[:, 512:640], start=True,
                            stop=True, inc=[("pe", 1)])
            P.wait("vector", "pe")
            for si in range(8):
                dve_edge()
                P.vector.tensor_reduce(snapw[0:1, si:si + 1],
                                       ps2[0:1, 128 + 16 * si:144 + 16 * si], AX.X,
                                       ALU.max, inc=[("dve", 1)])
            P.wait("tensor", "dve")
            P.tensor.matmul(ps2[0:8, 256:257], snapw[0:1, 0:8], onescf[0:1, :],
                            start=True, stop=True, inc=[("pe", 1)])
            P.wait("vector", "pe")
            P.vector.tensor_copy(bw[0:8, 12:13], ps2[0:8, 256:257], inc=[("dve", 1)])
            P.wait("tensor", "dve")
            P.tensor.matmul(ps2[:, 2:3], sel16t[:], bw[0:8, 12:13], start=True, stop=True,
                            inc=[("pe", 1)])
            P.wait("vector", "pe")
            P.vector.tensor_copy(TAU, ps2[:, 2:3], inc=[("dve", 1)])
            dve_edge()
            P.vector.tensor_scalar(f1[:, 0:512], bis[:], TAU, None, ALU.is_ge,
                                   inc=[("dve", 1)])
            dve_edge()
            P.vector.tensor_reduce(bw[:, 3:4], f1[:, 0:512], AX.X, ALU.add,
                                   inc=[("dve", 1)])
            P.wait("tensor", "dve")
            P.tensor.matmul(ps2[0:32, 4:5], selw32[:], bw[:, 3:4], start=True, stop=True,
                            inc=[("pe", 1)])
            P.wait("vector", "pe")
            P.vector.tensor_copy(cnt32[:], ps2[0:32, 4:5], inc=[("dve", 1)])
            P.wait("tensor", "dve")
            P.tensor.matmul(ps2[0:32, 5:6], prefixw[:], cnt32[:], start=True, stop=True,
                            inc=[("pe", 1)])
            P.wait("vector", "pe")
            P.vector.tensor_copy(bw[0:32, 10:11], ps2[0:32, 5:6], inc=[("dve", 1)])
            dve_edge()
            P.vector.tensor_tensor(bw[0:32, 11:12], nq32[:], bw[0:32, 10:11],
                                   ALU.subtract, inc=[("dve", 1)])
            dve_edge()
            P.vector.tensor_scalar(bw[0:32, 11:12], bw[0:32, 11:12], 0.0, None, ALU.max,
                                   inc=[("dve", 1)])
            dve_edge()
            P.vector.tensor_tensor(cnt32[:], cnt32[:], bw[0:32, 11:12], ALU.min,
                                   inc=[("dve", 1)])
            P.wait("tensor", "dve")
            P.tensor.matmul(ps2[0:1, 8:16], bw[0:8, 12:13], ident8[:], start=True,
                            stop=True, inc=[("pe", 1)])
            P.wait("scalar", "pe")
            P.scalar.copy(rowb[0:1, 0:8], ps2[0:1, 8:16], inc=[("act", 1)])
            P.wait("tensor", "act")
            P.tensor.matmul(ps2[:, 8:16], onesr[:], rowb[0:1, 0:8], start=True, stop=True,
                            inc=[("pe", 1)])
            P.wait("vector", "pe")
            P.vector.tensor_copy(taub[:], ps2[:, 8:16], inc=[("dve", 1)])
            P.wait("tensor", "dve")
            P.tensor.matmul(ps2[0:1, 16:24], cnt32[:], prevw[:], start=True, stop=True,
                            inc=[("pe", 1)])
            P.wait("scalar", "pe")
            P.scalar.copy(offrow[:], ps2[0:1, 16:24], inc=[("act", 1)])
            P.wait("tensor", "act")
            P.tensor.matmul(ps2[:, 16:24], onesr[:], offrow[:], start=True, stop=True,
                            inc=[("pe", 1)])
            P.wait("vector", "pe")
            P.vector.tensor_copy(offib[:], ps2[:, 16:24], inc=[("dve", 1)])
            P.wait("tensor", "dve")
            P.tensor.matmul(ps2[0:20, 24:25], cntsel[:], cnt32[:], start=True, stop=True,
                            inc=[("pe", 1)])
            P.wait("scalar", "pe")
            P.scalar.copy(stat[0:20, 12:13], ps2[0:20, 24:25], inc=[("act", 1)])
            P.wait("tensor", "act")
            P.tensor.matmul(ps2[0:1, 32:52], stat[0:20, 12:13], ident32[0:20, 0:20],
                            start=True, stop=True, inc=[("pe", 1)])
            P.wait("scalar", "pe")
            P.scalar.copy(rowb[0:1, 8:28], ps2[0:1, 32:52], inc=[("act", 1)])
            P.wait("tensor", "act")
            P.tensor.matmul(ps2[:, 32:52], onesr[:], rowb[0:1, 8:28], start=True, stop=True,
                            inc=[("pe", 1)])
            P.wait("vector", "pe")
            P.vector.tensor_copy(cnt20b[:], ps2[:, 32:52], inc=[("dve", 1)])
            dve_edge()
            P.vector.tensor_scalar(stat[:, 13:14], iota_c[:], kbase20[:, 0:1], None,
                                   ALU.add, inc=[("dve", 1)]) if False else None
            P.vector.tensor_tensor(valid20[:], kbase20[:], cnt20b[:], ALU.is_lt,
                                   inc=[("dve", 1)]) if False else None
            P.vector.tensor_scalar(valid20[:], kbase20[:], iota_c[:], None, ALU.add,
                                   inc=[("dve", 1)])
            dve_edge()
            P.vector.tensor_tensor(valid20[:], valid20[:], cnt20b[:], ALU.is_lt,
                                   inc=[("dve", 1)])
            P.mark("bisdone")

            P.wmark("vector", "bisdone", "dve")
            for i in range(NT):
                dve_edge()
                P.vector.tensor_tensor(pose[:, 0:8], logit_tok[:, 8 * i:8 * (i + 1)],
                                       taub[:], ALU.is_ge, inc=[("dve", 1)])
                psl = ps0 if i % 2 == 0 else ps1
                P.wait("tensor", "dve")
                P.wmark("tensor", "bisdone", "act")
                P.tensor.matmul(psl[:, 0:8], tris[:], pose[:, 0:8], start=True, stop=False)
                P.tensor.matmul(psl[:, 0:8], onesr[:], offrow[:], start=False, stop=True,
                                inc=[("pe", 1)])
                P.tensor.matmul(psl[0:1, 8:16], onescf[:], pose[:, 0:8], start=True,
                                stop=True, inc=[("pe", 1)])
                P.wait("vector", "pe")
                P.vector.tensor_copy(f1[:, 0:8], psl[:, 0:8], inc=[("dve", 1)])
                P.wait("scalar", "pe")
                P.scalar.copy(rowb[0:1, 32:40], psl[0:1, 8:16], inc=[("act", 1)])
                P.wait("vector", "act")
                dve_edge()
                P.vector.tensor_tensor(offrow[:], offrow[:], rowb[0:1, 32:40], ALU.add,
                                       inc=[("dve", 1)])
                dve_edge()
                P.vector.tensor_tensor(f1[:, 8:16], f1[:, 0:8], nqcap[:], ALU.is_lt,
                                       inc=[("dve", 1)])
                dve_edge()
                P.vector.tensor_tensor(selr[:, 16 * i:16 * i + 8], pose[:, 0:8],
                                       f1[:, 8:16], ALU.mult, inc=[("dve", 1)])
                dve_edge()
                P.vector.tensor_tensor(selr[:, 16 * i + 8:16 * i + 16], f1[:, 0:8],
                                       offib[:], ALU.subtract, inc=[("dve", 1)])
                P.mark(f"sel{i}")

            for i in range(NT):
                if i >= 1:
                    P.wmark("vector", f"pos{i - 1}", "pe")
                dve_edge()
                P.vector.tensor_scalar(pose[:, 24:32], selr[:, 16 * i + 8:16 * i + 16],
                                       128.0, None, ALU.is_ge, inc=[("dve", 1)])
                dve_edge()
                P.vector.tensor_scalar(pose[:, 16:24], selr[:, 16 * i + 8:16 * i + 16],
                                       256.0, None, ALU.is_ge, inc=[("dve", 1)])
                dve_edge()
                P.vector.tensor_tensor(pose[:, 24:32], pose[:, 24:32], pose[:, 16:24],
                                       ALU.add, inc=[("dve", 1)])
                dve_edge()
                P.vector.tensor_scalar(pose[:, 16:24], selr[:, 16 * i + 8:16 * i + 16],
                                       384.0, None, ALU.is_ge, inc=[("dve", 1)])
                dve_edge()
                P.vector.tensor_tensor(pose[:, 24:32], pose[:, 24:32], pose[:, 16:24],
                                       ALU.add, inc=[("dve", 1)])
                dve_edge()
                P.vector.scalar_tensor_tensor(pose[:, 8:16], pose[:, 24:32], -128.0,
                                              selr[:, 16 * i + 8:16 * i + 16],
                                              ALU.mult, ALU.add, inc=[("dve", 1)])
                dve_edge()
                P.vector.tensor_tensor(
                    ehi[:], iblk4[:],
                    pose.ap()[:, 24:32, None].broadcast_to([128, 8, 4]),
                    ALU.is_equal, inc=[("dve", 1)])
                dve_edge()
                P.vector.tensor_tensor(
                    ehi[:], ehi[:],
                    selr.ap()[:, 16 * i:16 * i + 8, None].broadcast_to([128, 8, 4]),
                    ALU.mult, inc=[("dve", 1)])
                dve_edge()
                P.vector.tensor_scalar(pose[:, 32:33], iota_c[:], float(128 * i), WOFFB,
                                       ALU.add, op1=ALU.add, inc=[("dve", 1)])
                dve_edge()
                P.vector.tensor_scalar(ehi[:], ehi[:], pose[:, 32:33], None, ALU.mult,
                                       inc=[("dve", 1)])
                dve_edge()
                P.vector.tensor_tensor(
                    elo[:], iblk128[:],
                    pose.ap()[:, 8:16, None].broadcast_to([128, 8, 128]),
                    ALU.is_equal, inc=[("dve", 1)])
                P.wait("tensor", "dve")
                P.tensor.matmul(ps2[0:32, 0:512], ehi[:], elo[:, 0:512],
                                start=(i == 0), stop=(i == NT - 1),
                                inc=[("pe", 1)] if i == NT - 1 else ())
                P.tensor.matmul(ps2[0:32, 512:1024], ehi[:], elo[:, 512:1024],
                                start=(i == 0), stop=(i == NT - 1),
                                inc=[("pe", 2)] if i == NT - 1 else [("pe", 1)])
                P.mark(f"pos{i}")
            P.wait("vector", "pe")
            P.vector.tensor_copy(possb[:], ps2[0:32, 0:1024], inc=[("dve", 1)])
            P.wait("sync", "dve")
            for inst in range(8):
                P.sync.dma_start(
                    out=pos_bi[inst:inst + 1, :].rearrange("1 (h l) -> h l", h=4),
                    in_=possb[4 * inst:4 * inst + 4, 128 * inst:128 * (inst + 1)],
                    inc=[("stw", 16)])
            P.wait("gpsimd", "stw")
            P.gpsimd.collective_compute("AllGather", ALU.bypass, replica_groups=RG,
                                        ins=[pos_bi.ap().opt()], outs=[pos_bo.ap().opt()],
                                        inc=[("cc", 1)])
            P.mark("posdone")

            P.wait("sync", "cc", 4)
            P.sync.dma_start(out=possb[0:32, 0:512], in_=pos_bo[:], inc=[("ld", 16)])
            P.wait("tensor", "ld")
            for j in range(20):
                k = [0, 1, 2] * 4 + [0, 1] * 4
                psl = ps0 if j % 2 == 0 else ps1
                if j >= 2:
                    P.wmark("tensor", f"idx{j - 2}", "dve")
                P.tensor.matmul(psl[0:1, 0:128], possel[:, j:j + 1],
                                possb[0:32, 128 * k[j]:128 * (k[j] + 1)],
                                start=True, stop=True, inc=[("pe", 1)])
                P.wait("scalar", "pe")
                P.scalar.copy(rowb[0:1, 0:0].ap() if False else stat[0:1, 14:15],
                              psl[0:1, 0:1], inc=[("act", 1)]) if False else None
                P.scalar.copy(logits_sb[0:1, 128 * (j % 8):128 * (j % 8) + 128],
                              psl[0:1, 0:128], inc=[("act", 1)])
                P.wait("tensor", "act")
                P.tensor.matmul(psl[:, 128:129],
                                logits_sb[0:1, 128 * (j % 8):128 * (j % 8) + 128],
                                onescf[0:1, 0:1], start=True, stop=True, inc=[("pe", 1)])
                P.wait("vector", "pe")
                P.vector.tensor_copy(idx_f[:, j:j + 1], psl[:, 128:129], inc=[("dve", 1)])
                P.mark(f"idx{j}")
            dve_edge()
            P.vector.tensor_tensor(f1[:, 0:20], idx_f[:, 0:20], valid20[:], ALU.mult,
                                   inc=[("dve", 1)])
            dve_edge()
            P.vector.tensor_scalar(f1[:, 20:40], valid20[:], 1.0, -BIGIDX, ALU.subtract,
                                   op1=ALU.mult, inc=[("dve", 1)])
            dve_edge()
            P.vector.tensor_tensor(f1[:, 40:60], f1[:, 0:20], f1[:, 20:40], ALU.add,
                                   inc=[("dve", 1)])
            dve_edge()
            P.vector.tensor_copy(idxq_i[:, 0:20], f1[:, 40:60], inc=[("dve", 1)])
            P.mark("idxdone")
            gbuf = [a2(1032 * j, AUGW) for j in range(20)]
            P.wmark("gpsimd", "eg15", "pe")
            P.wmark("gpsimd", "m2done", "pe")
            for j in range(20):
                P.gpsimd.memset(gbuf[j][:], 0.0, inc=[("pool", 1)])
            P.wmark("gpsimd", "idxdone", "dve")
            P.wait("gpsimd", "pool")
            for j in range(20):
                gqs = ["gq", "gqb", "gqc"][j % 3]
                P.gpsimd.indirect_dma_start(
                    out=gbuf[j][:], out_offset=None,
                    in_=z2aug_bo[:],
                    in_offset=IndirectOffsetOnAxis(idxq_i[:, j:j + 1], 0),
                    bounds_check=4 * WIN - 1, oob_is_err=False, inc=[(gqs, 16)])
            for j in range(20):
                gqs = ["gq", "gqb", "gqc"][j % 3]
                buf = gbuf[j]
                P.wait("vector", gqs)
                dve_edge()
                if j >= 3:
                    P.wait("vector", "ste")
                gsel = gselq if j < 12 else gselk
                P.vector.tensor_tensor(f1[:, 60 + 8 * (j % 4):68 + 8 * (j % 4)],
                                       buf[:, 1024:1032], gsel[:], ALU.mult,
                                       inc=[("dve", 1)])
                dve_edge()
                if j < 12:
                    P.vector.tensor_reduce(stat[:, 13 + j % 3:14 + j % 3],
                                           f1[:, 60 + 8 * (j % 4):68 + 8 * (j % 4)],
                                           AX.X, ALU.add, inc=[("dve", 1)])
                else:
                    P.vector.tensor_reduce(sig_kv[:, j - 12:j - 11],
                                           f1[:, 60 + 8 * (j % 4):68 + 8 * (j % 4)],
                                           AX.X, ALU.add, inc=[("dve", 1)])
                P.wait("scalar", "dve")
                P.wait("scalar", gqs)
                if j < 12:
                    wq_, kq_ = j // 3, j % 3
                    nrows = 96 if kq_ == 2 else 128
                    r0q = QCAP * wq_ + 128 * kq_
                    P.scalar.dma_start(out=qcomp_dram[r0q:r0q + nrows, :],
                                       in_=buf[0:nrows, 0:1024], inc=[("ste", 16)])
                    P.scalar.dma_start(out=qlog_dram[r0q:r0q + nrows, :],
                                       in_=stat[0:nrows, 13 + j % 3:14 + j % 3],
                                       inc=[("ste", 16)])
                else:
                    wk_, kk_ = (j - 12) // 2, (j - 12) % 2
                    r0k = 256 * wk_ + 128 * kk_
                    P.scalar.dma_start(out=kvcomp_dram[r0k:r0k + 128, :],
                                       in_=buf[:, 0:1024], inc=[("ste", 16)])
                P.mark(f"gst{j}")
            P.wait("sync", "ste")
            for i in range(8):
                P.wait("sync", "dve")
                P.sync.dma_start(out=xs[0][:, 0:DI], in_=wqT[128 * i:128 * (i + 1), :],
                                 inc=[("ld", 16)])
                P.wait("vector", "ld")
                P.wait("vector", "ste")
                P.vector.tensor_copy(wq_b[i][:], xs[0][:, 0:DI], inc=[("dve", 1)])
                P.wait("sync", "dve")
                P.sync.dma_start(out=xs[0][:], in_=wkvT[128 * i:128 * (i + 1), :],
                                 inc=[("ld", 16)])
                P.wait("vector", "ld")
                P.vector.tensor_copy(wkv_b[i][:], xs[0][:], inc=[("dve", 1)])
            for i in range(4):
                P.wait("sync", "dve")
                P.sync.dma_start(out=xs[0][:], in_=woutT[128 * i:128 * (i + 1), :],
                                 inc=[("ld", 16)])
                P.wait("vector", "ld")
                P.vector.tensor_copy(wout_b[i][:], xs[0][:], inc=[("dve", 1)])
            P.wait("sync", "dve")
            P.sync.dma_start(out=xs[0][:, 0:4], in_=nk_in[:], inc=[("ld", 16)])
            P.sync.dma_start(out=xs[1][0:1, 0:8 * 65], in_=nv_in[:], inc=[("ld", 16)])
            P.wait("vector", "ld")
            P.vector.tensor_copy(nkb[:], xs[0][:, 0:4], inc=[("dve", 1)])
            dve_edge()
            P.vector.tensor_copy(nvb[:], xs[1][0:1, 0:8 * 65], inc=[("dve", 1)])
            P.mark("fwts")

            dve_edge()
            P.vector.tensor_scalar(sig_kv[:], sig_kv[:], -1.0, None, ALU.mult,
                                   inc=[("dve", 1)])
            P.wait("scalar", "dve")
            P.scalar.activation(sig_kv[:], sig_kv[:], AF.Exp, inc=[("act", 1)])
            P.wait("vector", "act")
            P.vector.tensor_scalar(sig_kv[:], sig_kv[:], 1.0, None, ALU.add,
                                   inc=[("dve", 1)])
            dve_edge()
            P.vector.reciprocal(sig_kv[:], sig_kv[:], inc=[("dve", 1)])
            dve_edge()
            P.vector.tensor_scalar(kbias[:], valid20[:, 12:20], 1.0, 1e9, ALU.subtract,
                                   op1=ALU.mult, inc=[("dve", 1)])
            dve_edge()
            P.vector.tensor_scalar(kbias[:], kbias[:], SHIFT, None, ALU.add,
                                   inc=[("dve", 1)])
            P.mark("gdone")

            P.wmark("sync", "gdone", "ste")
            for dl in range(8):
                P.sync.dma_start(out=z_f[dl][:, 0:QS],
                                 in_=qcomp_dram[:, 128 * dl:128 * (dl + 1)],
                                 transpose=True, inc=[("lr", 16)])
            P.wait("tensor", "lr")
            P.wmark("tensor", "fwts", "dve")
            for et in range(4):
                for nch in range(3):
                    n0, n1 = 512 * nch, min(512 * (nch + 1), QS)
                    psl = ps0 if (et * 3 + nch) % 2 == 0 else ps1
                    if et * 3 + nch >= 2:
                        P.wmark("tensor", f"qev{et * 3 + nch - 2}", "dve")
                    for kt in range(8):
                        P.tensor.matmul(psl[:, 0:n1 - n0],
                                        wq_b[kt][:, 128 * et:128 * (et + 1)],
                                        z_f[kt][:, n0:n1],
                                        start=(kt == 0), stop=(kt == 7),
                                        inc=[("pe", 1)] if kt == 7 else ())
                    P.wait("vector", "pe")
                    P.vector.tensor_copy(qh[et][:, n0:n1], psl[:, 0:n1 - n0],
                                         inc=[("dve", 1)])
                    P.mark(f"qev{et * 3 + nch}")
            for wv in range(2):
                for t in range(4):
                    P.wait("sync", "dve") if wv == 1 else None
                    P.sync.dma_start(out=kvf[t][:],
                                     in_=kvcomp_dram[:, 128 * (4 * wv + t):
                                                     128 * (4 * wv + t + 1)],
                                     transpose=True, inc=[("lw", 16)])
                P.wait("tensor", "lw")
                if wv == 0:
                    P.wmark("tensor", "qev11", "dve")
                for et in range(4):
                    for nch in range(2):
                        idx2 = et * 2 + nch
                        psl = ps0 if idx2 % 2 == 0 else ps1
                        if wv == 1 or idx2 >= 2:
                            P.wait("tensor", "dve")
                        for kt in range(4):
                            P.tensor.matmul(psl[:, 0:512],
                                            wkv_b[4 * wv + kt][:, 128 * et:128 * (et + 1)],
                                            kvf[kt][:, 512 * nch:512 * (nch + 1)],
                                            start=(kt == 0), stop=(kt == 3),
                                            inc=[("pe", 1)] if kt == 3 else ())
                        P.wait("vector", "pe")
                        if wv == 0:
                            P.vector.tensor_copy(kf[et][:, 512 * nch:512 * (nch + 1)],
                                                 psl[:, 0:512], inc=[("dve", 1)])
                        else:
                            P.vector.tensor_tensor(kf[et][:, 512 * nch:512 * (nch + 1)],
                                                   kf[et][:, 512 * nch:512 * (nch + 1)],
                                                   psl[:, 0:512], ALU.add, inc=[("dve", 1)])
                for st in range(8):
                    psl = ps2 if st % 2 == 0 else ps3
                    if wv == 1 or st >= 2:
                        P.wait("tensor", "dve")
                    for kt in range(4):
                        P.tensor.matmul(
                            psl[:, 0:512],
                            kvf[kt][:, 128 * st:128 * (st + 1)],
                            wkv_b[4 * wv + kt][:, DI:2 * DI],
                            start=(kt == 0), stop=(kt == 3),
                            inc=[("pe", 1)] if kt == 3 else ())
                    P.wait("vector", "pe")
                    vview = vtok[st].rearrange("p (h e) -> p h e", h=8)
                    pview = psl.ap()[:, 0:512].rearrange("p (h e) -> p h e", h=8)
                    if wv == 0:
                        P.vector.tensor_copy(vview[:, :, 0:64], pview, inc=[("dve", 1)])
                    else:
                        P.vector.tensor_tensor(vview[:, :, 0:64], vview[:, :, 0:64], pview,
                                               ALU.add, inc=[("dve", 1)])
                        dve_edge()
                        P.vector.tensor_scalar(vview[:, :, 0:64], vview[:, :, 0:64],
                                               sig_kv[:, st:st + 1], None, ALU.mult,
                                               inc=[("dve", 1)])
                        dve_edge()
                        P.vector.memset(vview[:, :, 64:65], 1.0, inc=[("dve", 1)])
            P.wait("vector", "act")
            for et in range(4):
                dve_edge()
                P.vector.tensor_copy(kf[et][:, KS:KS + 1], nkb[:, et:et + 1],
                                     inc=[("dve", 1)])
            P.mark("projs")

            for h in range(8):
                et, oo = h // 2, 64 * (h % 2)
                for jt in range(8):
                    psl = [ps0.ap()[:, 0:512], ps0.ap()[:, 512:1024],
                           ps1.ap()[:, 0:512]][0] if False else None
                    for nch in range(3):
                        n0 = 512 * nch
                        n1 = min(512 * (nch + 1), QS)
                        pss = [ps0.ap()[:, 0:512], ps0.ap()[:, 512:1024],
                               ps1.ap()[:, 0:512]][nch]
                        if h + jt > 0:
                            P.wmark("tensor", f"exp{h}_{jt - 1}_{nch}" if jt > 0
                                    else f"exp{h - 1}_7_{nch}", "act")
                        P.wmark("tensor", "projs", "dve") if (h == 0 and jt == 0
                                                              and nch == 0) else None
                        P.tensor.matmul(pss[:, 0:n1 - n0],
                                        kf[et][64 * (h % 2):64 * (h % 2) + 64,
                                               128 * jt:128 * (jt + 1)],
                                        qh[et][64 * (h % 2):64 * (h % 2) + 64, n0:n1],
                                        start=True, stop=True, inc=[("pe", 1)])
                        P.wait("scalar", "pe")
                        P.scalar.activation(expT[jt][:, n0:n1], pss[:, 0:n1 - n0],
                                            AF.Exp, scale=0.125,
                                            bias=kbias[:, jt:jt + 1], inc=[("act", 1)])
                        P.mark(f"exp{h}_{jt}_{nch}")
                for nch in range(3):
                    n0 = 512 * nch
                    n1 = min(512 * (nch + 1), QS)
                    pss = [ps0.ap()[:, 0:512], ps0.ap()[:, 512:1024],
                           ps1.ap()[:, 0:512]][nch]
                    P.wmark("tensor", f"exp{h}_7_{nch}", "act")
                    P.tensor.matmul(pss[0:1, 0:n1 - n0],
                                    kf[et][64 * (h % 2):64 * (h % 2) + 64, KS:KS + 1],
                                    qh[et][64 * (h % 2):64 * (h % 2) + 64, n0:n1],
                                    start=True, stop=True, inc=[("pe", 1)])
                    P.wait("scalar", "pe")
                    P.scalar.activation(expN[0:1, n0:n1], pss[0:1, 0:n1 - n0], AF.Exp,
                                        scale=0.125, bias=scal[0:1, 10:11],
                                        inc=[("act", 1)])
                    P.mark(f"expN{h}_{nch}")
                P.wait("tensor", "act")
                P.wait("tensor", "dve")
                for jt in range(8):
                    for nch in range(3):
                        n0 = 512 * nch
                        n1 = min(512 * (nch + 1), QS)
                        psO = [ps2.ap()[0:65, 0:512], ps2.ap()[0:65, 512:1024],
                               ps3.ap()[0:65, 0:512]][nch]
                        P.tensor.matmul(psO[:, 0:n1 - n0],
                                        vtok[jt][:, 65 * h:65 * (h + 1)],
                                        expT[jt][:, n0:n1],
                                        start=(jt == 0), stop=False)
                for nch in range(3):
                    n0 = 512 * nch
                    n1 = min(512 * (nch + 1), QS)
                    psO = [ps2.ap()[0:65, 0:512], ps2.ap()[0:65, 512:1024],
                           ps3.ap()[0:65, 0:512]][nch]
                    P.tensor.matmul(psO[:, 0:n1 - n0], nvb[0:1, 65 * h:65 * (h + 1)],
                                    expN[0:1, n0:n1], start=False, stop=True,
                                    inc=[("pe", 1)])
                P.wait("scalar", "pe")
                for nch in range(3):
                    n0 = 512 * nch
                    n1 = min(512 * (nch + 1), QS)
                    psO = [ps2.ap()[0:65, 0:512], ps2.ap()[0:65, 512:1024],
                           ps3.ap()[0:65, 0:512]][nch]
                    P.scalar.copy(sumrow[0:1, n0:n1], psO[64:65, 0:n1 - n0],
                                  inc=[("act", 1)])
                P.wait("vector", "act")
                P.vector.reciprocal(sumrow[0:1, 0:QS], sumrow[0:1, 0:QS], inc=[("dve", 1)])
                P.wait("tensor", "dve")
                for nch in range(3):
                    n0 = 512 * nch
                    n1 = min(512 * (nch + 1), QS)
                    P.wait("tensor", "act")
                    P.tensor.matmul(ps1.ap()[0:64, 512:512 + n1 - n0],
                                    onesr[0:1, 0:64], sumrow[0:1, n0:n1],
                                    start=True, stop=True, inc=[("pe", 1)])
                    P.wait("scalar", "pe")
                    P.scalar.copy(recb[:, n0:n1], ps1.ap()[0:64, 512:512 + n1 - n0],
                                  inc=[("act", 1)])
                    P.wait("vector", "act")
                    psO = [ps2.ap()[0:65, 0:512], ps2.ap()[0:65, 512:1024],
                           ps3.ap()[0:65, 0:512]][nch]
                    P.vector.tensor_tensor(oT[et][oo:oo + 64, n0:n1], psO[0:64, 0:n1 - n0],
                                           recb[:, n0:n1], ALU.mult, inc=[("dve", 1)])
                    P.mark(f"oev{h}_{nch}")
            P.mark("attn")

            P.wmark("tensor", "attn", "dve")
            for st in range(11):
                n0 = 128 * st
                psl = ps0 if st % 2 == 0 else ps1
                if st >= 2:
                    P.wmark("tensor", f"oproj{st - 2}", "dve")
                for hf in range(2):
                    for kt in range(4):
                        P.tensor.matmul(psl[:, 512 * hf:512 * (hf + 1)],
                                        oT[kt][:, n0:n0 + 128],
                                        wout_b[kt][:, 512 * hf:512 * (hf + 1)],
                                        start=(kt == 0), stop=(kt == 3),
                                        inc=[("pe", 1)] if kt == 3 else ())
                P.wait("gpsimd", "ste")
                P.gpsimd.dma_start(out=stat[:, 15:16], in_=qlog_dram[n0:n0 + 128, :],
                                   inc=[("gz", 16)]) if False else None
                P.sync.dma_start(out=stat[:, 15:16], in_=qlog_dram[n0:n0 + 128, :],
                                 inc=[("ld", 16)])
                P.wait("scalar", "ld")
                P.scalar.activation(stat[:, 14:15], stat[:, 15:16], AF.Exp, scale=-1.0,
                                    inc=[("act", 1)])
                P.wait("vector", "act")
                P.vector.tensor_scalar(stat[:, 13:14], stat[:, 14:15], 1.0, None, ALU.add,
                                       inc=[("dve", 1)])
                dve_edge()
                P.vector.reciprocal(stat[:, 12:13], stat[:, 13:14], inc=[("dve", 1)])
                P.wait("vector", "pe")
                dve_edge()
                buf = gen_tok if st % 2 == 0 else a2(38848, 1024)
                if st >= 2:
                    P.wait("vector", "so0" if st % 2 == 0 else "so1")
                P.vector.tensor_scalar(buf[:], psl[:], stat[:, 12:13], None, ALU.mult,
                                       inc=[("dve", 1)])
                P.wait("sync", "dve")
                P.sync.dma_start(out=o_bi[n0:n0 + 128, :], in_=buf[:],
                                 inc=[("so0" if st % 2 == 0 else "so1", 16)])
                P.mark(f"ost{st}")
                P.mark(f"oproj{st}")
            P.wait("gpsimd", "so0")
            P.wait("gpsimd", "so1")
            P.gpsimd.collective_compute("AllGather", ALU.bypass, replica_groups=RG,
                                        ins=[o_bi.ap().opt()], outs=[o_bo.ap().opt()],
                                        inc=[("cc", 1)])

            dve_edge()
            P.vector.tensor_scalar(stat[:, 11:12], WOFFB, float(QCAP) / 2048.0, None,
                                   ALU.mult, inc=[("dve", 1)])
            P.wait("tensor", "dve")
            P.tensor.matmul(ps2[0:1, 52:64], cnt32[:], combsel[:], start=True, stop=True,
                            inc=[("pe", 1)])
            P.wait("scalar", "pe")
            P.scalar.copy(rowb[0:1, 40:52], ps2[0:1, 52:64], inc=[("act", 1)])
            P.wait("tensor", "act")
            P.tensor.matmul(ps2[:, 52:64], onesr[:], rowb[0:1, 40:52], start=True,
                            stop=True, inc=[("pe", 1)])
            P.wait("vector", "pe")
            P.vector.tensor_copy(cm_f[:], ps2[:, 52:64], inc=[("dve", 1)])
            dve_edge()
            P.vector.tensor_scalar(f1[:, 0:12], kbase12[:], iota_c[:], None, ALU.add,
                                   inc=[("dve", 1)])
            dve_edge()
            P.vector.tensor_tensor(f1[:, 12:24], f1[:, 0:12], cm_f[:], ALU.is_lt,
                                   inc=[("dve", 1)])
            dve_edge()
            P.vector.tensor_scalar(f1[:, 24:36], kbase12[:], iota_c[:], stat[:, 11:12],
                                   ALU.add, op1=ALU.add, inc=[("dve", 1)])
            for gg in range(4):
                dve_edge()
                P.vector.tensor_scalar(f1[:, 36 + 3 * gg:39 + 3 * gg],
                                       f1[:, 24 + 3 * gg:27 + 3 * gg],
                                       float(QS * gg), None, ALU.add, inc=[("dve", 1)])
            dve_edge()
            P.vector.tensor_tensor(f1[:, 48:60], f1[:, 36:48], f1[:, 12:24], ALU.mult,
                                   inc=[("dve", 1)])
            dve_edge()
            P.vector.tensor_scalar(f1[:, 60:72], f1[:, 12:24], 1.0, -BIGIDX,
                                   ALU.subtract, op1=ALU.mult, inc=[("dve", 1)])
            dve_edge()
            P.vector.tensor_tensor(f1[:, 48:60], f1[:, 48:60], f1[:, 60:72], ALU.add,
                                   inc=[("dve", 1)])
            dve_edge()
            P.vector.tensor_copy(cm_i[:], f1[:, 48:60], inc=[("dve", 1)])
            P.wait("tensor", "dve")
            for j in range(12):
                k3 = j % 3
                psl = ps0 if j % 2 == 0 else ps1
                if j >= 2:
                    P.wmark("tensor", f"cidx{j - 2}", "dve")
                P.tensor.matmul(psl[0:1, 0:128], combsel2[:, j:j + 1],
                                possb[0:32, 128 * k3:128 * (k3 + 1)],
                                start=True, stop=True, inc=[("pe", 1)])
                P.wait("scalar", "pe")
                P.scalar.copy(logits_sb[0:1, 128 * (j % 8):128 * (j % 8) + 128],
                              psl[0:1, 0:128], inc=[("act", 1)])
                P.wait("tensor", "act")
                P.tensor.matmul(psl[:, 128:129],
                                logits_sb[0:1, 128 * (j % 8):128 * (j % 8) + 128],
                                onescf[0:1, 0:1], start=True, stop=True, inc=[("pe", 1)])
                P.wait("vector", "pe")
                P.vector.tensor_scalar(f1[:, 72 + j:73 + j], psl[:, 128:129], WOFFB, None,
                                       ALU.subtract, inc=[("dve", 1)])
                P.mark(f"cidx{j}")
            dve_edge()
            P.vector.tensor_tensor(f1[:, 84:96], f1[:, 72:84], f1[:, 12:24], ALU.mult,
                                   inc=[("dve", 1)])
            dve_edge()
            P.vector.tensor_tensor(f1[:, 84:96], f1[:, 84:96], f1[:, 60:72], ALU.add,
                                   inc=[("dve", 1)])
            dve_edge()
            P.vector.tensor_scalar(f1[:, 96:108], f1[:, 84:96], 0.0, None, ALU.is_lt,
                                   inc=[("dve", 1)])
            dve_edge()
            P.vector.scalar_tensor_tensor(f1[:, 84:96], f1[:, 96:108], 2.0 * BIGIDX,
                                          f1[:, 84:96], ALU.mult, ALU.add,
                                          inc=[("dve", 1)])
            dve_edge()
            P.vector.tensor_copy(cm2_i[:], f1[:, 84:96], inc=[("dve", 1)])
            P.mark("cmidx")
            P.wait("gpsimd", "cc", 5)
            P.wmark("gpsimd", "cmidx", "dve")
            ob = [a2(20544 + 1024 * j, 1024) for j in range(12)]
            for j in range(12):
                P.gpsimd.memset(ob[j][:], 0.0, inc=[("pool", 1)])
            P.wait("gpsimd", "pool")
            for j in range(12):
                gqs = ["gq", "gqb", "gqc"][j % 3]
                P.gpsimd.indirect_dma_start(
                    out=ob[j][:], out_offset=None,
                    in_=o_bo[:],
                    in_offset=IndirectOffsetOnAxis(cm_i[:, j:j + 1], 0),
                    bounds_check=4 * QS - 1, oob_is_err=False, inc=[(gqs, 16)])
            P.wait("sync", "pe")
            P.wait("sync", "dve")
            iwb = a4(2560, 2048)
            P.sync.dma_start(out=iwb[:], in_=IWINBC[:], inc=[("lw", 16)])
            P.wait("vector", "pe")
            P.wait("vector", "lw")
            P.wait("vector", "so0")
            P.wait("vector", "so1")
            Ebuf = [a2(2048 * j, 2048) for j in range(10)] + [a2(38848, 2048),
                                                             a2(40896, 2048)]
            for j in range(12):
                dve_edge()
                P.vector.tensor_scalar(Ebuf[j][:], iwb[:], f1[:, 84 + j:85 + j], None,
                                       ALU.is_equal, inc=[("dve", 1)])
            P.mark("ebld")
            P.wait("sync", "dve")
            P.wait("sync", "act")
            P.sync.dma_start(out=nullb[:], in_=null_in[:, 0:1024], inc=[("ld", 16)])
            P.wmark("tensor", "ebld", "dve")
            P.wait("tensor", "act")
            for sm in ["gq", "gqb", "gqc"]:
                P.wait("tensor", sm)
            fbuf = [f1, ocomb[0]]
            for i in range(NT):
                r0 = 128 * i
                psl = ps0 if i % 2 == 0 else ps1
                fb = fbuf[i % 2]
                if i < 2:
                    P.wait("sync", "dve")
                else:
                    P.wmark("sync", f"cdve{i - 2}", "dve")
                P.sync.dma_start(out=xs[i % 2][:], in_=y1_dram[r0:r0 + 128, :],
                                 inc=[("lx", 16)])
                if i >= 2:
                    P.wmark("tensor", f"cdve{i - 2}", "dve")
                for j in range(12):
                    for h in range(2):
                        P.tensor.matmul(psl[:, 512 * h:512 * (h + 1)],
                                        Ebuf[j][:, r0:r0 + 128],
                                        ob[j][:, 512 * h:512 * (h + 1)],
                                        start=(j == 0), stop=(j == 11),
                                        inc=[("pe", 1)] if (j == 11 and h == 1) else ())
                P.mark(f"cmm{i}")
                P.wmark("vector", f"cmm{i}", "pe")
                P.wait("vector", "lx")
                P.wait("vector", "ld") if i == 0 else None
                if i >= 2:
                    P.wmark("vector", f"cmb{i - 2}", "so0" if i % 2 == 0 else "so1")
                dve_edge()
                P.vector.tensor_reduce(stat[:, 0:1], selr[:, 16 * i:16 * i + 4], AX.X,
                                       ALU.add, inc=[("dve", 1)])
                dve_edge()
                P.vector.tensor_scalar(stat[:, 1:2], stat[:, 0:1], 1.0, None, ALU.max,
                                       inc=[("dve", 1)])
                dve_edge()
                P.vector.reciprocal(stat[:, 2:3], stat[:, 1:2], inc=[("dve", 1)])
                dve_edge()
                P.vector.tensor_scalar(stat[:, 3:4], stat[:, 0:1], 0.0, None, ALU.is_le,
                                       inc=[("dve", 1)])
                dve_edge()
                P.vector.scalar_tensor_tensor(fb[:], psl[:], stat[:, 2:3],
                                              xs[i % 2][:], ALU.mult, ALU.add,
                                              inc=[("dve", 1)])
                dve_edge()
                P.vector.scalar_tensor_tensor(fb[:], nullb[:], stat[:, 3:4], fb[:],
                                              ALU.mult, ALU.add, inc=[("dve", 1)])
                P.mark(f"cdve{i}")
                P.wait("sync", "dve")
                P.sync.dma_start(out=out_y[r0:r0 + 128, :], in_=fb[:],
                                 inc=[("so0" if i % 2 == 0 else "so1", 16)])
                P.mark(f"cmb{i}")
            for sm in ["ste", "str", "stz", "stw"]:
                P.wait("sync", sm)
            for sm in ["gp", "gs", "gz", "gz1", "gq", "gqb", "gqc", "sw", "pool", "so0",
                       "so1", "dbg2", "ld", "lt", "lx", "lr", "lw", "dbg"]:
                P.wait("gpsimd", sm)
            P.wait("scalar", "act")
            P.wait("vector", "dve")
            P.wait("tensor", "pe")
            return


        with nc.Block() as block:
            for eng in Prog.ENGINES:
                def make(e):
                    def f(_):
                        P.active = e
                        P.reset()
                        program()
                    return f
                getattr(block, eng)(make(eng))

    return nc


def _consts():
    c = {}
    c["DIFFS_T"] = np.maximum(np.arange(C)[None, :] - 1 - np.arange(C)[:, None], 0) \
        .astype(np.float32)
    c["CAUSAL_T"] = (np.arange(C)[None, :] > np.arange(C)[:, None]).astype(np.float32)
    c["IMODBC"] = np.repeat((np.arange(WIN) % C)[None, :].astype(np.float32),
                             128, axis=0)
    c["IWINBC"] = np.repeat(np.arange(WIN, dtype=np.float32)[None, :], 128, axis=0)
    p = np.arange(128, dtype=np.float32)
    c["REV_COL"] = np.stack([C - 1 - 128 * tl - p for tl in range(4)], axis=1) \
        .astype(np.float32)
    c["TRIS_F"] = (np.arange(128)[:, None] < np.arange(128)[None, :]).astype(np.float32)
    c["ONES_ROW_F"] = np.ones((1, 128), np.float32)
    c["ONES_COL_B"] = np.ones((128, 1), ml_dtypes.bfloat16)
    c["ONES_COL_F"] = np.ones((128, 1), np.float32)
    c["SEL16"] = (np.arange(128)[:, None] // 16 == np.arange(8)[None, :]).astype(np.float32)
    c["SEL16T"] = c["SEL16"].T.copy()
    iw = (np.arange(128) % 16) // 4
    ii = np.arange(128) // 16
    c["SELW32"] = (ii[:, None] * 4 + iw[:, None] == np.arange(32)[None, :]).astype(np.float32)
    c["NQCAPC"] = np.repeat(np.array([[1024.0] * 4 + [512.0] * 4], np.float32), 128, axis=0)
    c["TGT"] = np.where(np.arange(128)[:, None] < 64, 1024.0, 512.0).astype(np.float32)
    c["IBLK4"] = np.repeat((np.arange(32) % 4)[None, :].astype(np.float32), 128, axis=0)
    c["IBLK128"] = np.repeat((np.arange(1024) % 128)[None, :].astype(ml_dtypes.bfloat16),
                             128, axis=0)
    c["IOTA_COL"] = np.arange(128, dtype=np.float32)[:, None]
    c["IDENT8"] = np.eye(8, dtype=np.float32)
    c["IDENT32"] = np.eye(32, dtype=np.float32)
    return c


_CONSTS = _consts()


def build_in_maps(L0_attn_out, L0_attn_W_write, L0_attn_W_read, L0_attn_decay, L0_attn_log_alpha,
           L0_ffn_w_qr, L0_ffn_w_kvr, L0_ffn_w_q, L0_ffn_w_kv, L0_ffn_w_out,
           L0_ffn_null_kv, L0_ffn_null_token):
    x = np.asarray(L0_attn_out, np.float32)
    Ww = np.asarray(L0_attn_W_write, np.float32)
    Wr = np.asarray(L0_attn_W_read, np.float32)
    w8 = np.concatenate([np.asarray(L0_ffn_w_qr, np.float32).T,
                         np.asarray(L0_ffn_w_kvr, np.float32).T], axis=1)
    wq = np.asarray(L0_ffn_w_q, np.float32)
    wkv = np.asarray(L0_ffn_w_kv, np.float32)
    wout = np.asarray(L0_ffn_w_out, np.float32)
    nkv = np.asarray(L0_ffn_null_kv, np.float32)
    ntok = np.asarray(L0_ffn_null_token, np.float32).reshape(1, D)

    in_maps = []
    for cid in range(8):
        b, w = cid // 4, cid % 4
        g = w
        xw = np.zeros((WIN + 1, D), np.float32)
        lo = 2048 * w
        if w == 0:
            xw[1:] = x[b, 0:2048]
        else:
            xw[:] = x[b, lo - 1:lo + 2048]
        expw = np.array([[max(2048.0 * (w - 1 - j), 0.0) for j in range(4)]], np.float32)
        maskw = np.array([[1.0 if j < w else 0.0 for j in range(4)]], np.float32)
        pv = np.zeros((32, 8), np.float32)
        for i in range(8):
            for wp in range(w):
                pv[i * 4 + wp, i] = 1.0
        nv = np.concatenate([nkv[1, g].reshape(8, 64), np.ones((8, 1), np.float32)],
                            axis=1).reshape(1, 8 * 65)
        possel = np.zeros((32, 20), np.float32)
        for j in range(12):
            possel[8 * (j // 3) % 32, 0] += 0
            possel[(8 * (j // 3) + g) % 32, j] = 1.0
        for j in range(12, 20):
            possel[(8 * ((j - 12) // 2) + 4 + g) % 32, j] = 1.0
        gselq_a = np.zeros((128, 8), np.float32)
        gselq_a[:, g] = 1.0
        gselk_a = np.zeros((128, 8), np.float32)
        gselk_a[:, 4 + g] = 1.0
        kbase20 = np.zeros((128, 20), np.float32)
        for j in range(12):
            kbase20[:, j] = 128.0 * (j % 3)
        for j in range(12, 20):
            kbase20[:, j] = 128.0 * ((j - 12) % 2)
        combsel_a = np.zeros((32, 12), np.float32)
        combsel2_a = np.zeros((32, 12), np.float32)
        for j in range(12):
            gp_ = j // 3
            combsel_a[4 * gp_ + w, j] = 1.0
            combsel2_a[8 * w + gp_, j] = 1.0
        kbase12 = np.repeat((128.0 * (np.arange(12) % 3))[None, :].astype(np.float32),
                            128, axis=0)
        prefixw_a = np.zeros((32, 32), np.float32)
        for i in range(8):
            for wnum in range(4):
                for wp in range(wnum):
                    prefixw_a[4 * i + wp, 4 * i + wnum] = 1.0
        nq32_a = np.array([[1024.0] * 16 + [512.0] * 16], np.float32).reshape(32, 1)
        cntsel_a = np.zeros((32, 20), np.float32)
        for j in range(12):
            cntsel_a[4 * g + j // 3, j] = 1.0
        for j in range(12, 20):
            cntsel_a[16 + 4 * g + (j - 12) // 2, j] = 1.0
        m = dict(
            x_win=xw, WwT=Ww.T.copy(), WrT=Wr.T.copy(),
            decay=np.asarray(L0_attn_decay, np.float32).reshape(1, 1),
            log_alpha=np.asarray(L0_attn_log_alpha, np.float32).reshape(1, 1),
            w8in=w8.copy(), wqT=wq[g].T.copy(), wkvT=wkv[g].T.copy(),
            woutT=wout[g].T.copy(),
            nk_in=nkv[0, g].reshape(4, 128).T.copy(),
            nv_in=nv,
            null_in=np.repeat(ntok, 128, axis=0),
            EXPW=expw, MASKW=maskw, PREVW=pv,
            WOFF=np.array([[2048.0 * w]], np.float32),
            POSSEL=possel, GSELQ=gselq_a, GSELK=gselk_a, KBASE20=kbase20,
            COMBSEL=combsel_a, COMBSEL2=combsel2_a, KBASE12=kbase12,
            PREFIXW=prefixw_a, NQ32=nq32_a, CNTSEL=cntsel_a,
            **_CONSTS,
        )
        in_maps.append(m)

    return in_maps


def kernel(**inputs):
    nc = build_nc()
    in_maps = build_in_maps(**inputs)
    res = run_bass_kernel_spmd(nc, in_maps, core_ids=list(range(8)))
    global _LAST_RES
    _LAST_RES = res
    out = np.zeros((B, T, D), np.float32)
    for cid in range(8):
        b, w = cid // 4, cid % 4
        out[b, 2048 * w:2048 * (w + 1)] = res.results[cid]["out_y"]
    return out


# version 08
# speedup vs baseline: 1.0457x; 1.0457x over previous
import numpy as np
import ml_dtypes
from contextlib import ExitStack

import concourse.bass as bass
import concourse.mybir as mybir
from concourse.bass_utils import run_bass_kernel_spmd
from concourse.bass_core_rust import IndirectOffsetOnAxis

F32 = mybir.dt.float32
BF16 = mybir.dt.bfloat16
I32 = mybir.dt.int32
AF = mybir.ActivationFunctionType
ALU = mybir.AluOpType
AX = mybir.AxisListType

B, T, D = 2, 8192, 1024
WIN = 2048
C = 512
NCH = WIN // C
NT = WIN // 128
G, NQ, NK, H, DH = 4, 1024, 512, 8, 64
DI = H * DH
QCAP = 352
QS = G * QCAP
KCAP = 256
KS = 4 * KCAP
PADW = 512
BIS_IT = 40
AUGW = 1032
SHIFT = -24.0
RG = [[0, 1, 2, 3], [4, 5, 6, 7]]
BIGIDX = float(1 << 22)

PHASE = 3


class Prog:
    ENGINES = ("tensor", "vector", "scalar", "gpsimd", "sync")

    def __init__(self, nc, sems):
        self.nc = nc
        self.sems = sems
        self.counts = {k: 0 for k in sems}
        self.active = None
        self.marks = {}

    def reset(self):
        self.counts = {k: 0 for k in self.sems}
        self.marks = {}

    def _do(self, eng, opname, *args, inc=(), **kw):
        if self.active == eng:
            instr = getattr(getattr(self.nc, eng), opname)(*args, **kw)
            for (s, n) in inc:
                instr = instr.then_inc(self.sems[s], n)
        for (s, n) in inc:
            self.counts[s] += n

    def wait(self, eng, sem, val=None):
        v = self.counts[sem] if val is None else val
        if self.active == eng:
            getattr(self.nc, eng).wait_ge(self.sems[sem], v)

    def mark(self, name):
        self.marks[name] = dict(self.counts)

    def wmark(self, eng, name, *semlist):
        for s in semlist:
            if self.active == eng:
                getattr(self.nc, eng).wait_ge(self.sems[s], self.marks[name][s])

    def __getattr__(self, name):
        if name in Prog.ENGINES:
            prog = self

            class _E:
                def __getattr__(self, op):
                    def f(*a, inc=(), **kw):
                        prog._do(name, op, *a, inc=inc, **kw)
                    return f
            return _E()
        raise AttributeError(name)


def build_nc():
    nc = bass.Bass("TRN2", target_bir_lowering=False)
    dt = nc.dram_tensor

    x_win = dt("x_win", [WIN + 1, D], F32, kind="ExternalInput")
    WwT = dt("WwT", [D, D], F32, kind="ExternalInput")
    WrT = dt("WrT", [D, D], F32, kind="ExternalInput")
    decay = dt("decay", [1, 1], F32, kind="ExternalInput")
    log_alpha = dt("log_alpha", [1, 1], F32, kind="ExternalInput")
    w8in = dt("w8in", [D, 8], F32, kind="ExternalInput")
    wqT = dt("wqT", [D, DI], F32, kind="ExternalInput")
    wkvT = dt("wkvT", [D, 2 * DI], F32, kind="ExternalInput")
    woutT = dt("woutT", [DI, D], F32, kind="ExternalInput")
    nk_in = dt("nk_in", [128, 4], F32, kind="ExternalInput")
    nv_in = dt("nv_in", [1, 8 * 65], F32, kind="ExternalInput")
    null_in = dt("null_in", [128, D], F32, kind="ExternalInput")
    DIFFS_T = dt("DIFFS_T", [C, C], F32, kind="ExternalInput")
    CAUSAL_T = dt("CAUSAL_T", [C, C], F32, kind="ExternalInput")
    IMODBC = dt("IMODBC", [128, WIN], F32, kind="ExternalInput")
    IWINBC = dt("IWINBC", [128, WIN], F32, kind="ExternalInput")
    REV_COL = dt("REV_COL", [128, 4], F32, kind="ExternalInput")
    EXPW = dt("EXPW", [1, 4], F32, kind="ExternalInput")
    MASKW = dt("MASKW", [1, 4], F32, kind="ExternalInput")
    TRIS_F = dt("TRIS_F", [128, 128], F32, kind="ExternalInput")
    ONES_ROW_F = dt("ONES_ROW_F", [1, 128], F32, kind="ExternalInput")
    ONES_COL_B = dt("ONES_COL_B", [128, 1], BF16, kind="ExternalInput")
    ONES_COL_F = dt("ONES_COL_F", [128, 1], F32, kind="ExternalInput")
    SEL16 = dt("SEL16", [128, 8], F32, kind="ExternalInput")
    SEL16T = dt("SEL16T", [8, 128], F32, kind="ExternalInput")
    SELW32 = dt("SELW32", [128, 32], F32, kind="ExternalInput")
    PREVW = dt("PREVW", [32, 8], F32, kind="ExternalInput")
    NQCAPC = dt("NQCAPC", [128, 8], F32, kind="ExternalInput")
    TGT = dt("TGT", [128, 1], F32, kind="ExternalInput")
    IBLK4 = dt("IBLK4", [128, 32], F32, kind="ExternalInput")
    IBLK128 = dt("IBLK128", [128, 1024], BF16, kind="ExternalInput")
    IOTA_COL = dt("IOTA_COL", [128, 1], F32, kind="ExternalInput")
    IDENT8 = dt("IDENT8", [8, 8], F32, kind="ExternalInput")
    IDENT32 = dt("IDENT32", [32, 32], F32, kind="ExternalInput")
    WOFF = dt("WOFF", [1, 1], F32, kind="ExternalInput")
    POSSEL = dt("POSSEL", [32, 20], F32, kind="ExternalInput")
    GSELQ = dt("GSELQ", [128, 8], F32, kind="ExternalInput")
    GSELK = dt("GSELK", [128, 8], F32, kind="ExternalInput")
    KBASE20 = dt("KBASE20", [128, 20], F32, kind="ExternalInput")
    COMBSEL = dt("COMBSEL", [32, 12], F32, kind="ExternalInput")
    COMBSEL2 = dt("COMBSEL2", [32, 12], F32, kind="ExternalInput")
    KBASE12 = dt("KBASE12", [128, 12], F32, kind="ExternalInput")
    PREFIXW = dt("PREFIXW", [32, 32], F32, kind="ExternalInput")
    CNTSEL = dt("CNTSEL", [32, 20], F32, kind="ExternalInput")
    NQ32 = dt("NQ32", [32, 1], F32, kind="ExternalInput")

    out_y = dt("out_y", [WIN, D], F32, kind="ExternalOutput")
    dbg_y1 = dt("dbg_y1", [WIN, D], F32, kind="ExternalOutput")
    dbg_z = dt("dbg_z", [WIN + 1, D], BF16, kind="ExternalOutput")
    dbg_reads = dt("dbg_reads", [WIN, D], BF16, kind="ExternalOutput")
    dbg_wst = dt("dbg_wst", [D, D], BF16, kind="ExternalOutput")
    dbg_aseg = dt("dbg_aseg", [D, D], BF16, kind="ExternalOutput")
    dbg_mix = dt("dbg_mix", [128, 6144], BF16, kind="ExternalOutput")
    dbg_s = dt("dbg_s", [128, 32], F32, kind="ExternalOutput")

    z_t_dram = dt("z_t_dram", [WIN + 1, D], BF16)
    reads_dram = dt("reads_dram", [WIN, D], BF16)
    y1_dram = dt("y1_dram", [WIN, D], F32)
    z2lo_dram = dt("z2lo_dram", [WIN, D], BF16)
    aseg_bi = dt("aseg_bi", [D, D], BF16)
    aseg_bo = dt("aseg_bo", [4 * D, D], BF16)
    wst_dram = dt("wst_dram", [D, D], BF16)
    logit_bi = dt("logit_bi", [8, WIN], F32)
    logit_bo = dt("logit_bo", [32, WIN], F32)
    z2aug_bi = dt("z2aug_bi", [WIN, AUGW], BF16)
    z2aug_bo = dt("z2aug_bo", [4 * WIN, AUGW], BF16)
    pos_bi = dt("pos_bi", [8, PADW], F32)
    pos_bo = dt("pos_bo", [32, PADW], F32)
    qcomp_dram = dt("qcomp_dram", [QS, D], BF16)
    kvcomp_dram = dt("kvcomp_dram", [KS, D], BF16)
    qlog_dram = dt("qlog_dram", [QS, 1], F32)
    o_bi = dt("o_bi", [QS, D], BF16)
    o_bo = dt("o_bo", [4 * QS, D], BF16)
    winbuf = dt("winbuf", [WIN, D], F32)

    al = ExitStack()
    sbf = lambda n, s, d: al.enter_context(nc.sbuf_tensor(n, s, d))
    pst = lambda n, s: al.enter_context(nc.psum_tensor(n, s, F32))

    with al:
        z_f = [sbf(f"z_f{i}", [128, WIN + 16], BF16) for i in range(8)]
        z_fs = [sbf(f"z_fs{i}", [128, WIN + 16], BF16) for i in range(8)]
        gpatw = sbf("gpatw", [128, WIN], BF16)

        AR2N = 47104
        AR2 = sbf("AR2", [128, AR2N], BF16)

        def a2(off, w):
            return AR2[:, off:off + w]

        WwT_b = [a2(1024 * i, 1024) for i in range(8)]
        Wst = [[a2(8192 + 8192 * p + 1024 * i, 1024) for i in range(8)] for p in range(2)]
        wk_sc = [a2(24576 + 1024 * i, 1024) for i in range(4)]
        v_sb = [a2(28672 + 1024 * i, 1024) for i in range(4)]
        SMt = [a2(32768 + 512 * i, 512) for i in range(4)]
        M_T = [a2(34816 + 512 * i, 512) for i in range(4)]
        rd_t = [a2(36864 + 1024 * i, 1024) for i in range(2)]
        WrT_b = [a2(38912 + 1024 * i, 1024) for i in range(8)]
        readsf = [a2(1024 * i, 1024) for i in range(8)]
        wstT_b = [a2(8192 + 1024 * i, 1024) for i in range(8)]
        m2t = [a2(24576 + 1024 * i, 1024) for i in range(8)]
        zfg = [a2(32768 + 128 * i, 128) for i in range(8)]
        wq_b = [a2(512 * i, 512) for i in range(8)]
        wkv_b = [a2(4096 + 1024 * i, 1024) for i in range(8)]
        wout_b = [a2(12288 + 1024 * i, 1024) for i in range(4)]
        vtok = [a2(16384 + 520 * i, 520) for i in range(8)]
        expT = [a2(20544 + 1408 * i, 1408) for i in range(8)]
        oT = [a2(31808 + 1408 * i, 1408) for i in range(4)]
        recb = a2(37440, 1408)[0:64, :]
        grow = [a2(38848 + 1032 * i, AUGW) for i in range(3)]
        kvf = [a2(41944 + 1024 * i, 1024) for i in range(4)]

        xs = [sbf(f"xs{i}", [128, D], F32) for i in range(2)]
        f1 = sbf("f1", [128, D], F32)
        gen_tok = sbf("gen_tok", [128, D], BF16)
        stat = sbf("stat", [128, 16], F32)
        scal = sbf("scal", [128, 16], F32)
        tiny = sbf("tiny", [1, 64], F32)
        rowb = sbf("rowb", [1, 64], F32)
        gw4 = sbf("gw4", [128, 4], F32)
        logit_tok = sbf("logit_tok", [128, 8 * NT], F32)
        bw = sbf("bw", [128, 16], F32)
        cnt32 = sbf("cnt32", [32, 1], F32)
        cntb32 = sbf("cntb32", [128, 32], F32)
        selr = sbf("selr", [128, 16 * NT], F32)
        offrow = sbf("offrow", [1, 8], F32)
        offib = sbf("offib", [128, 8], F32)
        taub = sbf("taub", [128, 8], F32)
        pose = sbf("pose", [128, 48], F32)
        ehi = sbf("ehi", [128, 32], F32)
        idx_f = sbf("idx_f", [128, 24], F32)
        idx_i = sbf("idx_i", [128, 24], I32)

        AR4N = 6016
        AR4 = sbf("AR4", [128, AR4N], F32)

        def a4(off, w):
            return AR4[:, off:off + w]

        logits_sb = a4(0, 2048)[0:8, :]
        ocomb = [a4(1024, 1024), a4(2048, 1024)]
        bis = a4(2048, 512)
        elo = a4(2560, 1024)
        possb = a4(3584, 1024)[0:32, :]
        sumrow = a4(4608, 1408)[0:1, :]
        blk16 = a4(3072, 128)

        tris = sbf("tris", [128, 128], F32)
        onesr = sbf("onesr", [1, 128], F32)
        onescb = sbf("onescb", [128, 1], BF16)
        onescf = sbf("onescf", [128, 1], F32)
        sel16 = sbf("sel16", [128, 8], F32)
        sel16t = sbf("sel16t", [8, 128], F32)
        selw32 = sbf("selw32", [128, 32], F32)
        prevw = sbf("prevw", [32, 8], F32)
        nqcap = sbf("nqcap", [128, 8], F32)
        tgt = sbf("tgt", [128, 1], F32)
        iblk4 = sbf("iblk4", [128, 32], F32)
        iblk128 = sbf("iblk128", [128, 1024], BF16)
        iota_c = sbf("iota_c", [128, 1], F32)
        ident8 = sbf("ident8", [8, 8], F32)
        ident32 = sbf("ident32", [32, 32], F32)
        rev4 = sbf("rev4", [128, 4], F32)
        nullb = a4(0, 1024)
        nkb = sbf("nkb", [128, 4], BF16)
        nvb = sbf("nvb", [1, 8 * 65], BF16)
        possel = sbf("possel", [32, 20], F32)
        gselq = sbf("gselq", [128, 8], F32)
        gselk = sbf("gselk", [128, 8], F32)
        kbase20 = sbf("kbase20", [128, 20], F32)
        combsel = sbf("combsel", [32, 12], F32)
        combsel2 = sbf("combsel2", [32, 12], F32)
        kbase12 = sbf("kbase12", [128, 12], F32)
        prefixw = sbf("prefixw", [32, 32], F32)
        cntsel = sbf("cntsel", [32, 20], F32)
        nq32 = sbf("nq32", [32, 1], F32)
        cnt20b = sbf("cnt20b", [128, 20], F32)
        valid20 = sbf("valid20", [128, 20], F32)
        idxq_i = sbf("idxq_i", [128, 20], I32)
        cm_f = sbf("cm_f", [128, 12], F32)
        cm_i = sbf("cm_i", [128, 12], I32)
        cm2_i = sbf("cm2_i", [128, 12], I32)
        snapw = sbf("snapw", [1, 16], F32)

        w8hi = sbf("w8hi", [128, 64], BF16)
        w8lo = sbf("w8lo", [128, 64], BF16)
        expN = a2(38848, QS)[0:1, :]
        sig_kv = sbf("sig_kv", [128, 8], F32)
        sig_q = sbf("sig_q", [128, 12], F32)
        kbias = sbf("kbias", [128, 8], F32)
        zt_eg = [a2(34816 + 1024 * i, 1024) for i in range(2)]
        kf = [z_fs[i].ap()[:, 0:KS + 1] for i in range(4)]
        qh = [z_fs[4 + i].ap()[:, 0:QS] for i in range(4)]
        qf = [z_f[i].ap()[:, 0:QS] for i in range(8)]

        ps0 = pst("ps0", [128, 1024])
        ps1 = pst("ps1", [128, 1024])
        ps2 = pst("ps2", [128, 1024])
        ps3 = pst("ps3", [128, 1024])

        sems = {k: al.enter_context(nc.semaphore(k))
                for k in ["ld", "lt", "lx", "stz", "str", "ste", "stw", "gp", "gs", "gz",
                          "lr", "lw", "gz1", "gq", "gqb", "gqc", "sw", "pool", "so0", "so1", "pe", "dve", "act", "cc", "dbg", "dbg2"]}
        P = Prog(nc, sems)

        def dve_edge():
            P.wait("vector", "dve")

        def ln_tile(src_f32, rn, out_bf):
            P.vector.tensor_reduce(stat[0:rn, 0:1], src_f32[0:rn, :], AX.X, ALU.add,
                                   inc=[("dve", 1)])
            P.wait("scalar", "dve")
            P.scalar.activation(f1[0:rn, :], src_f32[0:rn, :], AF.Square, inc=[("act", 1)])
            P.wait("vector", "act")
            P.vector.tensor_reduce(stat[0:rn, 1:2], f1[0:rn, :], AX.X, ALU.add, inc=[("dve", 1)])
            dve_edge()
            P.vector.tensor_scalar(stat[0:rn, 2:3], stat[0:rn, 0:1], 1.0 / D, None, ALU.mult,
                                   inc=[("dve", 1)])
            dve_edge()
            P.vector.tensor_scalar(stat[0:rn, 3:4], stat[0:rn, 1:2], 1.0 / D, None, ALU.mult,
                                   inc=[("dve", 1)])
            dve_edge()
            P.vector.tensor_tensor(stat[0:rn, 4:5], stat[0:rn, 2:3], stat[0:rn, 2:3], ALU.mult,
                                   inc=[("dve", 1)])
            dve_edge()
            P.vector.tensor_tensor(stat[0:rn, 5:6], stat[0:rn, 3:4], stat[0:rn, 4:5],
                                   ALU.subtract, inc=[("dve", 1)])
            P.wait("scalar", "dve")
            P.scalar.activation(stat[0:rn, 6:7], stat[0:rn, 5:6], AF.Ln, bias=scal[0:rn, 8:9],
                                inc=[("act", 1)])
            P.wait("scalar", "act")
            P.scalar.activation(stat[0:rn, 7:8], stat[0:rn, 6:7], AF.Exp, scale=-0.5,
                                inc=[("act", 1)])
            P.wait("vector", "act")
            P.vector.tensor_scalar(out_bf[0:rn, :], src_f32[0:rn, :], stat[0:rn, 2:3],
                                   stat[0:rn, 7:8], ALU.subtract, op1=ALU.mult, inc=[("dve", 1)])

        def program():
            LG = scal[:, 5:6]
            ALPHA = scal[:, 6:7]
            GC = scal[:, 7:8]
            WOFFB = scal[:, 14:15]
            WJ = [stat[:, 8 + j:9 + j] for j in range(4)]

            P.vector.memset(tiny[:], 0.0, inc=[("dve", 1)])
            P.vector.memset(rowb[:], 0.0, inc=[("dve", 1)])
            P.vector.memset(bw[:], 0.0, inc=[("dve", 1)])
            P.vector.memset(stat[:], 0.0, inc=[("dve", 1)])
            P.vector.memset(scal[:], 0.0, inc=[("dve", 1)])
            P.wait("sync", "dve")
            for (dst, src) in [(tris, TRIS_F), (onesr, ONES_ROW_F), (onescb, ONES_COL_B),
                               (onescf, ONES_COL_F), (sel16, SEL16), (sel16t, SEL16T),
                               (selw32, SELW32), (prevw, PREVW), (nqcap, NQCAPC),
                               (tgt, TGT), (iblk4, IBLK4), (iblk128, IBLK128),
                               (iota_c, IOTA_COL), (ident8, IDENT8), (ident32, IDENT32),
                               (rev4, REV_COL), (possel, POSSEL), (gselq, GSELQ),
                               (gselk, GSELK), (kbase20, KBASE20), (combsel, COMBSEL),
                               (combsel2, COMBSEL2), (kbase12, KBASE12),
                               (prefixw, PREFIXW), (nq32, NQ32),
                               (cntsel, CNTSEL)]:
                P.sync.dma_start(out=dst[:], in_=src[:], inc=[("ld", 16)])
            P.sync.dma_start(out=tiny[0:1, 0:1], in_=decay[:], inc=[("ld", 16)])
            P.sync.dma_start(out=tiny[0:1, 1:2], in_=log_alpha[:], inc=[("ld", 16)])
            P.sync.dma_start(out=tiny[0:1, 8:12], in_=EXPW[:], inc=[("ld", 16)])
            P.sync.dma_start(out=tiny[0:1, 12:16], in_=MASKW[:], inc=[("ld", 16)])
            P.sync.dma_start(out=tiny[0:1, 30:31], in_=WOFF[:], inc=[("ld", 16)])
            for i in range(8):
                for (dsts, srcs, w) in [(WwT_b, WwT, D), (WrT_b, WrT, D)]:
                    P.wait("sync", "dve")
                    P.sync.dma_start(out=xs[0][:, 0:w], in_=srcs[128 * i:128 * (i + 1), :],
                                     inc=[("ld", 16)])
                    P.wait("vector", "ld")
                    P.vector.tensor_copy(dsts[i][:], xs[0][:, 0:w], inc=[("dve", 1)])
            for i in range(8):
                P.wait("sync", "dve")
                P.sync.dma_start(out=xs[0][:, 0:8], in_=w8in[128 * i:128 * (i + 1), :],
                                 inc=[("ld", 16)])
                P.wait("vector", "ld")
                P.vector.tensor_copy(w8hi[:, 8 * i:8 * (i + 1)], xs[0][:, 0:8], inc=[("dve", 1)])
                dve_edge()
                P.vector.tensor_tensor(w8lo[:, 8 * i:8 * (i + 1)], xs[0][:, 0:8],
                                       w8hi[:, 8 * i:8 * (i + 1)], ALU.subtract, inc=[("dve", 1)])
            for et in range(8):
                P.vector.memset(Wst[0][et][:], 0.0, inc=[("dve", 1)])
            P.wait("scalar", "ld")
            P.scalar.activation(tiny[0:1, 2:3], tiny[0:1, 0:1], AF.Exp, scale=-1.0,
                                inc=[("act", 1)])
            P.wait("vector", "act")
            P.vector.tensor_scalar(tiny[0:1, 3:4], tiny[0:1, 2:3], 1.0, None, ALU.add,
                                   inc=[("dve", 1)])
            dve_edge()
            P.vector.reciprocal(tiny[0:1, 4:5], tiny[0:1, 3:4], inc=[("dve", 1)])
            P.wait("scalar", "dve")
            P.scalar.activation(tiny[0:1, 5:6], tiny[0:1, 4:5], AF.Ln, inc=[("act", 1)])
            P.scalar.activation(tiny[0:1, 6:7], tiny[0:1, 1:2], AF.Exp, inc=[("act", 1)])
            P.wait("vector", "act")
            P.vector.tensor_scalar(tiny[0:1, 16:20], tiny[0:1, 8:12], tiny[0:1, 5:6], None,
                                   ALU.mult, inc=[("dve", 1)])
            P.wait("scalar", "dve")
            P.scalar.activation(tiny[0:1, 20:24], tiny[0:1, 16:20], AF.Exp, inc=[("act", 1)])
            P.wait("vector", "act")
            P.vector.tensor_tensor(tiny[0:1, 24:28], tiny[0:1, 20:24], tiny[0:1, 12:16],
                                   ALU.mult, inc=[("dve", 1)])
            P.wait("tensor", "dve")
            P.wait("tensor", "act")
            P.wait("tensor", "ld")
            P.tensor.matmul(ps0[:, 0:16], onesr[:], tiny[0:1, 0:16], start=True, stop=True,
                            inc=[("pe", 1)])
            P.tensor.matmul(ps0[:, 16:20], onesr[:], tiny[0:1, 24:28], start=True, stop=True,
                            inc=[("pe", 1)])
            P.tensor.matmul(ps0[:, 20:21], onesr[:], tiny[0:1, 30:31], start=True, stop=True,
                            inc=[("pe", 1)])
            P.wait("vector", "pe")
            P.vector.tensor_copy(scal[:, 0:14], ps0[:, 0:14], inc=[("dve", 1)])
            dve_edge()
            P.vector.tensor_copy(stat[:, 8:12], ps0[:, 16:20], inc=[("dve", 1)])
            dve_edge()
            P.vector.tensor_copy(scal[:, 14:15], ps0[:, 20:21], inc=[("dve", 1)])
            P.wait("scalar", "dve")
            P.scalar.activation(scal[:, 7:8], LG, AF.Exp, scale=float(C), inc=[("act", 1)])
            P.scalar.activation(gw4[:], rev4[:], AF.Exp, scale=LG, inc=[("act", 1)])
            P.vector.memset(scal[:, 8:9], 1e-5, inc=[("dve", 1)])
            dve_edge()
            P.vector.memset(scal[:, 10:11], SHIFT, inc=[("dve", 1)])
            for o in range(0, WIN, D):
                P.wait("sync", "act")
                P.wait("sync", "dve")
                P.sync.dma_start(out=xs[0][:], in_=IWINBC[:, o:o + D], inc=[("ld", 16)])
                P.wait("scalar", "ld")
                P.scalar.activation(gpatw[:, o:o + D], xs[0][:], AF.Exp, scale=LG,
                                    inc=[("act", 1)])
            for kt in range(4):
                P.wait("sync", "act")
                P.wait("sync", "dve")
                P.sync.dma_start(out=xs[0][:, 0:C], in_=DIFFS_T[128 * kt:128 * (kt + 1), :],
                                 inc=[("ld", 16)])
                P.wait("scalar", "ld")
                P.scalar.activation(f1[:, 0:C], xs[0][:, 0:C], AF.Exp, scale=LG, inc=[("act", 1)])
                P.wait("sync", "act")
                P.sync.dma_start(out=xs[0][:, 0:C], in_=CAUSAL_T[128 * kt:128 * (kt + 1), :],
                                 inc=[("ld", 16)])
                P.wait("vector", "ld")
                P.wait("vector", "act")
                P.vector.tensor_tensor(M_T[kt][:], f1[:, 0:C], xs[0][:, 0:C], ALU.mult,
                                       inc=[("dve", 1)])
            P.mark("setup")

            nrow = WIN + 1
            ntile = (nrow + 127) // 128

            def ln_emit(i):
                r0, rn = 128 * i, min(128, nrow - 128 * i)
                buf = xs[i % 2]
                if i >= 2:
                    P.wmark("sync", f"ln{i - 2}", "dve")
                else:
                    P.wmark("sync", "setup", "dve", "act")
                P.sync.dma_start(out=buf[0:rn, :], in_=x_win[r0:r0 + rn, :], inc=[("ld", 16)])
                P.wait("vector", "ld")
                P.wait("scalar", "ld")
                if i < 2:
                    P.wmark("vector", "setup", "act")
                    P.wmark("scalar", "setup", "dve")
                if i >= 1:
                    P.wait("vector", "stz")
                ln_tile(buf, rn, gen_tok)
                P.wait("sync", "dve")
                P.sync.dma_start(out=z_t_dram[r0:r0 + rn, :], in_=gen_tok[0:rn, :],
                                 inc=[("stz", 16)])
                P.mark(f"ln{i}")

            def ztrans_chunk(cc):
                P.wmark("sync", f"ln{4 * cc + 4}", "stz")
                for dl in range(8):
                    P.sync.dma_start(
                        out=z_f[dl][:, 16 + 512 * cc:16 + 512 * (cc + 1)],
                        in_=z_t_dram[1 + 512 * cc:1 + 512 * (cc + 1),
                                     128 * dl:128 * (dl + 1)],
                        transpose=True,
                        inc=[("lt" if cc % 2 == 0 else "lx", 16)])

            def zfs_chunk(cc):
                o = 512 * cc
                P.wmark("sync", f"ln{4 * cc + 4}", "dve")
                P.wait("sync", "act")
                P.sync.dma_start(out=f1[:, 512:1024], in_=IMODBC[:, o:o + 512],
                                 inc=[("lw", 16)])
                P.wait("scalar", "lw")
                P.wait("scalar", "dve")
                P.scalar.activation(f1[:, 0:512], f1[:, 512:1024], AF.Exp, scale=LG,
                                    inc=[("act", 1)])
                P.wait("vector", "act")
                P.wait("vector", "lt" if cc % 2 == 0 else "lx")
                for dl in range(8):
                    P.vector.tensor_tensor(z_fs[dl][:, 16 + o:16 + o + 512],
                                           z_f[dl][:, 16 + o:16 + o + 512],
                                           f1[:, 0:512], ALU.mult, inc=[("dve", 1)])
                P.mark(f"zfc{cc}")

            for i in range(5):
                ln_emit(i)
            P.wmark("sync", "ln0", "stz")
            for dl in range(8):
                P.sync.dma_start(out=z_fs[dl][:, 0:16],
                                 in_=z_t_dram[0:16, 128 * dl:128 * (dl + 1)],
                                 transpose=True, inc=[("lx", 16)])
            P.wait("vector", "lx")
            for dl in range(8):
                P.vector.tensor_copy(z_f[dl][:, 15:16], z_fs[dl][:, 0:1],
                                     inc=[("dve", 1)])
            ztrans_chunk(0)
            zfs_chunk(0)

            def scan_chunk(c):
                cw = C * c
                pcur, pnxt = c % 2, (c + 1) % 2
                P.wmark("gpsimd", f"ln{4 * c + 4}", "stz")
                for tt in range(4):
                    if c > 0:
                        P.wmark("gpsimd", f"Adone{c - 1}", "pe")
                    P.gpsimd.dma_start(out=wk_sc[tt][:],
                                       in_=z_t_dram[cw + 128 * tt:cw + 128 * (tt + 1), :],
                                       inc=[("gp", 16)])
                P.wait("scalar", "gp")
                for tt in range(4):
                    P.scalar.mul(wk_sc[tt][:], wk_sc[tt][:], gw4[:, tt:tt + 1], inc=[("act", 1)])
                P.mark(f"wk{c}")
                P.wmark("tensor", f"zfc{c}", "dve")
                P.wait("tensor", "lt" if c % 2 == 0 else "lx")
                if c > 0:
                    P.wmark("tensor", f"Adone{c - 1}", "dve")
                for tt in range(4):
                    psv = ps0 if tt % 2 == 0 else ps1
                    if tt >= 2:
                        P.wmark("tensor", f"vev{c}_{tt - 2}", "dve")
                    for hf in range(2):
                        for kt in range(8):
                            P.tensor.matmul(
                                psv[:, 512 * hf:512 * (hf + 1)],
                                z_f[kt][:, 16 + cw + 128 * tt:16 + cw + 128 * (tt + 1)],
                                WwT_b[kt][:, 512 * hf:512 * (hf + 1)],
                                start=(kt == 0), stop=(kt == 7),
                                inc=[("pe", 1)] if kt == 7 else ())
                    P.wait("vector", "pe")
                    P.vector.tensor_copy(v_sb[tt][:], psv[:], inc=[("dve", 1)])
                    P.mark(f"vev{c}_{tt}")
                for kt in range(4):
                    pss = ps2[:, 0:512] if kt % 2 == 0 else ps2[:, 512:1024]
                    if kt >= 2:
                        P.wmark("tensor", f"sev{c}_{kt - 2}", "dve")
                    for jt in range(8):
                        P.tensor.matmul(
                            pss,
                            z_f[jt][:, 15 + cw + 128 * kt:15 + cw + 128 * (kt + 1)],
                            z_f[jt][:, 16 + cw:16 + cw + C],
                            start=(jt == 0), stop=(jt == 7),
                            inc=[("pe", 1)] if jt == 7 else ())
                    P.wait("vector", "pe")
                    P.vector.tensor_tensor(SMt[kt][:], pss, M_T[kt][:], ALU.mult,
                                           inc=[("dve", 1)])
                    P.mark(f"sev{c}_{kt}")
                P.wmark("tensor", f"wk{c}", "act")
                P.wmark("tensor", f"sev{c}_3", "dve")
                P.wmark("tensor", f"vev{c}_3", "dve")
                for ct in range(4):
                    psr = ps3 if ct % 2 == 0 else ps0
                    if ct == 1:
                        P.wmark("tensor", f"vev{c}_2", "dve")
                    if ct >= 2:
                        P.wmark("tensor", f"rev{c}_{ct - 2}", "dve")
                    for hf in range(2):
                        for kt in range(4):
                            P.tensor.matmul(
                                psr[:, 512 * hf:512 * (hf + 1)],
                                SMt[kt][:, 128 * ct:128 * (ct + 1)],
                                v_sb[kt][:, 512 * hf:512 * (hf + 1)],
                                start=(kt == 0), stop=False)
                        for jt in range(8):
                            P.tensor.matmul(
                                psr[:, 512 * hf:512 * (hf + 1)],
                                z_fs[jt][:, 16 + cw + 128 * ct:16 + cw + 128 * (ct + 1)],
                                Wst[pcur][jt][:, 512 * hf:512 * (hf + 1)],
                                start=False, stop=(jt == 7),
                                inc=[("pe", 1)] if jt == 7 else ())
                    P.wait("vector", "pe")
                    bufr = rd_t[ct % 2]
                    P.wait("vector", "str")
                    P.vector.tensor_copy(bufr[:], psr[:], inc=[("dve", 1)])
                    P.wait("gpsimd", "dve")
                    P.gpsimd.dma_start(out=reads_dram[cw + 128 * ct:cw + 128 * (ct + 1), :],
                                       in_=bufr[:], inc=[("str", 16)])
                    P.mark(f"rst{c}_{ct}")
                    P.mark(f"rev{c}_{ct}")
                for et in range(8):
                    for hf in range(2):
                        idx = 2 * et + hf
                        psa = ps1[:, 0:512] if idx % 2 == 0 else ps1[:, 512:1024]
                        if idx >= 2:
                            P.wmark("tensor", f"fold{c}_{idx - 2}", "dve")
                        if idx < 2:
                            P.wmark("tensor", f"vev{c}_3", "dve")
                        for tt in range(4):
                            P.tensor.matmul(
                                psa,
                                wk_sc[tt][:, 128 * et:128 * (et + 1)],
                                v_sb[tt][:, 512 * hf:512 * (hf + 1)],
                                start=(tt == 0), stop=(tt == 3),
                                inc=[("pe", 1)] if tt == 3 else ())
                        P.wait("vector", "pe")
                        P.vector.scalar_tensor_tensor(
                            Wst[pnxt][et][:, 512 * hf:512 * (hf + 1)],
                            Wst[pcur][et][:, 512 * hf:512 * (hf + 1)],
                            GC, psa, ALU.mult, ALU.add, inc=[("dve", 1)])
                        P.mark(f"fold{c}_{idx}")
                P.mark(f"fold{c}")
                P.mark(f"Adone{c}")

            scan_chunk(0)
            for c in range(1, NCH):
                for i in range(4 * c + 1, 4 * c + 5):
                    ln_emit(i)
                ztrans_chunk(c)
                zfs_chunk(c)
                scan_chunk(c)

            pfin = NCH % 2
            P.wmark("sync", f"fold{NCH - 1}", "dve")
            for et in range(8):
                P.sync.dma_start(out=aseg_bi[128 * et:128 * (et + 1), :],
                                 in_=Wst[pfin][et][:], inc=[("stw", 16)])
            P.wait("gpsimd", "stw")
            P.gpsimd.collective_compute("AllGather", ALU.bypass, replica_groups=RG,
                                        ins=[aseg_bi.ap().opt()], outs=[aseg_bo.ap().opt()],
                                        inc=[("cc", 1)])
            P.wait("gpsimd", "cc")
            pw = (NCH + 1) % 2
            for et in range(8):
                P.wait("gpsimd", "dve")
                for j in range(4):
                    P.wmark("gpsimd", f"Adone{NCH - 1}", "pe")
                    P.gpsimd.dma_start(out=v_sb[j][:],
                                       in_=aseg_bo[D * j + 128 * et:D * j + 128 * (et + 1), :],
                                       inc=[("gs", 16)])
                P.wait("vector", "gs")
                P.vector.tensor_scalar(Wst[pw][et][:], v_sb[0][:], WJ[0], None, ALU.mult,
                                       inc=[("dve", 1)])
                for j in range(1, 4):
                    dve_edge()
                    P.vector.scalar_tensor_tensor(Wst[pw][et][:], v_sb[j][:], WJ[j],
                                                  Wst[pw][et][:], ALU.mult, ALU.add,
                                                  inc=[("dve", 1)])
                P.wait("sync", "dve")
                P.sync.dma_start(out=wst_dram[128 * et:128 * (et + 1), :], in_=Wst[pw][et][:],
                                 inc=[("stw", 16)])
            P.mark("wstdone")
            P.wait("sync", "stw")
            for il in range(8):
                P.sync.dma_start(out=wstT_b[il][:], in_=wst_dram[:, 128 * il:128 * (il + 1)],
                                 transpose=True, inc=[("lw", 16)])
            P.wait("tensor", "lw")
            for jt in range(8):
                psm = ps2 if jt % 2 == 0 else ps3
                if jt >= 2:
                    P.wmark("tensor", f"m2ev{jt - 2}", "dve")
                for hf in range(2):
                    for il in range(8):
                        P.tensor.matmul(psm[:, 512 * hf:512 * (hf + 1)],
                                        wstT_b[il][:, 128 * jt:128 * (jt + 1)],
                                        WrT_b[il][:, 512 * hf:512 * (hf + 1)],
                                        start=(il == 0), stop=(il == 7),
                                        inc=[("pe", 1)] if il == 7 else ())
                P.wait("vector", "pe")
                P.vector.tensor_copy(m2t[jt][:], psm[:], inc=[("dve", 1)])
                P.mark(f"m2ev{jt}")
            P.mark("m2done")

            P.wait("sync", "str")
            for dl in range(8):
                P.sync.dma_start(out=readsf[dl][:],
                                 in_=reads_dram[0:1024, 128 * dl:128 * (dl + 1)],
                                 transpose=True, inc=[("lr", 16)])
            P.mark("rtr0")

            zfgb = [[a2(32768 + 1024 * b + 128 * dl, 128) for dl in range(8)]
                    for b in range(2)]
            fbu = [f1, a4(2560, 1024)]
            sqb = a4(3584, 1024)
            P.wmark("vector", "m2done", "dve")
            for dl in range(8):
                P.vector.tensor_tensor(zfgb[0][dl][:], z_f[dl][:, 16:144],
                                       gpatw[:, 0:128], ALU.mult, inc=[("dve", 1)])
            P.mark("zfg0")
            for i in range(NT):
                r0 = 128 * i
                hf_sc = i // 8
                if i == 8:
                    P.wmark("sync", "eg7", "pe")
                    for dl in range(8):
                        P.sync.dma_start(out=readsf[dl][:],
                                         in_=reads_dram[1024:2048, 128 * dl:128 * (dl + 1)],
                                         transpose=True, inc=[("lr", 16)])
                    P.mark("rtr1")
                col = r0 - 1024 * hf_sc
                P.wmark("sync", f"eg{i - 2}", "dve") if i >= 2 else None
                P.sync.dma_start(out=xs[i % 2][:], in_=x_win[1 + r0:1 + r0 + 128, :],
                                 inc=[("lx", 16)])
                if i >= 2:
                    P.wmark("gpsimd", f"eg{i - 2}", "dve")
                P.gpsimd.dma_start(out=zt_eg[i % 2][:], in_=z_t_dram[1 + r0:1 + r0 + 128, :],
                                   inc=[("gz" if i % 2 == 0 else "gz1", 16)])
                psr = ps0 if i % 2 == 0 else ps1
                P.wmark("tensor", "rtr0" if hf_sc == 0 else "rtr1", "lr")
                P.wmark("tensor", f"zfg{i}", "dve")
                if i >= 2:
                    P.wmark("tensor", f"y1v{i - 2}", "dve")
                if i == 0:
                    P.wmark("tensor", "m2done", "dve")
                for hf in range(2):
                    for kt in range(8):
                        P.tensor.matmul(psr[:, 512 * hf:512 * (hf + 1)],
                                        readsf[kt][:, col:col + 128],
                                        WrT_b[kt][:, 512 * hf:512 * (hf + 1)],
                                        start=(kt == 0), stop=False)
                    for jt in range(8):
                        P.tensor.matmul(psr[:, 512 * hf:512 * (hf + 1)],
                                        zfgb[i % 2][jt][:],
                                        m2t[jt][:, 512 * hf:512 * (hf + 1)],
                                        start=False, stop=(jt == 7),
                                        inc=[("pe", 1)] if jt == 7 else ())
                P.mark(f"egmm{i}")
                if i + 1 < NT:
                    r1n = 128 * (i + 1)
                    for dl in range(8):
                        if i >= 1:
                            P.wmark("vector", f"egmm{i - 1}", "pe")
                        P.vector.tensor_tensor(zfgb[(i + 1) % 2][dl][:],
                                               z_f[dl][:, 16 + r1n:16 + r1n + 128],
                                               gpatw[:, r1n:r1n + 128], ALU.mult,
                                               inc=[("dve", 1)])
                    P.mark(f"zfg{i + 1}")
                fb = fbu[i % 2]
                P.wait("vector", "pe")
                P.wait("vector", "lx")
                P.wait("vector", "ste")
                P.wait("vector", "dbg")
                P.wait("vector", "gz" if i % 2 == 0 else "gz1")
                P.vector.scalar_tensor_tensor(fb[:], psr[:], ALPHA, xs[i % 2][:],
                                              ALU.mult, ALU.add, inc=[("dve", 1)])
                dve_edge()
                P.vector.tensor_tensor(fb[:], fb[:], zt_eg[i % 2][:], ALU.add,
                                       inc=[("dve", 1)])
                P.mark(f"y1v{i}")
                P.wait("sync", "dve")
                P.sync.dma_start(out=y1_dram[r0:r0 + 128, :], in_=fb[:], inc=[("ste", 16)])
                P.sync.dma_start(out=dbg_y1[r0:r0 + 128, :], in_=fb[:], inc=[("dbg", 16)])
                dve_edge()
                P.vector.tensor_reduce(stat[:, 0:1], fb[:], AX.X, ALU.add, inc=[("dve", 1)])
                P.wait("scalar", "dve")
                P.wait("scalar", "act")
                P.scalar.activation(sqb[:], fb[:], AF.Square, inc=[("act", 1)])
                P.wait("vector", "act")
                P.vector.tensor_reduce(stat[:, 1:2], sqb[:], AX.X, ALU.add, inc=[("dve", 1)])
                dve_edge()
                P.vector.tensor_scalar(stat[:, 2:3], stat[:, 0:1], 1.0 / D, None, ALU.mult,
                                       inc=[("dve", 1)])
                dve_edge()
                P.vector.tensor_scalar(stat[:, 3:4], stat[:, 1:2], 1.0 / D, None, ALU.mult,
                                       inc=[("dve", 1)])
                dve_edge()
                P.vector.tensor_tensor(stat[:, 4:5], stat[:, 2:3], stat[:, 2:3], ALU.mult,
                                       inc=[("dve", 1)])
                dve_edge()
                P.vector.tensor_tensor(stat[:, 5:6], stat[:, 3:4], stat[:, 4:5],
                                       ALU.subtract, inc=[("dve", 1)])
                P.wait("scalar", "dve")
                P.scalar.activation(stat[:, 6:7], stat[:, 5:6], AF.Ln, bias=scal[:, 8:9],
                                    inc=[("act", 1)])
                P.wait("scalar", "act")
                P.scalar.activation(stat[:, 7:8], stat[:, 6:7], AF.Exp, scale=-0.5,
                                    inc=[("act", 1)])
                P.wait("vector", "act")
                P.vector.tensor_scalar(gen_tok[:], fb[:], stat[:, 2:3], stat[:, 7:8],
                                       ALU.subtract, op1=ALU.mult, inc=[("dve", 1)])
                dve_edge()
                P.vector.tensor_scalar(sqb[:], fb[:], stat[:, 2:3], stat[:, 7:8],
                                       ALU.subtract, op1=ALU.mult, inc=[("dve", 1)])
                dve_edge()
                P.vector.tensor_tensor(rd_t[0][:], sqb[:], gen_tok[:], ALU.subtract,
                                       inc=[("dve", 1)])
                P.wait("sync", "dve")
                P.sync.dma_start(out=z2aug_bi[r0:r0 + 128, 0:D], in_=gen_tok[:],
                                 inc=[("ste", 16)])
                P.sync.dma_start(out=z2lo_dram[r0:r0 + 128, :], in_=rd_t[0][:],
                                 inc=[("ste", 16)])
                P.mark(f"eg{i}")
            if PHASE == 1:
                P.gpsimd.dma_start(out=dbg_z[:], in_=z_t_dram[:], inc=[("dbg2", 16)])
                P.gpsimd.dma_start(out=dbg_reads[:], in_=reads_dram[:], inc=[("dbg2", 16)])
                P.gpsimd.dma_start(out=dbg_wst[:], in_=wst_dram[:], inc=[("dbg2", 16)])
                P.gpsimd.dma_start(out=dbg_aseg[:], in_=aseg_bi[:], inc=[("dbg2", 16)])
                P.wait("sync", "ste")
                P.wait("sync", "str")
                P.wait("sync", "stz")
                P.wait("sync", "stw")
                P.wait("gpsimd", "gp")
                P.wait("gpsimd", "ld")
                P.wait("gpsimd", "lt")
                P.wait("gpsimd", "lx")
                P.wait("gpsimd", "lr")
                P.wait("gpsimd", "lw")
                P.wait("gpsimd", "gs")
                P.wait("gpsimd", "gz")
                P.wait("gpsimd", "gz1")
                P.wait("gpsimd", "dbg")
                P.wait("gpsimd", "dbg2")
                P.wait("scalar", "act")
                P.wait("vector", "dve")
                P.wait("tensor", "pe")
                return


            P.wait("sync", "ste")
            P.wmark("sync", "eg15", "dve", "pe")
            for hf in range(2):
                for dl in range(8):
                    P.sync.dma_start(out=z_f[dl][:, 1024 * hf:1024 * (hf + 1)],
                                     in_=z2aug_bi[1024 * hf:1024 * (hf + 1),
                                                  128 * dl:128 * (dl + 1)],
                                     transpose=True, inc=[("lt", 16)])
                    P.sync.dma_start(out=z_fs[dl][:, 1024 * hf:1024 * (hf + 1)],
                                     in_=z2lo_dram[1024 * hf:1024 * (hf + 1),
                                                   128 * dl:128 * (dl + 1)],
                                     transpose=True, inc=[("lt", 16)])
            P.wait("tensor", "lt")
            for hf in range(2):
                if hf == 1:
                    P.wmark("tensor", "lgev0", "act")
                combos = [(w8hi, z_f), (w8hi, z_fs), (w8lo, z_f)]
                for ci, (wmat, zmat) in enumerate(combos):
                    for kt in range(8):
                        for nh in range(2):
                            P.tensor.matmul(ps2[0:8, 512 * nh:512 * (nh + 1)],
                                            wmat[:, 8 * kt:8 * (kt + 1)],
                                            zmat[kt][:, 1024 * hf + 512 * nh:
                                                      1024 * hf + 512 * (nh + 1)],
                                            start=(ci == 0 and kt == 0),
                                            stop=(ci == 2 and kt == 7),
                                            inc=[("pe", 1)] if (ci == 2 and kt == 7
                                                                and nh == 1) else ())
                P.wait("scalar", "pe")
                P.scalar.copy(logits_sb[0:8, 1024 * hf:1024 * (hf + 1)], ps2[0:8, 0:1024],
                              inc=[("act", 1)])
                P.mark(f"lgev{hf}")
            P.wait("sync", "act")
            P.sync.dma_start(out=logit_bi[:], in_=logits_sb[:], inc=[("ste", 16)])
            P.wait("tensor", "act")
            for i in range(NT):
                psl = ps0 if i % 2 == 0 else ps1
                if i >= 2:
                    P.wmark("tensor", f"ltok{i - 2}", "dve")
                P.tensor.matmul(psl[:, 0:8], logits_sb[0:8, 128 * i:128 * (i + 1)],
                                ident8[:], start=True, stop=True, inc=[("pe", 1)])
                P.wait("vector", "pe")
                P.wait("vector", "ste")
                P.vector.tensor_copy(logit_tok[:, 8 * i:8 * (i + 1)], psl[:, 0:8],
                                     inc=[("dve", 1)])
                dve_edge()
                P.vector.tensor_copy(gen_tok[:, 0:8], psl[:, 0:8], inc=[("dve", 1)])
                P.wait("sync", "dve")
                P.sync.dma_start(out=z2aug_bi[128 * i:128 * (i + 1), 1024:1032],
                                 in_=gen_tok[:, 0:8], inc=[("ste", 16)])
                P.mark(f"ltok{i}")
            P.wait("gpsimd", "ste")
            P.gpsimd.collective_compute("AllGather", ALU.bypass, replica_groups=RG,
                                        ins=[logit_bi.ap().opt()], outs=[logit_bo.ap().opt()],
                                        inc=[("cc", 1)])
            P.gpsimd.collective_compute("AllGather", ALU.bypass, replica_groups=RG,
                                        ins=[z2aug_bi.ap().opt()], outs=[z2aug_bo.ap().opt()],
                                        inc=[("cc", 1)])

            P.wait("sync", "cc", 2)
            for inst in range(8):
                for w in range(4):
                    P.sync.dma_start(
                        out=bis[16 * inst + 4 * w:16 * inst + 4 * w + 4, :],
                        in_=logit_bo[8 * w + inst:8 * w + inst + 1, :].rearrange(
                            "1 (a f) -> a f", a=4),
                        inc=[("ld", 16)])
            P.wait("tensor", "ld")
            P.tensor.matmul(ps2[:, 384:512], sel16t[:], sel16t[:], start=True, stop=True,
                            inc=[("pe", 1)])
            P.wait("vector", "pe")
            P.wait("vector", "dbg")
            P.wait("vector", "ste")
            P.vector.tensor_copy(blk16[:], ps2[:, 384:512], inc=[("dve", 1)])
            P.wait("vector", "ld")
            P.vector.memset(bw[:, 0:1], -40.0, inc=[("dve", 1)])
            P.vector.memset(bw[:, 1:2], 40.0, inc=[("dve", 1)])
            P.vector.memset(bw[:, 2:3], 0.0, inc=[("dve", 1)])
            LO, HI, TAU = bw[:, 0:1], bw[:, 1:2], bw[:, 2:3]
            for it in range(BIS_IT):
                dve_edge()
                P.vector.tensor_scalar(f1[:, 0:512], bis[:], TAU, None, ALU.is_ge,
                                       inc=[("dve", 1)])
                dve_edge()
                P.vector.tensor_reduce(bw[:, 3:4], f1[:, 0:512], AX.X, ALU.add,
                                       inc=[("dve", 1)])
                P.wait("tensor", "dve")
                P.tensor.matmul(ps2[:, 2:3], blk16[:], bw[:, 3:4], start=True, stop=True,
                                inc=[("pe", 1)])
                P.wait("vector", "pe")
                P.vector.tensor_copy(bw[:, 5:6], ps2[:, 2:3], inc=[("dve", 1)])
                dve_edge()
                P.vector.tensor_tensor(bw[:, 6:7], bw[:, 5:6], tgt[:], ALU.is_ge,
                                       inc=[("dve", 1)])
                dve_edge()
                P.vector.tensor_tensor(bw[:, 7:8], TAU, LO, ALU.subtract, inc=[("dve", 1)])
                dve_edge()
                P.vector.scalar_tensor_tensor(LO, bw[:, 7:8], bw[:, 6:7], LO,
                                              ALU.mult, ALU.add, inc=[("dve", 1)])
                dve_edge()
                P.vector.tensor_tensor(bw[:, 9:10], HI, TAU, ALU.subtract, inc=[("dve", 1)])
                dve_edge()
                P.vector.scalar_tensor_tensor(HI, bw[:, 9:10], bw[:, 6:7], TAU,
                                              ALU.mult, ALU.add, inc=[("dve", 1)])
                dve_edge()
                P.vector.tensor_scalar(TAU, LO, HI, 0.5, ALU.add, op1=ALU.mult,
                                       inc=[("dve", 1)])
            dve_edge()
            P.vector.tensor_scalar(f1[:, 0:512], bis[:], LO, None, ALU.is_ge,
                                   inc=[("dve", 1)])
            dve_edge()
            P.vector.tensor_scalar(f1[:, 512:1024], bis[:], HI, None, ALU.is_ge,
                                   inc=[("dve", 1)])
            dve_edge()
            P.vector.tensor_tensor(f1[:, 0:512], f1[:, 0:512], f1[:, 512:1024],
                                   ALU.subtract, inc=[("dve", 1)])
            dve_edge()
            P.vector.tensor_scalar(f1[:, 512:1024], f1[:, 0:512], 1e30, -1e30, ALU.mult,
                                   op1=ALU.add, inc=[("dve", 1)])
            dve_edge()
            P.vector.tensor_tensor(f1[:, 0:512], f1[:, 0:512], bis[:], ALU.mult,
                                   inc=[("dve", 1)])
            dve_edge()
            P.vector.tensor_tensor(f1[:, 0:512], f1[:, 0:512], f1[:, 512:1024],
                                   ALU.add, inc=[("dve", 1)])
            dve_edge()
            P.vector.tensor_reduce(bw[:, 4:5], f1[:, 0:512], AX.X, ALU.max,
                                   inc=[("dve", 1)])
            dve_edge()
            P.vector.tensor_copy(f1[:, 640:768], iblk128[:, 0:128], inc=[("dve", 1)])
            dve_edge()
            P.vector.tensor_scalar(f1[:, 512:640], f1[:, 640:768], iota_c[:], None,
                                   ALU.is_equal, inc=[("dve", 1)])
            P.wait("tensor", "dve")
            P.tensor.matmul(ps2[0:1, 128:256], bw[:, 4:5], f1[:, 512:640], start=True,
                            stop=True, inc=[("pe", 1)])
            P.wait("vector", "pe")
            for si in range(8):
                dve_edge()
                P.vector.tensor_reduce(snapw[0:1, si:si + 1],
                                       ps2[0:1, 128 + 16 * si:144 + 16 * si], AX.X,
                                       ALU.max, inc=[("dve", 1)])
            P.wait("tensor", "dve")
            P.tensor.matmul(ps2[0:8, 256:257], snapw[0:1, 0:8], onescf[0:1, :],
                            start=True, stop=True, inc=[("pe", 1)])
            P.wait("vector", "pe")
            P.vector.tensor_copy(bw[0:8, 12:13], ps2[0:8, 256:257], inc=[("dve", 1)])
            P.wait("tensor", "dve")
            P.tensor.matmul(ps2[:, 2:3], sel16t[:], bw[0:8, 12:13], start=True, stop=True,
                            inc=[("pe", 1)])
            P.wait("vector", "pe")
            P.vector.tensor_copy(TAU, ps2[:, 2:3], inc=[("dve", 1)])
            dve_edge()
            P.vector.tensor_scalar(f1[:, 0:512], bis[:], TAU, None, ALU.is_ge,
                                   inc=[("dve", 1)])
            dve_edge()
            P.vector.tensor_reduce(bw[:, 3:4], f1[:, 0:512], AX.X, ALU.add,
                                   inc=[("dve", 1)])
            P.wait("tensor", "dve")
            P.tensor.matmul(ps2[0:32, 4:5], selw32[:], bw[:, 3:4], start=True, stop=True,
                            inc=[("pe", 1)])
            P.wait("vector", "pe")
            P.vector.tensor_copy(cnt32[:], ps2[0:32, 4:5], inc=[("dve", 1)])
            P.wait("tensor", "dve")
            P.tensor.matmul(ps2[0:32, 5:6], prefixw[:], cnt32[:], start=True, stop=True,
                            inc=[("pe", 1)])
            P.wait("vector", "pe")
            P.vector.tensor_copy(bw[0:32, 10:11], ps2[0:32, 5:6], inc=[("dve", 1)])
            dve_edge()
            P.vector.tensor_tensor(bw[0:32, 11:12], nq32[:], bw[0:32, 10:11],
                                   ALU.subtract, inc=[("dve", 1)])
            dve_edge()
            P.vector.tensor_scalar(bw[0:32, 11:12], bw[0:32, 11:12], 0.0, None, ALU.max,
                                   inc=[("dve", 1)])
            dve_edge()
            P.vector.tensor_tensor(cnt32[:], cnt32[:], bw[0:32, 11:12], ALU.min,
                                   inc=[("dve", 1)])
            P.wait("tensor", "dve")
            P.tensor.matmul(ps2[0:1, 8:16], bw[0:8, 12:13], ident8[:], start=True,
                            stop=True, inc=[("pe", 1)])
            P.wait("scalar", "pe")
            P.scalar.copy(rowb[0:1, 0:8], ps2[0:1, 8:16], inc=[("act", 1)])
            P.wait("tensor", "act")
            P.tensor.matmul(ps2[:, 8:16], onesr[:], rowb[0:1, 0:8], start=True, stop=True,
                            inc=[("pe", 1)])
            P.wait("vector", "pe")
            P.vector.tensor_copy(taub[:], ps2[:, 8:16], inc=[("dve", 1)])
            P.wait("tensor", "dve")
            P.tensor.matmul(ps2[0:1, 16:24], cnt32[:], prevw[:], start=True, stop=True,
                            inc=[("pe", 1)])
            P.wait("scalar", "pe")
            P.scalar.copy(offrow[:], ps2[0:1, 16:24], inc=[("act", 1)])
            P.wait("tensor", "act")
            P.tensor.matmul(ps2[:, 16:24], onesr[:], offrow[:], start=True, stop=True,
                            inc=[("pe", 1)])
            P.wait("vector", "pe")
            P.vector.tensor_copy(offib[:], ps2[:, 16:24], inc=[("dve", 1)])
            P.wait("tensor", "dve")
            P.tensor.matmul(ps2[0:20, 24:25], cntsel[:], cnt32[:], start=True, stop=True,
                            inc=[("pe", 1)])
            P.wait("scalar", "pe")
            P.scalar.copy(stat[0:20, 12:13], ps2[0:20, 24:25], inc=[("act", 1)])
            P.wait("tensor", "act")
            P.tensor.matmul(ps2[0:1, 32:52], stat[0:20, 12:13], ident32[0:20, 0:20],
                            start=True, stop=True, inc=[("pe", 1)])
            P.wait("scalar", "pe")
            P.scalar.copy(rowb[0:1, 8:28], ps2[0:1, 32:52], inc=[("act", 1)])
            P.wait("tensor", "act")
            P.tensor.matmul(ps2[:, 32:52], onesr[:], rowb[0:1, 8:28], start=True, stop=True,
                            inc=[("pe", 1)])
            P.wait("vector", "pe")
            P.vector.tensor_copy(cnt20b[:], ps2[:, 32:52], inc=[("dve", 1)])
            dve_edge()
            P.vector.tensor_scalar(stat[:, 13:14], iota_c[:], kbase20[:, 0:1], None,
                                   ALU.add, inc=[("dve", 1)]) if False else None
            P.vector.tensor_tensor(valid20[:], kbase20[:], cnt20b[:], ALU.is_lt,
                                   inc=[("dve", 1)]) if False else None
            P.vector.tensor_scalar(valid20[:], kbase20[:], iota_c[:], None, ALU.add,
                                   inc=[("dve", 1)])
            dve_edge()
            P.vector.tensor_tensor(valid20[:], valid20[:], cnt20b[:], ALU.is_lt,
                                   inc=[("dve", 1)])
            P.mark("bisdone")

            P.wmark("vector", "bisdone", "dve")
            for i in range(NT):
                dve_edge()
                P.vector.tensor_tensor(pose[:, 0:8], logit_tok[:, 8 * i:8 * (i + 1)],
                                       taub[:], ALU.is_ge, inc=[("dve", 1)])
                psl = ps0 if i % 2 == 0 else ps1
                P.wait("tensor", "dve")
                P.wmark("tensor", "bisdone", "act")
                P.tensor.matmul(psl[:, 0:8], tris[:], pose[:, 0:8], start=True, stop=False)
                P.tensor.matmul(psl[:, 0:8], onesr[:], offrow[:], start=False, stop=True,
                                inc=[("pe", 1)])
                P.tensor.matmul(psl[0:1, 8:16], onescf[:], pose[:, 0:8], start=True,
                                stop=True, inc=[("pe", 1)])
                P.wait("vector", "pe")
                P.vector.tensor_copy(f1[:, 0:8], psl[:, 0:8], inc=[("dve", 1)])
                P.wait("scalar", "pe")
                P.scalar.copy(rowb[0:1, 32:40], psl[0:1, 8:16], inc=[("act", 1)])
                P.wait("vector", "act")
                dve_edge()
                P.vector.tensor_tensor(offrow[:], offrow[:], rowb[0:1, 32:40], ALU.add,
                                       inc=[("dve", 1)])
                dve_edge()
                P.vector.tensor_tensor(f1[:, 8:16], f1[:, 0:8], nqcap[:], ALU.is_lt,
                                       inc=[("dve", 1)])
                dve_edge()
                P.vector.tensor_tensor(selr[:, 16 * i:16 * i + 8], pose[:, 0:8],
                                       f1[:, 8:16], ALU.mult, inc=[("dve", 1)])
                dve_edge()
                P.vector.tensor_tensor(selr[:, 16 * i + 8:16 * i + 16], f1[:, 0:8],
                                       offib[:], ALU.subtract, inc=[("dve", 1)])
                P.mark(f"sel{i}")

            for i in range(NT):
                if i >= 1:
                    P.wmark("vector", f"pos{i - 1}", "pe")
                dve_edge()
                P.vector.tensor_scalar(pose[:, 24:32], selr[:, 16 * i + 8:16 * i + 16],
                                       128.0, None, ALU.is_ge, inc=[("dve", 1)])
                dve_edge()
                P.vector.tensor_scalar(pose[:, 16:24], selr[:, 16 * i + 8:16 * i + 16],
                                       256.0, None, ALU.is_ge, inc=[("dve", 1)])
                dve_edge()
                P.vector.tensor_tensor(pose[:, 24:32], pose[:, 24:32], pose[:, 16:24],
                                       ALU.add, inc=[("dve", 1)])
                dve_edge()
                P.vector.tensor_scalar(pose[:, 16:24], selr[:, 16 * i + 8:16 * i + 16],
                                       384.0, None, ALU.is_ge, inc=[("dve", 1)])
                dve_edge()
                P.vector.tensor_tensor(pose[:, 24:32], pose[:, 24:32], pose[:, 16:24],
                                       ALU.add, inc=[("dve", 1)])
                dve_edge()
                P.vector.scalar_tensor_tensor(pose[:, 8:16], pose[:, 24:32], -128.0,
                                              selr[:, 16 * i + 8:16 * i + 16],
                                              ALU.mult, ALU.add, inc=[("dve", 1)])
                dve_edge()
                P.vector.tensor_tensor(
                    ehi[:], iblk4[:],
                    pose.ap()[:, 24:32, None].broadcast_to([128, 8, 4]),
                    ALU.is_equal, inc=[("dve", 1)])
                dve_edge()
                P.vector.tensor_tensor(
                    ehi[:], ehi[:],
                    selr.ap()[:, 16 * i:16 * i + 8, None].broadcast_to([128, 8, 4]),
                    ALU.mult, inc=[("dve", 1)])
                dve_edge()
                P.vector.tensor_scalar(pose[:, 32:33], iota_c[:], float(128 * i), WOFFB,
                                       ALU.add, op1=ALU.add, inc=[("dve", 1)])
                dve_edge()
                P.vector.tensor_scalar(ehi[:], ehi[:], pose[:, 32:33], None, ALU.mult,
                                       inc=[("dve", 1)])
                dve_edge()
                P.vector.tensor_tensor(
                    elo[:], iblk128[:],
                    pose.ap()[:, 8:16, None].broadcast_to([128, 8, 128]),
                    ALU.is_equal, inc=[("dve", 1)])
                P.wait("tensor", "dve")
                P.tensor.matmul(ps2[0:32, 0:512], ehi[:], elo[:, 0:512],
                                start=(i == 0), stop=(i == NT - 1),
                                inc=[("pe", 1)] if i == NT - 1 else ())
                P.tensor.matmul(ps2[0:32, 512:1024], ehi[:], elo[:, 512:1024],
                                start=(i == 0), stop=(i == NT - 1),
                                inc=[("pe", 2)] if i == NT - 1 else [("pe", 1)])
                P.mark(f"pos{i}")
            P.wait("vector", "pe")
            P.vector.tensor_copy(possb[:], ps2[0:32, 0:1024], inc=[("dve", 1)])
            P.wait("sync", "dve")
            for inst in range(8):
                P.sync.dma_start(
                    out=pos_bi[inst:inst + 1, :].rearrange("1 (h l) -> h l", h=4),
                    in_=possb[4 * inst:4 * inst + 4, 128 * inst:128 * (inst + 1)],
                    inc=[("stw", 16)])
            P.wait("gpsimd", "stw")
            P.gpsimd.collective_compute("AllGather", ALU.bypass, replica_groups=RG,
                                        ins=[pos_bi.ap().opt()], outs=[pos_bo.ap().opt()],
                                        inc=[("cc", 1)])
            P.mark("posdone")

            P.wait("sync", "cc", 4)
            P.sync.dma_start(out=possb[0:32, 0:512], in_=pos_bo[:], inc=[("ld", 16)])
            P.wait("tensor", "ld")
            for j in range(20):
                k = [0, 1, 2] * 4 + [0, 1] * 4
                psl = ps0 if j % 2 == 0 else ps1
                if j >= 2:
                    P.wmark("tensor", f"idx{j - 2}", "dve")
                P.tensor.matmul(psl[0:1, 0:128], possel[:, j:j + 1],
                                possb[0:32, 128 * k[j]:128 * (k[j] + 1)],
                                start=True, stop=True, inc=[("pe", 1)])
                P.wait("scalar", "pe")
                P.scalar.copy(rowb[0:1, 0:0].ap() if False else stat[0:1, 14:15],
                              psl[0:1, 0:1], inc=[("act", 1)]) if False else None
                P.scalar.copy(logits_sb[0:1, 128 * (j % 8):128 * (j % 8) + 128],
                              psl[0:1, 0:128], inc=[("act", 1)])
                P.wait("tensor", "act")
                P.tensor.matmul(psl[:, 128:129],
                                logits_sb[0:1, 128 * (j % 8):128 * (j % 8) + 128],
                                onescf[0:1, 0:1], start=True, stop=True, inc=[("pe", 1)])
                P.wait("vector", "pe")
                P.vector.tensor_copy(idx_f[:, j:j + 1], psl[:, 128:129], inc=[("dve", 1)])
                P.mark(f"idx{j}")
            dve_edge()
            P.vector.tensor_tensor(f1[:, 0:20], idx_f[:, 0:20], valid20[:], ALU.mult,
                                   inc=[("dve", 1)])
            dve_edge()
            P.vector.tensor_scalar(f1[:, 20:40], valid20[:], 1.0, -BIGIDX, ALU.subtract,
                                   op1=ALU.mult, inc=[("dve", 1)])
            dve_edge()
            P.vector.tensor_tensor(f1[:, 40:60], f1[:, 0:20], f1[:, 20:40], ALU.add,
                                   inc=[("dve", 1)])
            dve_edge()
            P.vector.tensor_copy(idxq_i[:, 0:20], f1[:, 40:60], inc=[("dve", 1)])
            P.mark("idxdone")
            gbuf = [a2(1032 * j, AUGW) for j in range(20)]
            P.wmark("gpsimd", "eg15", "pe")
            P.wmark("gpsimd", "m2done", "pe")
            for j in range(20):
                P.gpsimd.memset(gbuf[j][:], 0.0, inc=[("pool", 1)])
            P.wmark("gpsimd", "idxdone", "dve")
            P.wait("gpsimd", "pool")
            for j in range(20):
                gqs = ["gq", "gqb", "gqc"][j % 3]
                P.gpsimd.indirect_dma_start(
                    out=gbuf[j][:], out_offset=None,
                    in_=z2aug_bo[:],
                    in_offset=IndirectOffsetOnAxis(idxq_i[:, j:j + 1], 0),
                    bounds_check=4 * WIN - 1, oob_is_err=False, inc=[(gqs, 16)])
            for j in range(20):
                gqs = ["gq", "gqb", "gqc"][j % 3]
                buf = gbuf[j]
                P.wait("vector", gqs)
                dve_edge()
                if j >= 3:
                    P.wait("vector", "ste")
                gsel = gselq if j < 12 else gselk
                P.vector.tensor_tensor(f1[:, 60 + 8 * (j % 4):68 + 8 * (j % 4)],
                                       buf[:, 1024:1032], gsel[:], ALU.mult,
                                       inc=[("dve", 1)])
                dve_edge()
                if j < 12:
                    P.vector.tensor_reduce(stat[:, 13 + j % 3:14 + j % 3],
                                           f1[:, 60 + 8 * (j % 4):68 + 8 * (j % 4)],
                                           AX.X, ALU.add, inc=[("dve", 1)])
                else:
                    P.vector.tensor_reduce(sig_kv[:, j - 12:j - 11],
                                           f1[:, 60 + 8 * (j % 4):68 + 8 * (j % 4)],
                                           AX.X, ALU.add, inc=[("dve", 1)])
                P.wait("scalar", "dve")
                P.wait("scalar", gqs)
                if j < 12:
                    wq_, kq_ = j // 3, j % 3
                    nrows = 96 if kq_ == 2 else 128
                    r0q = QCAP * wq_ + 128 * kq_
                    P.scalar.dma_start(out=qcomp_dram[r0q:r0q + nrows, :],
                                       in_=buf[0:nrows, 0:1024], inc=[("ste", 16)])
                    P.scalar.dma_start(out=qlog_dram[r0q:r0q + nrows, :],
                                       in_=stat[0:nrows, 13 + j % 3:14 + j % 3],
                                       inc=[("ste", 16)])
                else:
                    wk_, kk_ = (j - 12) // 2, (j - 12) % 2
                    r0k = 256 * wk_ + 128 * kk_
                    P.scalar.dma_start(out=kvcomp_dram[r0k:r0k + 128, :],
                                       in_=buf[:, 0:1024], inc=[("ste", 16)])
                P.mark(f"gst{j}")
            P.wait("sync", "ste")
            for i in range(8):
                P.wait("sync", "dve")
                P.sync.dma_start(out=xs[0][:, 0:DI], in_=wqT[128 * i:128 * (i + 1), :],
                                 inc=[("ld", 16)])
                P.wait("vector", "ld")
                P.wait("vector", "ste")
                P.vector.tensor_copy(wq_b[i][:], xs[0][:, 0:DI], inc=[("dve", 1)])
                P.wait("sync", "dve")
                P.sync.dma_start(out=xs[0][:], in_=wkvT[128 * i:128 * (i + 1), :],
                                 inc=[("ld", 16)])
                P.wait("vector", "ld")
                P.vector.tensor_copy(wkv_b[i][:], xs[0][:], inc=[("dve", 1)])
            for i in range(4):
                P.wait("sync", "dve")
                P.sync.dma_start(out=xs[0][:], in_=woutT[128 * i:128 * (i + 1), :],
                                 inc=[("ld", 16)])
                P.wait("vector", "ld")
                P.vector.tensor_copy(wout_b[i][:], xs[0][:], inc=[("dve", 1)])
            P.wait("sync", "dve")
            P.sync.dma_start(out=xs[0][:, 0:4], in_=nk_in[:], inc=[("ld", 16)])
            P.sync.dma_start(out=xs[1][0:1, 0:8 * 65], in_=nv_in[:], inc=[("ld", 16)])
            P.wait("vector", "ld")
            P.vector.tensor_copy(nkb[:], xs[0][:, 0:4], inc=[("dve", 1)])
            dve_edge()
            P.vector.tensor_copy(nvb[:], xs[1][0:1, 0:8 * 65], inc=[("dve", 1)])
            P.mark("fwts")

            dve_edge()
            P.vector.tensor_scalar(sig_kv[:], sig_kv[:], -1.0, None, ALU.mult,
                                   inc=[("dve", 1)])
            P.wait("scalar", "dve")
            P.scalar.activation(sig_kv[:], sig_kv[:], AF.Exp, inc=[("act", 1)])
            P.wait("vector", "act")
            P.vector.tensor_scalar(sig_kv[:], sig_kv[:], 1.0, None, ALU.add,
                                   inc=[("dve", 1)])
            dve_edge()
            P.vector.reciprocal(sig_kv[:], sig_kv[:], inc=[("dve", 1)])
            dve_edge()
            P.vector.tensor_scalar(kbias[:], valid20[:, 12:20], 1.0, 1e9, ALU.subtract,
                                   op1=ALU.mult, inc=[("dve", 1)])
            dve_edge()
            P.vector.tensor_scalar(kbias[:], kbias[:], SHIFT, None, ALU.add,
                                   inc=[("dve", 1)])
            P.mark("gdone")

            P.wmark("sync", "gdone", "ste")
            for dl in range(8):
                P.sync.dma_start(out=z_f[dl][:, 0:QS],
                                 in_=qcomp_dram[:, 128 * dl:128 * (dl + 1)],
                                 transpose=True, inc=[("lr", 16)])
            P.wait("tensor", "lr")
            P.wmark("tensor", "fwts", "dve")
            for et in range(4):
                for nch in range(3):
                    n0, n1 = 512 * nch, min(512 * (nch + 1), QS)
                    psl = ps0 if (et * 3 + nch) % 2 == 0 else ps1
                    if et * 3 + nch >= 2:
                        P.wmark("tensor", f"qev{et * 3 + nch - 2}", "dve")
                    for kt in range(8):
                        P.tensor.matmul(psl[:, 0:n1 - n0],
                                        wq_b[kt][:, 128 * et:128 * (et + 1)],
                                        z_f[kt][:, n0:n1],
                                        start=(kt == 0), stop=(kt == 7),
                                        inc=[("pe", 1)] if kt == 7 else ())
                    P.wait("vector", "pe")
                    P.vector.tensor_copy(qh[et][:, n0:n1], psl[:, 0:n1 - n0],
                                         inc=[("dve", 1)])
                    P.mark(f"qev{et * 3 + nch}")
            for wv in range(2):
                for t in range(4):
                    P.wait("sync", "dve") if wv == 1 else None
                    P.sync.dma_start(out=kvf[t][:],
                                     in_=kvcomp_dram[:, 128 * (4 * wv + t):
                                                     128 * (4 * wv + t + 1)],
                                     transpose=True, inc=[("lw", 16)])
                P.wait("tensor", "lw")
                if wv == 0:
                    P.wmark("tensor", "qev11", "dve")
                for et in range(4):
                    for nch in range(2):
                        idx2 = et * 2 + nch
                        psl = ps0 if idx2 % 2 == 0 else ps1
                        if wv == 1 or idx2 >= 2:
                            P.wait("tensor", "dve")
                        for kt in range(4):
                            P.tensor.matmul(psl[:, 0:512],
                                            wkv_b[4 * wv + kt][:, 128 * et:128 * (et + 1)],
                                            kvf[kt][:, 512 * nch:512 * (nch + 1)],
                                            start=(kt == 0), stop=(kt == 3),
                                            inc=[("pe", 1)] if kt == 3 else ())
                        P.wait("vector", "pe")
                        if wv == 0:
                            P.vector.tensor_copy(kf[et][:, 512 * nch:512 * (nch + 1)],
                                                 psl[:, 0:512], inc=[("dve", 1)])
                        else:
                            P.vector.tensor_tensor(kf[et][:, 512 * nch:512 * (nch + 1)],
                                                   kf[et][:, 512 * nch:512 * (nch + 1)],
                                                   psl[:, 0:512], ALU.add, inc=[("dve", 1)])
                for st in range(8):
                    psl = ps2 if st % 2 == 0 else ps3
                    if wv == 1 or st >= 2:
                        P.wait("tensor", "dve")
                    for kt in range(4):
                        P.tensor.matmul(
                            psl[:, 0:512],
                            kvf[kt][:, 128 * st:128 * (st + 1)],
                            wkv_b[4 * wv + kt][:, DI:2 * DI],
                            start=(kt == 0), stop=(kt == 3),
                            inc=[("pe", 1)] if kt == 3 else ())
                    P.wait("vector", "pe")
                    vview = vtok[st].rearrange("p (h e) -> p h e", h=8)
                    pview = psl.ap()[:, 0:512].rearrange("p (h e) -> p h e", h=8)
                    if wv == 0:
                        P.vector.tensor_copy(vview[:, :, 0:64], pview, inc=[("dve", 1)])
                    else:
                        P.vector.tensor_tensor(vview[:, :, 0:64], vview[:, :, 0:64], pview,
                                               ALU.add, inc=[("dve", 1)])
                        dve_edge()
                        P.vector.tensor_scalar(vview[:, :, 0:64], vview[:, :, 0:64],
                                               sig_kv[:, st:st + 1], None, ALU.mult,
                                               inc=[("dve", 1)])
                        dve_edge()
                        P.vector.memset(vview[:, :, 64:65], 1.0, inc=[("dve", 1)])
            P.wait("vector", "act")
            for et in range(4):
                dve_edge()
                P.vector.tensor_copy(kf[et][:, KS:KS + 1], nkb[:, et:et + 1],
                                     inc=[("dve", 1)])
            P.mark("projs")

            for h in range(8):
                et, oo = h // 2, 64 * (h % 2)
                for jt in range(8):
                    psl = [ps0.ap()[:, 0:512], ps0.ap()[:, 512:1024],
                           ps1.ap()[:, 0:512]][0] if False else None
                    for nch in range(3):
                        n0 = 512 * nch
                        n1 = min(512 * (nch + 1), QS)
                        pss = [ps0.ap()[:, 0:512], ps0.ap()[:, 512:1024],
                               ps1.ap()[:, 0:512]][nch]
                        if h + jt > 0:
                            P.wmark("tensor", f"exp{h}_{jt - 1}_{nch}" if jt > 0
                                    else f"exp{h - 1}_7_{nch}", "act")
                        P.wmark("tensor", "projs", "dve") if (h == 0 and jt == 0
                                                              and nch == 0) else None
                        P.tensor.matmul(pss[:, 0:n1 - n0],
                                        kf[et][64 * (h % 2):64 * (h % 2) + 64,
                                               128 * jt:128 * (jt + 1)],
                                        qh[et][64 * (h % 2):64 * (h % 2) + 64, n0:n1],
                                        start=True, stop=True, inc=[("pe", 1)])
                        P.wait("scalar", "pe")
                        P.scalar.activation(expT[jt][:, n0:n1], pss[:, 0:n1 - n0],
                                            AF.Exp, scale=0.125,
                                            bias=kbias[:, jt:jt + 1], inc=[("act", 1)])
                        P.mark(f"exp{h}_{jt}_{nch}")
                for nch in range(3):
                    n0 = 512 * nch
                    n1 = min(512 * (nch + 1), QS)
                    pss = [ps0.ap()[:, 0:512], ps0.ap()[:, 512:1024],
                           ps1.ap()[:, 0:512]][nch]
                    P.wmark("tensor", f"exp{h}_7_{nch}", "act")
                    P.tensor.matmul(pss[0:1, 0:n1 - n0],
                                    kf[et][64 * (h % 2):64 * (h % 2) + 64, KS:KS + 1],
                                    qh[et][64 * (h % 2):64 * (h % 2) + 64, n0:n1],
                                    start=True, stop=True, inc=[("pe", 1)])
                    P.wait("scalar", "pe")
                    P.scalar.activation(expN[0:1, n0:n1], pss[0:1, 0:n1 - n0], AF.Exp,
                                        scale=0.125, bias=scal[0:1, 10:11],
                                        inc=[("act", 1)])
                    P.mark(f"expN{h}_{nch}")
                P.wait("tensor", "act")
                P.wait("tensor", "dve")
                for jt in range(8):
                    for nch in range(3):
                        n0 = 512 * nch
                        n1 = min(512 * (nch + 1), QS)
                        psO = [ps2.ap()[0:65, 0:512], ps2.ap()[0:65, 512:1024],
                               ps3.ap()[0:65, 0:512]][nch]
                        P.tensor.matmul(psO[:, 0:n1 - n0],
                                        vtok[jt][:, 65 * h:65 * (h + 1)],
                                        expT[jt][:, n0:n1],
                                        start=(jt == 0), stop=False)
                for nch in range(3):
                    n0 = 512 * nch
                    n1 = min(512 * (nch + 1), QS)
                    psO = [ps2.ap()[0:65, 0:512], ps2.ap()[0:65, 512:1024],
                           ps3.ap()[0:65, 0:512]][nch]
                    P.tensor.matmul(psO[:, 0:n1 - n0], nvb[0:1, 65 * h:65 * (h + 1)],
                                    expN[0:1, n0:n1], start=False, stop=True,
                                    inc=[("pe", 1)])
                P.wait("scalar", "pe")
                for nch in range(3):
                    n0 = 512 * nch
                    n1 = min(512 * (nch + 1), QS)
                    psO = [ps2.ap()[0:65, 0:512], ps2.ap()[0:65, 512:1024],
                           ps3.ap()[0:65, 0:512]][nch]
                    P.scalar.copy(sumrow[0:1, n0:n1], psO[64:65, 0:n1 - n0],
                                  inc=[("act", 1)])
                P.wait("vector", "act")
                P.vector.reciprocal(sumrow[0:1, 0:QS], sumrow[0:1, 0:QS], inc=[("dve", 1)])
                P.wait("tensor", "dve")
                for nch in range(3):
                    n0 = 512 * nch
                    n1 = min(512 * (nch + 1), QS)
                    P.wait("tensor", "act")
                    P.tensor.matmul(ps1.ap()[0:64, 512:512 + n1 - n0],
                                    onesr[0:1, 0:64], sumrow[0:1, n0:n1],
                                    start=True, stop=True, inc=[("pe", 1)])
                    P.wait("scalar", "pe")
                    P.scalar.copy(recb[:, n0:n1], ps1.ap()[0:64, 512:512 + n1 - n0],
                                  inc=[("act", 1)])
                    P.wait("vector", "act")
                    psO = [ps2.ap()[0:65, 0:512], ps2.ap()[0:65, 512:1024],
                           ps3.ap()[0:65, 0:512]][nch]
                    P.vector.tensor_tensor(oT[et][oo:oo + 64, n0:n1], psO[0:64, 0:n1 - n0],
                                           recb[:, n0:n1], ALU.mult, inc=[("dve", 1)])
                    P.mark(f"oev{h}_{nch}")
            P.mark("attn")

            P.wmark("tensor", "attn", "dve")
            for st in range(11):
                n0 = 128 * st
                psl = ps0 if st % 2 == 0 else ps1
                if st >= 2:
                    P.wmark("tensor", f"oproj{st - 2}", "dve")
                for hf in range(2):
                    for kt in range(4):
                        P.tensor.matmul(psl[:, 512 * hf:512 * (hf + 1)],
                                        oT[kt][:, n0:n0 + 128],
                                        wout_b[kt][:, 512 * hf:512 * (hf + 1)],
                                        start=(kt == 0), stop=(kt == 3),
                                        inc=[("pe", 1)] if kt == 3 else ())
                P.wait("gpsimd", "ste")
                P.gpsimd.dma_start(out=stat[:, 15:16], in_=qlog_dram[n0:n0 + 128, :],
                                   inc=[("gz", 16)]) if False else None
                P.sync.dma_start(out=stat[:, 15:16], in_=qlog_dram[n0:n0 + 128, :],
                                 inc=[("ld", 16)])
                P.wait("scalar", "ld")
                P.scalar.activation(stat[:, 14:15], stat[:, 15:16], AF.Exp, scale=-1.0,
                                    inc=[("act", 1)])
                P.wait("vector", "act")
                P.vector.tensor_scalar(stat[:, 13:14], stat[:, 14:15], 1.0, None, ALU.add,
                                       inc=[("dve", 1)])
                dve_edge()
                P.vector.reciprocal(stat[:, 12:13], stat[:, 13:14], inc=[("dve", 1)])
                P.wait("vector", "pe")
                dve_edge()
                buf = gen_tok if st % 2 == 0 else a2(38848, 1024)
                if st >= 2:
                    P.wait("vector", "so0" if st % 2 == 0 else "so1")
                P.vector.tensor_scalar(buf[:], psl[:], stat[:, 12:13], None, ALU.mult,
                                       inc=[("dve", 1)])
                P.wait("sync", "dve")
                P.sync.dma_start(out=o_bi[n0:n0 + 128, :], in_=buf[:],
                                 inc=[("so0" if st % 2 == 0 else "so1", 16)])
                P.mark(f"ost{st}")
                P.mark(f"oproj{st}")
            P.wait("gpsimd", "so0")
            P.wait("gpsimd", "so1")
            P.gpsimd.collective_compute("AllGather", ALU.bypass, replica_groups=RG,
                                        ins=[o_bi.ap().opt()], outs=[o_bo.ap().opt()],
                                        inc=[("cc", 1)])

            dve_edge()
            P.vector.tensor_scalar(stat[:, 11:12], WOFFB, float(QCAP) / 2048.0, None,
                                   ALU.mult, inc=[("dve", 1)])
            P.wait("tensor", "dve")
            P.tensor.matmul(ps2[0:1, 52:64], cnt32[:], combsel[:], start=True, stop=True,
                            inc=[("pe", 1)])
            P.wait("scalar", "pe")
            P.scalar.copy(rowb[0:1, 40:52], ps2[0:1, 52:64], inc=[("act", 1)])
            P.wait("tensor", "act")
            P.tensor.matmul(ps2[:, 52:64], onesr[:], rowb[0:1, 40:52], start=True,
                            stop=True, inc=[("pe", 1)])
            P.wait("vector", "pe")
            P.vector.tensor_copy(cm_f[:], ps2[:, 52:64], inc=[("dve", 1)])
            dve_edge()
            P.vector.tensor_scalar(f1[:, 0:12], kbase12[:], iota_c[:], None, ALU.add,
                                   inc=[("dve", 1)])
            dve_edge()
            P.vector.tensor_tensor(f1[:, 12:24], f1[:, 0:12], cm_f[:], ALU.is_lt,
                                   inc=[("dve", 1)])
            dve_edge()
            P.vector.tensor_scalar(f1[:, 24:36], kbase12[:], iota_c[:], stat[:, 11:12],
                                   ALU.add, op1=ALU.add, inc=[("dve", 1)])
            for gg in range(4):
                dve_edge()
                P.vector.tensor_scalar(f1[:, 36 + 3 * gg:39 + 3 * gg],
                                       f1[:, 24 + 3 * gg:27 + 3 * gg],
                                       float(QS * gg), None, ALU.add, inc=[("dve", 1)])
            dve_edge()
            P.vector.tensor_tensor(f1[:, 48:60], f1[:, 36:48], f1[:, 12:24], ALU.mult,
                                   inc=[("dve", 1)])
            dve_edge()
            P.vector.tensor_scalar(f1[:, 60:72], f1[:, 12:24], 1.0, -BIGIDX,
                                   ALU.subtract, op1=ALU.mult, inc=[("dve", 1)])
            dve_edge()
            P.vector.tensor_tensor(f1[:, 48:60], f1[:, 48:60], f1[:, 60:72], ALU.add,
                                   inc=[("dve", 1)])
            dve_edge()
            P.vector.tensor_copy(cm_i[:], f1[:, 48:60], inc=[("dve", 1)])
            P.wait("tensor", "dve")
            for j in range(12):
                k3 = j % 3
                psl = ps0 if j % 2 == 0 else ps1
                if j >= 2:
                    P.wmark("tensor", f"cidx{j - 2}", "dve")
                P.tensor.matmul(psl[0:1, 0:128], combsel2[:, j:j + 1],
                                possb[0:32, 128 * k3:128 * (k3 + 1)],
                                start=True, stop=True, inc=[("pe", 1)])
                P.wait("scalar", "pe")
                P.scalar.copy(logits_sb[0:1, 128 * (j % 8):128 * (j % 8) + 128],
                              psl[0:1, 0:128], inc=[("act", 1)])
                P.wait("tensor", "act")
                P.tensor.matmul(psl[:, 128:129],
                                logits_sb[0:1, 128 * (j % 8):128 * (j % 8) + 128],
                                onescf[0:1, 0:1], start=True, stop=True, inc=[("pe", 1)])
                P.wait("vector", "pe")
                P.vector.tensor_scalar(f1[:, 72 + j:73 + j], psl[:, 128:129], WOFFB, None,
                                       ALU.subtract, inc=[("dve", 1)])
                P.mark(f"cidx{j}")
            dve_edge()
            P.vector.tensor_tensor(f1[:, 84:96], f1[:, 72:84], f1[:, 12:24], ALU.mult,
                                   inc=[("dve", 1)])
            dve_edge()
            P.vector.tensor_tensor(f1[:, 84:96], f1[:, 84:96], f1[:, 60:72], ALU.add,
                                   inc=[("dve", 1)])
            dve_edge()
            P.vector.tensor_scalar(f1[:, 96:108], f1[:, 84:96], 0.0, None, ALU.is_lt,
                                   inc=[("dve", 1)])
            dve_edge()
            P.vector.scalar_tensor_tensor(f1[:, 84:96], f1[:, 96:108], 2.0 * BIGIDX,
                                          f1[:, 84:96], ALU.mult, ALU.add,
                                          inc=[("dve", 1)])
            dve_edge()
            P.vector.tensor_copy(cm2_i[:], f1[:, 84:96], inc=[("dve", 1)])
            P.mark("cmidx")
            P.wait("gpsimd", "cc", 5)
            P.wmark("gpsimd", "cmidx", "dve")
            ob = [a2(20544 + 1024 * j, 1024) for j in range(12)]
            for j in range(12):
                P.gpsimd.memset(ob[j][:], 0.0, inc=[("pool", 1)])
            P.wait("gpsimd", "pool")
            for j in range(12):
                gqs = ["gq", "gqb", "gqc"][j % 3]
                P.gpsimd.indirect_dma_start(
                    out=ob[j][:], out_offset=None,
                    in_=o_bo[:],
                    in_offset=IndirectOffsetOnAxis(cm_i[:, j:j + 1], 0),
                    bounds_check=4 * QS - 1, oob_is_err=False, inc=[(gqs, 16)])
            P.wait("sync", "pe")
            P.wait("sync", "dve")
            iwb = a4(2560, 2048)
            P.sync.dma_start(out=iwb[:], in_=IWINBC[:], inc=[("lw", 16)])
            P.wait("vector", "pe")
            P.wait("vector", "lw")
            P.wait("vector", "so0")
            P.wait("vector", "so1")
            Ebuf = [a2(2048 * j, 2048) for j in range(10)] + [a2(38848, 2048),
                                                             a2(40896, 2048)]
            for j in range(12):
                dve_edge()
                P.vector.tensor_scalar(Ebuf[j][:], iwb[:], f1[:, 84 + j:85 + j], None,
                                       ALU.is_equal, inc=[("dve", 1)])
            P.mark("ebld")
            P.wait("sync", "dve")
            P.wait("sync", "act")
            P.sync.dma_start(out=nullb[:], in_=null_in[:, 0:1024], inc=[("ld", 16)])
            P.wmark("tensor", "ebld", "dve")
            P.wait("tensor", "act")
            for sm in ["gq", "gqb", "gqc"]:
                P.wait("tensor", sm)
            fbuf = [f1, ocomb[0]]
            for i in range(NT):
                r0 = 128 * i
                psl = ps0 if i % 2 == 0 else ps1
                fb = fbuf[i % 2]
                if i < 2:
                    P.wait("sync", "dve")
                else:
                    P.wmark("sync", f"cdve{i - 2}", "dve")
                P.sync.dma_start(out=xs[i % 2][:], in_=y1_dram[r0:r0 + 128, :],
                                 inc=[("lx", 16)])
                if i >= 2:
                    P.wmark("tensor", f"cdve{i - 2}", "dve")
                for j in range(12):
                    for h in range(2):
                        P.tensor.matmul(psl[:, 512 * h:512 * (h + 1)],
                                        Ebuf[j][:, r0:r0 + 128],
                                        ob[j][:, 512 * h:512 * (h + 1)],
                                        start=(j == 0), stop=(j == 11),
                                        inc=[("pe", 1)] if (j == 11 and h == 1) else ())
                P.mark(f"cmm{i}")
                P.wmark("vector", f"cmm{i}", "pe")
                P.wait("vector", "lx")
                P.wait("vector", "ld") if i == 0 else None
                if i >= 2:
                    P.wmark("vector", f"cmb{i - 2}", "so0" if i % 2 == 0 else "so1")
                dve_edge()
                P.vector.tensor_reduce(stat[:, 0:1], selr[:, 16 * i:16 * i + 4], AX.X,
                                       ALU.add, inc=[("dve", 1)])
                dve_edge()
                P.vector.tensor_scalar(stat[:, 1:2], stat[:, 0:1], 1.0, None, ALU.max,
                                       inc=[("dve", 1)])
                dve_edge()
                P.vector.reciprocal(stat[:, 2:3], stat[:, 1:2], inc=[("dve", 1)])
                dve_edge()
                P.vector.tensor_scalar(stat[:, 3:4], stat[:, 0:1], 0.0, None, ALU.is_le,
                                       inc=[("dve", 1)])
                dve_edge()
                P.vector.scalar_tensor_tensor(fb[:], psl[:], stat[:, 2:3],
                                              xs[i % 2][:], ALU.mult, ALU.add,
                                              inc=[("dve", 1)])
                dve_edge()
                P.vector.scalar_tensor_tensor(fb[:], nullb[:], stat[:, 3:4], fb[:],
                                              ALU.mult, ALU.add, inc=[("dve", 1)])
                P.mark(f"cdve{i}")
                P.wait("sync", "dve")
                P.sync.dma_start(out=out_y[r0:r0 + 128, :], in_=fb[:],
                                 inc=[("so0" if i % 2 == 0 else "so1", 16)])
                P.mark(f"cmb{i}")
            for sm in ["ste", "str", "stz", "stw"]:
                P.wait("sync", sm)
            for sm in ["gp", "gs", "gz", "gz1", "gq", "gqb", "gqc", "sw", "pool", "so0",
                       "so1", "dbg2", "ld", "lt", "lx", "lr", "lw", "dbg"]:
                P.wait("gpsimd", sm)
            P.wait("scalar", "act")
            P.wait("vector", "dve")
            P.wait("tensor", "pe")
            return


        with nc.Block() as block:
            for eng in Prog.ENGINES:
                def make(e):
                    def f(_):
                        P.active = e
                        P.reset()
                        program()
                    return f
                getattr(block, eng)(make(eng))

    return nc


def _consts():
    c = {}
    c["DIFFS_T"] = np.maximum(np.arange(C)[None, :] - 1 - np.arange(C)[:, None], 0) \
        .astype(np.float32)
    c["CAUSAL_T"] = (np.arange(C)[None, :] > np.arange(C)[:, None]).astype(np.float32)
    c["IMODBC"] = np.repeat((np.arange(WIN) % C)[None, :].astype(np.float32),
                             128, axis=0)
    c["IWINBC"] = np.repeat(np.arange(WIN, dtype=np.float32)[None, :], 128, axis=0)
    p = np.arange(128, dtype=np.float32)
    c["REV_COL"] = np.stack([C - 1 - 128 * tl - p for tl in range(4)], axis=1) \
        .astype(np.float32)
    c["TRIS_F"] = (np.arange(128)[:, None] < np.arange(128)[None, :]).astype(np.float32)
    c["ONES_ROW_F"] = np.ones((1, 128), np.float32)
    c["ONES_COL_B"] = np.ones((128, 1), ml_dtypes.bfloat16)
    c["ONES_COL_F"] = np.ones((128, 1), np.float32)
    c["SEL16"] = (np.arange(128)[:, None] // 16 == np.arange(8)[None, :]).astype(np.float32)
    c["SEL16T"] = c["SEL16"].T.copy()
    iw = (np.arange(128) % 16) // 4
    ii = np.arange(128) // 16
    c["SELW32"] = (ii[:, None] * 4 + iw[:, None] == np.arange(32)[None, :]).astype(np.float32)
    c["NQCAPC"] = np.repeat(np.array([[1024.0] * 4 + [512.0] * 4], np.float32), 128, axis=0)
    c["TGT"] = np.where(np.arange(128)[:, None] < 64, 1024.0, 512.0).astype(np.float32)
    c["IBLK4"] = np.repeat((np.arange(32) % 4)[None, :].astype(np.float32), 128, axis=0)
    c["IBLK128"] = np.repeat((np.arange(1024) % 128)[None, :].astype(ml_dtypes.bfloat16),
                             128, axis=0)
    c["IOTA_COL"] = np.arange(128, dtype=np.float32)[:, None]
    c["IDENT8"] = np.eye(8, dtype=np.float32)
    c["IDENT32"] = np.eye(32, dtype=np.float32)
    return c


_CONSTS = _consts()


def build_in_maps(L0_attn_out, L0_attn_W_write, L0_attn_W_read, L0_attn_decay, L0_attn_log_alpha,
           L0_ffn_w_qr, L0_ffn_w_kvr, L0_ffn_w_q, L0_ffn_w_kv, L0_ffn_w_out,
           L0_ffn_null_kv, L0_ffn_null_token):
    x = np.asarray(L0_attn_out, np.float32)
    Ww = np.asarray(L0_attn_W_write, np.float32)
    Wr = np.asarray(L0_attn_W_read, np.float32)
    w8 = np.concatenate([np.asarray(L0_ffn_w_qr, np.float32).T,
                         np.asarray(L0_ffn_w_kvr, np.float32).T], axis=1)
    wq = np.asarray(L0_ffn_w_q, np.float32)
    wkv = np.asarray(L0_ffn_w_kv, np.float32)
    wout = np.asarray(L0_ffn_w_out, np.float32)
    nkv = np.asarray(L0_ffn_null_kv, np.float32)
    ntok = np.asarray(L0_ffn_null_token, np.float32).reshape(1, D)

    in_maps = []
    for cid in range(8):
        b, w = cid // 4, cid % 4
        g = w
        xw = np.zeros((WIN + 1, D), np.float32)
        lo = 2048 * w
        if w == 0:
            xw[1:] = x[b, 0:2048]
        else:
            xw[:] = x[b, lo - 1:lo + 2048]
        expw = np.array([[max(2048.0 * (w - 1 - j), 0.0) for j in range(4)]], np.float32)
        maskw = np.array([[1.0 if j < w else 0.0 for j in range(4)]], np.float32)
        pv = np.zeros((32, 8), np.float32)
        for i in range(8):
            for wp in range(w):
                pv[i * 4 + wp, i] = 1.0
        nv = np.concatenate([nkv[1, g].reshape(8, 64), np.ones((8, 1), np.float32)],
                            axis=1).reshape(1, 8 * 65)
        possel = np.zeros((32, 20), np.float32)
        for j in range(12):
            possel[8 * (j // 3) % 32, 0] += 0
            possel[(8 * (j // 3) + g) % 32, j] = 1.0
        for j in range(12, 20):
            possel[(8 * ((j - 12) // 2) + 4 + g) % 32, j] = 1.0
        gselq_a = np.zeros((128, 8), np.float32)
        gselq_a[:, g] = 1.0
        gselk_a = np.zeros((128, 8), np.float32)
        gselk_a[:, 4 + g] = 1.0
        kbase20 = np.zeros((128, 20), np.float32)
        for j in range(12):
            kbase20[:, j] = 128.0 * (j % 3)
        for j in range(12, 20):
            kbase20[:, j] = 128.0 * ((j - 12) % 2)
        combsel_a = np.zeros((32, 12), np.float32)
        combsel2_a = np.zeros((32, 12), np.float32)
        for j in range(12):
            gp_ = j // 3
            combsel_a[4 * gp_ + w, j] = 1.0
            combsel2_a[8 * w + gp_, j] = 1.0
        kbase12 = np.repeat((128.0 * (np.arange(12) % 3))[None, :].astype(np.float32),
                            128, axis=0)
        prefixw_a = np.zeros((32, 32), np.float32)
        for i in range(8):
            for wnum in range(4):
                for wp in range(wnum):
                    prefixw_a[4 * i + wp, 4 * i + wnum] = 1.0
        nq32_a = np.array([[1024.0] * 16 + [512.0] * 16], np.float32).reshape(32, 1)
        cntsel_a = np.zeros((32, 20), np.float32)
        for j in range(12):
            cntsel_a[4 * g + j // 3, j] = 1.0
        for j in range(12, 20):
            cntsel_a[16 + 4 * g + (j - 12) // 2, j] = 1.0
        m = dict(
            x_win=xw, WwT=Ww.T.copy(), WrT=Wr.T.copy(),
            decay=np.asarray(L0_attn_decay, np.float32).reshape(1, 1),
            log_alpha=np.asarray(L0_attn_log_alpha, np.float32).reshape(1, 1),
            w8in=w8.copy(), wqT=wq[g].T.copy(), wkvT=wkv[g].T.copy(),
            woutT=wout[g].T.copy(),
            nk_in=nkv[0, g].reshape(4, 128).T.copy(),
            nv_in=nv,
            null_in=np.repeat(ntok, 128, axis=0),
            EXPW=expw, MASKW=maskw, PREVW=pv,
            WOFF=np.array([[2048.0 * w]], np.float32),
            POSSEL=possel, GSELQ=gselq_a, GSELK=gselk_a, KBASE20=kbase20,
            COMBSEL=combsel_a, COMBSEL2=combsel2_a, KBASE12=kbase12,
            PREFIXW=prefixw_a, NQ32=nq32_a, CNTSEL=cntsel_a,
            **_CONSTS,
        )
        in_maps.append(m)

    return in_maps


def kernel(**inputs):
    nc = build_nc()
    in_maps = build_in_maps(**inputs)
    res = run_bass_kernel_spmd(nc, in_maps, core_ids=list(range(8)))
    global _LAST_RES
    _LAST_RES = res
    out = np.zeros((B, T, D), np.float32)
    for cid in range(8):
        b, w = cid // 4, cid % 4
        out[b, 2048 * w:2048 * (w + 1)] = res.results[cid]["out_y"]
    return out


# version 10
# speedup vs baseline: 1.0580x; 1.0118x over previous
import numpy as np
import ml_dtypes
from contextlib import ExitStack

import concourse.bass as bass
import concourse.mybir as mybir
from concourse.bass_utils import run_bass_kernel_spmd
from concourse.bass_core_rust import IndirectOffsetOnAxis

F32 = mybir.dt.float32
BF16 = mybir.dt.bfloat16
I32 = mybir.dt.int32
AF = mybir.ActivationFunctionType
ALU = mybir.AluOpType
AX = mybir.AxisListType

B, T, D = 2, 8192, 1024
WIN = 2048
C = 512
NCH = WIN // C
NT = WIN // 128
G, NQ, NK, H, DH = 4, 1024, 512, 8, 64
DI = H * DH
QCAP = 352
QS = G * QCAP
KCAP = 256
KS = 4 * KCAP
PADW = 512
BIS_IT = 40
AUGW = 1032
SHIFT = -24.0
RG = [[0, 1, 2, 3], [4, 5, 6, 7]]
BIGIDX = float(1 << 22)

PHASE = 3


class Prog:
    ENGINES = ("tensor", "vector", "scalar", "gpsimd", "sync")

    def __init__(self, nc, sems):
        self.nc = nc
        self.sems = sems
        self.counts = {k: 0 for k in sems}
        self.active = None
        self.marks = {}

    def reset(self):
        self.counts = {k: 0 for k in self.sems}
        self.marks = {}

    def _do(self, eng, opname, *args, inc=(), **kw):
        if self.active == eng:
            instr = getattr(getattr(self.nc, eng), opname)(*args, **kw)
            for (s, n) in inc:
                instr = instr.then_inc(self.sems[s], n)
        for (s, n) in inc:
            self.counts[s] += n

    def wait(self, eng, sem, val=None):
        v = self.counts[sem] if val is None else val
        if self.active == eng:
            getattr(self.nc, eng).wait_ge(self.sems[sem], v)

    def mark(self, name):
        self.marks[name] = dict(self.counts)

    def wmark(self, eng, name, *semlist):
        for s in semlist:
            if self.active == eng:
                getattr(self.nc, eng).wait_ge(self.sems[s], self.marks[name][s])

    def __getattr__(self, name):
        if name in Prog.ENGINES:
            prog = self

            class _E:
                def __getattr__(self, op):
                    def f(*a, inc=(), **kw):
                        prog._do(name, op, *a, inc=inc, **kw)
                    return f
            return _E()
        raise AttributeError(name)


def build_nc():
    nc = bass.Bass("TRN2", target_bir_lowering=False)
    dt = nc.dram_tensor

    x_win = dt("x_win", [WIN + 1, D], F32, kind="ExternalInput")
    WwT = dt("WwT", [D, D], F32, kind="ExternalInput")
    WrT = dt("WrT", [D, D], F32, kind="ExternalInput")
    decay = dt("decay", [1, 1], F32, kind="ExternalInput")
    log_alpha = dt("log_alpha", [1, 1], F32, kind="ExternalInput")
    w8in = dt("w8in", [D, 8], F32, kind="ExternalInput")
    wqT = dt("wqT", [D, DI], F32, kind="ExternalInput")
    wkvT = dt("wkvT", [D, 2 * DI], F32, kind="ExternalInput")
    woutT = dt("woutT", [DI, D], F32, kind="ExternalInput")
    nk_in = dt("nk_in", [128, 4], F32, kind="ExternalInput")
    nv_in = dt("nv_in", [1, 8 * 65], F32, kind="ExternalInput")
    null_in = dt("null_in", [128, D], F32, kind="ExternalInput")
    DIFFS_T = dt("DIFFS_T", [C, C], F32, kind="ExternalInput")
    CAUSAL_T = dt("CAUSAL_T", [C, C], F32, kind="ExternalInput")
    IMODBC = dt("IMODBC", [128, WIN], F32, kind="ExternalInput")
    IWINBC = dt("IWINBC", [128, WIN], F32, kind="ExternalInput")
    REV_COL = dt("REV_COL", [128, 4], F32, kind="ExternalInput")
    EXPW = dt("EXPW", [1, 4], F32, kind="ExternalInput")
    MASKW = dt("MASKW", [1, 4], F32, kind="ExternalInput")
    TRIS_F = dt("TRIS_F", [128, 128], F32, kind="ExternalInput")
    ONES_ROW_F = dt("ONES_ROW_F", [1, 128], F32, kind="ExternalInput")
    ONES_COL_B = dt("ONES_COL_B", [128, 1], BF16, kind="ExternalInput")
    ONES_COL_F = dt("ONES_COL_F", [128, 1], F32, kind="ExternalInput")
    SEL16 = dt("SEL16", [128, 8], F32, kind="ExternalInput")
    SEL16T = dt("SEL16T", [8, 128], F32, kind="ExternalInput")
    SELW32 = dt("SELW32", [128, 32], F32, kind="ExternalInput")
    PREVW = dt("PREVW", [32, 8], F32, kind="ExternalInput")
    NQCAPC = dt("NQCAPC", [128, 8], F32, kind="ExternalInput")
    TGT = dt("TGT", [128, 1], F32, kind="ExternalInput")
    IBLK4 = dt("IBLK4", [128, 32], F32, kind="ExternalInput")
    IBLK128 = dt("IBLK128", [128, 1024], BF16, kind="ExternalInput")
    IOTA_COL = dt("IOTA_COL", [128, 1], F32, kind="ExternalInput")
    IDENT8 = dt("IDENT8", [8, 8], F32, kind="ExternalInput")
    IDENT32 = dt("IDENT32", [32, 32], F32, kind="ExternalInput")
    WOFF = dt("WOFF", [1, 1], F32, kind="ExternalInput")
    POSSEL = dt("POSSEL", [32, 20], F32, kind="ExternalInput")
    GSELQ = dt("GSELQ", [128, 8], F32, kind="ExternalInput")
    GSELK = dt("GSELK", [128, 8], F32, kind="ExternalInput")
    KBASE20 = dt("KBASE20", [128, 20], F32, kind="ExternalInput")
    COMBSEL = dt("COMBSEL", [32, 12], F32, kind="ExternalInput")
    COMBSEL2 = dt("COMBSEL2", [32, 12], F32, kind="ExternalInput")
    KBASE12 = dt("KBASE12", [128, 12], F32, kind="ExternalInput")
    PREFIXW = dt("PREFIXW", [32, 32], F32, kind="ExternalInput")
    CNTSEL = dt("CNTSEL", [32, 20], F32, kind="ExternalInput")
    NQ32 = dt("NQ32", [32, 1], F32, kind="ExternalInput")

    out_y = dt("out_y", [WIN, D], F32, kind="ExternalOutput")
    dbg_y1 = dt("dbg_y1", [WIN, D], F32, kind="ExternalOutput")
    dbg_z = dt("dbg_z", [WIN + 1, D], BF16, kind="ExternalOutput")
    dbg_reads = dt("dbg_reads", [WIN, D], BF16, kind="ExternalOutput")
    dbg_wst = dt("dbg_wst", [D, D], BF16, kind="ExternalOutput")
    dbg_aseg = dt("dbg_aseg", [D, D], BF16, kind="ExternalOutput")
    dbg_mix = dt("dbg_mix", [128, 6144], BF16, kind="ExternalOutput")
    dbg_s = dt("dbg_s", [128, 32], F32, kind="ExternalOutput")

    z_t_dram = dt("z_t_dram", [WIN + 1, D], BF16)
    reads_dram = dt("reads_dram", [WIN, D], BF16)
    y1_dram = dt("y1_dram", [WIN, D], F32)
    z2lo_dram = dt("z2lo_dram", [WIN, D], BF16)
    aseg_bi = dt("aseg_bi", [D, D], BF16)
    aseg_bo = dt("aseg_bo", [4 * D, D], BF16)
    wst_dram = dt("wst_dram", [D, D], BF16)
    logit_bi = dt("logit_bi", [8, WIN], F32)
    logit_bo = dt("logit_bo", [32, WIN], F32)
    z2aug_bi = dt("z2aug_bi", [WIN, AUGW], BF16)
    z2aug_bo = dt("z2aug_bo", [4 * WIN, AUGW], BF16)
    pos_bi = dt("pos_bi", [8, PADW], F32)
    pos_bo = dt("pos_bo", [32, PADW], F32)
    qcomp_dram = dt("qcomp_dram", [QS, D], BF16)
    kvcomp_dram = dt("kvcomp_dram", [KS, D], BF16)
    qlog_dram = dt("qlog_dram", [QS, 1], F32)
    o_bi = dt("o_bi", [QS, D], BF16)
    o_bo = dt("o_bo", [4 * QS, D], BF16)
    winbuf = dt("winbuf", [WIN, D], F32)

    al = ExitStack()
    sbf = lambda n, s, d: al.enter_context(nc.sbuf_tensor(n, s, d))
    pst = lambda n, s: al.enter_context(nc.psum_tensor(n, s, F32))

    with al:
        z_f = [sbf(f"z_f{i}", [128, WIN + 16], BF16) for i in range(8)]
        z_fs = [sbf(f"z_fs{i}", [128, WIN + 16], BF16) for i in range(8)]
        gpatw = sbf("gpatw", [128, WIN], BF16)

        AR2N = 47104
        AR2 = sbf("AR2", [128, AR2N], BF16)

        def a2(off, w):
            return AR2[:, off:off + w]

        WwT_b = [a2(1024 * i, 1024) for i in range(8)]
        Wst = [[a2(8192 + 8192 * p + 1024 * i, 1024) for i in range(8)] for p in range(2)]
        wk_sc = [a2(24576 + 1024 * i, 1024) for i in range(4)]
        v_sb = [a2(28672 + 1024 * i, 1024) for i in range(4)]
        SMt = [a2(32768 + 512 * i, 512) for i in range(4)]
        M_T = [a2(34816 + 512 * i, 512) for i in range(4)]
        rd_t = [a2(36864 + 1024 * i, 1024) for i in range(2)]
        WrT_b = [a2(38912 + 1024 * i, 1024) for i in range(8)]
        readsf = [a2(1024 * i, 1024) for i in range(8)]
        wstT_b = [a2(8192 + 1024 * i, 1024) for i in range(8)]
        m2t = [a2(24576 + 1024 * i, 1024) for i in range(8)]
        zfg = [a2(32768 + 128 * i, 128) for i in range(8)]
        wq_b = [a2(512 * i, 512) for i in range(8)]
        wkv_b = [a2(4096 + 1024 * i, 1024) for i in range(8)]
        wout_b = [a2(12288 + 1024 * i, 1024) for i in range(4)]
        vtok = [a2(16384 + 520 * i, 520) for i in range(8)]
        expT = [a2(20544 + 1408 * i, 1408) for i in range(8)]
        oT = [a2(31808 + 1408 * i, 1408) for i in range(4)]
        recb = a2(37440, 1408)[0:64, :]
        grow = [a2(38848 + 1032 * i, AUGW) for i in range(3)]
        kvf = [a2(41944 + 1024 * i, 1024) for i in range(4)]

        xs = [sbf(f"xs{i}", [128, D], F32) for i in range(2)]
        f1 = sbf("f1", [128, D], F32)
        gen_tok = sbf("gen_tok", [128, D], BF16)
        stat = sbf("stat", [128, 16], F32)
        scal = sbf("scal", [128, 16], F32)
        tiny = sbf("tiny", [1, 64], F32)
        rowb = sbf("rowb", [1, 64], F32)
        gw4 = sbf("gw4", [128, 4], F32)
        logit_tok = sbf("logit_tok", [128, 8 * NT], F32)
        bw = sbf("bw", [128, 16], F32)
        cnt32 = sbf("cnt32", [32, 1], F32)
        cntb32 = sbf("cntb32", [128, 32], F32)
        selr = sbf("selr", [128, 16 * NT], F32)
        offrow = sbf("offrow", [1, 8], F32)
        offib = sbf("offib", [128, 8], F32)
        taub = sbf("taub", [128, 8], F32)
        pose = sbf("pose", [128, 48], F32)
        ehi = sbf("ehi", [128, 32], F32)
        idx_f = sbf("idx_f", [128, 24], F32)
        idx_i = sbf("idx_i", [128, 24], I32)

        AR4N = 6016
        AR4 = sbf("AR4", [128, AR4N], F32)

        def a4(off, w):
            return AR4[:, off:off + w]

        logits_sb = a4(0, 2048)[0:8, :]
        ocomb = [a4(1024, 1024), a4(2048, 1024)]
        bis = a4(2048, 512)
        elo = a4(2560, 1024)
        possb = a4(3584, 1024)[0:32, :]
        sumrow = a4(4608, 1408)[0:1, :]
        blk16 = a4(3072, 128)

        tris = sbf("tris", [128, 128], F32)
        onesr = sbf("onesr", [1, 128], F32)
        onescb = sbf("onescb", [128, 1], BF16)
        onescf = sbf("onescf", [128, 1], F32)
        sel16 = sbf("sel16", [128, 8], F32)
        sel16t = sbf("sel16t", [8, 128], F32)
        selw32 = sbf("selw32", [128, 32], F32)
        prevw = sbf("prevw", [32, 8], F32)
        nqcap = sbf("nqcap", [128, 8], F32)
        tgt = sbf("tgt", [128, 1], F32)
        iblk4 = sbf("iblk4", [128, 32], F32)
        iblk128 = sbf("iblk128", [128, 1024], BF16)
        iota_c = sbf("iota_c", [128, 1], F32)
        ident8 = sbf("ident8", [8, 8], F32)
        ident32 = sbf("ident32", [32, 32], F32)
        rev4 = sbf("rev4", [128, 4], F32)
        nullb = a4(0, 1024)
        nkb = sbf("nkb", [128, 4], BF16)
        nvb = sbf("nvb", [1, 8 * 65], BF16)
        possel = sbf("possel", [32, 20], F32)
        gselq = sbf("gselq", [128, 8], F32)
        gselk = sbf("gselk", [128, 8], F32)
        kbase20 = sbf("kbase20", [128, 20], F32)
        combsel = sbf("combsel", [32, 12], F32)
        combsel2 = sbf("combsel2", [32, 12], F32)
        kbase12 = sbf("kbase12", [128, 12], F32)
        prefixw = sbf("prefixw", [32, 32], F32)
        cntsel = sbf("cntsel", [32, 20], F32)
        nq32 = sbf("nq32", [32, 1], F32)
        cnt20b = sbf("cnt20b", [128, 20], F32)
        valid20 = sbf("valid20", [128, 20], F32)
        idxq_i = sbf("idxq_i", [128, 20], I32)
        cm_f = sbf("cm_f", [128, 12], F32)
        cm_i = sbf("cm_i", [128, 12], I32)
        cm2_i = sbf("cm2_i", [128, 12], I32)
        snapw = sbf("snapw", [1, 16], F32)

        w8hi = sbf("w8hi", [128, 64], BF16)
        w8lo = sbf("w8lo", [128, 64], BF16)
        expN = a2(38848, QS)[0:1, :]
        sig_kv = sbf("sig_kv", [128, 8], F32)
        sig_q = sbf("sig_q", [128, 12], F32)
        kbias = sbf("kbias", [128, 8], F32)
        zt_eg = [a2(34816 + 1024 * i, 1024) for i in range(2)]
        kf = [z_fs[i].ap()[:, 0:KS + 1] for i in range(4)]
        qh = [z_fs[4 + i].ap()[:, 0:QS] for i in range(4)]
        qf = [z_f[i].ap()[:, 0:QS] for i in range(8)]

        ps0 = pst("ps0", [128, 1024])
        ps1 = pst("ps1", [128, 1024])
        ps2 = pst("ps2", [128, 1024])
        ps3 = pst("ps3", [128, 1024])

        sems = {k: al.enter_context(nc.semaphore(k))
                for k in ["ld", "lt", "lx", "stz", "str", "ste", "stw", "gp", "gs", "gz",
                          "lr", "lw", "gz1", "gq", "gqb", "gqc", "sw", "pool", "so0", "so1", "pe", "dve", "act", "cc", "dbg", "dbg2"]}
        P = Prog(nc, sems)

        def dve_edge():
            P.wait("vector", "dve")

        def ln_tile(src_f32, rn, out_bf):
            P.vector.tensor_reduce(stat[0:rn, 0:1], src_f32[0:rn, :], AX.X, ALU.add,
                                   inc=[("dve", 1)])
            P.wait("scalar", "dve")
            P.scalar.activation(f1[0:rn, :], src_f32[0:rn, :], AF.Square, inc=[("act", 1)])
            P.wait("vector", "act")
            P.vector.tensor_reduce(stat[0:rn, 1:2], f1[0:rn, :], AX.X, ALU.add, inc=[("dve", 1)])
            dve_edge()
            P.vector.tensor_scalar(stat[0:rn, 2:3], stat[0:rn, 0:1], 1.0 / D, None, ALU.mult,
                                   inc=[("dve", 1)])
            dve_edge()
            P.vector.tensor_scalar(stat[0:rn, 3:4], stat[0:rn, 1:2], 1.0 / D, None, ALU.mult,
                                   inc=[("dve", 1)])
            dve_edge()
            P.vector.tensor_tensor(stat[0:rn, 4:5], stat[0:rn, 2:3], stat[0:rn, 2:3], ALU.mult,
                                   inc=[("dve", 1)])
            dve_edge()
            P.vector.tensor_tensor(stat[0:rn, 5:6], stat[0:rn, 3:4], stat[0:rn, 4:5],
                                   ALU.subtract, inc=[("dve", 1)])
            P.wait("scalar", "dve")
            P.scalar.activation(stat[0:rn, 6:7], stat[0:rn, 5:6], AF.Ln, bias=scal[0:rn, 8:9],
                                inc=[("act", 1)])
            P.wait("scalar", "act")
            P.scalar.activation(stat[0:rn, 7:8], stat[0:rn, 6:7], AF.Exp, scale=-0.5,
                                inc=[("act", 1)])
            P.wait("vector", "act")
            P.vector.tensor_scalar(out_bf[0:rn, :], src_f32[0:rn, :], stat[0:rn, 2:3],
                                   stat[0:rn, 7:8], ALU.subtract, op1=ALU.mult, inc=[("dve", 1)])

        def program():
            LG = scal[:, 5:6]
            ALPHA = scal[:, 6:7]
            GC = scal[:, 7:8]
            WOFFB = scal[:, 14:15]
            WJ = [stat[:, 8 + j:9 + j] for j in range(4)]

            P.vector.memset(tiny[:], 0.0, inc=[("dve", 1)])
            P.vector.memset(rowb[:], 0.0, inc=[("dve", 1)])
            P.vector.memset(bw[:], 0.0, inc=[("dve", 1)])
            P.vector.memset(stat[:], 0.0, inc=[("dve", 1)])
            P.vector.memset(scal[:], 0.0, inc=[("dve", 1)])
            P.wait("sync", "dve")
            for (dst, src) in [(tris, TRIS_F), (onesr, ONES_ROW_F), (onescb, ONES_COL_B),
                               (onescf, ONES_COL_F), (sel16, SEL16), (sel16t, SEL16T),
                               (selw32, SELW32), (prevw, PREVW), (nqcap, NQCAPC),
                               (tgt, TGT), (iblk4, IBLK4), (iblk128, IBLK128),
                               (iota_c, IOTA_COL), (ident8, IDENT8), (ident32, IDENT32),
                               (rev4, REV_COL), (possel, POSSEL), (gselq, GSELQ),
                               (gselk, GSELK), (kbase20, KBASE20), (combsel, COMBSEL),
                               (combsel2, COMBSEL2), (kbase12, KBASE12),
                               (prefixw, PREFIXW), (nq32, NQ32),
                               (cntsel, CNTSEL)]:
                P.sync.dma_start(out=dst[:], in_=src[:], inc=[("ld", 16)])
            P.sync.dma_start(out=tiny[0:1, 0:1], in_=decay[:], inc=[("ld", 16)])
            P.sync.dma_start(out=tiny[0:1, 1:2], in_=log_alpha[:], inc=[("ld", 16)])
            P.sync.dma_start(out=tiny[0:1, 8:12], in_=EXPW[:], inc=[("ld", 16)])
            P.sync.dma_start(out=tiny[0:1, 12:16], in_=MASKW[:], inc=[("ld", 16)])
            P.sync.dma_start(out=tiny[0:1, 30:31], in_=WOFF[:], inc=[("ld", 16)])
            casts = [(WwT_b[i], WwT, i) for i in range(8)] + \
                    [(WrT_b[i], WrT, i) for i in range(8)]
            CB = [xs[0], xs[1], f1]
            CCLS = ["ld", "lt", "lx"]
            for n, (cdst, csrc, i) in enumerate(casts):
                b = n % 3
                if n >= 3:
                    P.wmark("sync", f"cast{n - 3}", "dve")
                else:
                    P.wait("sync", "dve")
                P.sync.dma_start(out=CB[b][:, 0:D], in_=csrc[128 * i:128 * (i + 1), :],
                                 inc=[(CCLS[b], 16)])
                P.wait("vector", CCLS[b])
                P.vector.tensor_copy(cdst[:], CB[b][:, 0:D], inc=[("dve", 1)])
                P.mark(f"cast{n}")
            for i in range(8):
                P.wait("sync", "dve")
                P.sync.dma_start(out=xs[0][:, 0:8], in_=w8in[128 * i:128 * (i + 1), :],
                                 inc=[("ld", 16)])
                P.wait("vector", "ld")
                P.vector.tensor_copy(w8hi[:, 8 * i:8 * (i + 1)], xs[0][:, 0:8], inc=[("dve", 1)])
                dve_edge()
                P.vector.tensor_tensor(w8lo[:, 8 * i:8 * (i + 1)], xs[0][:, 0:8],
                                       w8hi[:, 8 * i:8 * (i + 1)], ALU.subtract, inc=[("dve", 1)])
            for et in range(8):
                P.vector.memset(Wst[0][et][:], 0.0, inc=[("dve", 1)])
            P.wait("scalar", "ld")
            P.scalar.activation(tiny[0:1, 2:3], tiny[0:1, 0:1], AF.Exp, scale=-1.0,
                                inc=[("act", 1)])
            P.wait("vector", "act")
            P.vector.tensor_scalar(tiny[0:1, 3:4], tiny[0:1, 2:3], 1.0, None, ALU.add,
                                   inc=[("dve", 1)])
            dve_edge()
            P.vector.reciprocal(tiny[0:1, 4:5], tiny[0:1, 3:4], inc=[("dve", 1)])
            P.wait("scalar", "dve")
            P.scalar.activation(tiny[0:1, 5:6], tiny[0:1, 4:5], AF.Ln, inc=[("act", 1)])
            P.scalar.activation(tiny[0:1, 6:7], tiny[0:1, 1:2], AF.Exp, inc=[("act", 1)])
            P.wait("vector", "act")
            P.vector.tensor_scalar(tiny[0:1, 16:20], tiny[0:1, 8:12], tiny[0:1, 5:6], None,
                                   ALU.mult, inc=[("dve", 1)])
            P.wait("scalar", "dve")
            P.scalar.activation(tiny[0:1, 20:24], tiny[0:1, 16:20], AF.Exp, inc=[("act", 1)])
            P.wait("vector", "act")
            P.vector.tensor_tensor(tiny[0:1, 24:28], tiny[0:1, 20:24], tiny[0:1, 12:16],
                                   ALU.mult, inc=[("dve", 1)])
            P.wait("tensor", "dve")
            P.wait("tensor", "act")
            P.wait("tensor", "ld")
            P.tensor.matmul(ps0[:, 0:16], onesr[:], tiny[0:1, 0:16], start=True, stop=True,
                            inc=[("pe", 1)])
            P.tensor.matmul(ps0[:, 16:20], onesr[:], tiny[0:1, 24:28], start=True, stop=True,
                            inc=[("pe", 1)])
            P.tensor.matmul(ps0[:, 20:21], onesr[:], tiny[0:1, 30:31], start=True, stop=True,
                            inc=[("pe", 1)])
            P.wait("vector", "pe")
            P.vector.tensor_copy(scal[:, 0:14], ps0[:, 0:14], inc=[("dve", 1)])
            dve_edge()
            P.vector.tensor_copy(stat[:, 8:12], ps0[:, 16:20], inc=[("dve", 1)])
            dve_edge()
            P.vector.tensor_copy(scal[:, 14:15], ps0[:, 20:21], inc=[("dve", 1)])
            P.wait("scalar", "dve")
            P.scalar.activation(scal[:, 7:8], LG, AF.Exp, scale=float(C), inc=[("act", 1)])
            P.scalar.activation(gw4[:], rev4[:], AF.Exp, scale=LG, inc=[("act", 1)])
            P.vector.memset(scal[:, 8:9], 1e-5, inc=[("dve", 1)])
            dve_edge()
            P.vector.memset(scal[:, 10:11], SHIFT, inc=[("dve", 1)])
            for o in range(0, WIN, D):
                P.wait("sync", "act")
                P.wait("sync", "dve")
                P.sync.dma_start(out=xs[0][:], in_=IWINBC[:, o:o + D], inc=[("ld", 16)])
                P.wait("scalar", "ld")
                P.scalar.activation(gpatw[:, o:o + D], xs[0][:], AF.Exp, scale=LG,
                                    inc=[("act", 1)])
            for kt in range(4):
                P.wait("sync", "act")
                P.wait("sync", "dve")
                P.sync.dma_start(out=xs[0][:, 0:C], in_=DIFFS_T[128 * kt:128 * (kt + 1), :],
                                 inc=[("ld", 16)])
                P.wait("scalar", "ld")
                P.scalar.activation(f1[:, 0:C], xs[0][:, 0:C], AF.Exp, scale=LG, inc=[("act", 1)])
                P.wait("sync", "act")
                P.sync.dma_start(out=xs[0][:, 0:C], in_=CAUSAL_T[128 * kt:128 * (kt + 1), :],
                                 inc=[("ld", 16)])
                P.wait("vector", "ld")
                P.wait("vector", "act")
                P.vector.tensor_tensor(M_T[kt][:], f1[:, 0:C], xs[0][:, 0:C], ALU.mult,
                                       inc=[("dve", 1)])
            P.mark("setup")

            nrow = WIN + 1
            ntile = (nrow + 127) // 128

            def ln_emit(i):
                r0, rn = 128 * i, min(128, nrow - 128 * i)
                buf = xs[i % 2]
                if i >= 2:
                    P.wmark("sync", f"ln{i - 2}", "dve")
                else:
                    P.wmark("sync", "setup", "dve", "act")
                P.sync.dma_start(out=buf[0:rn, :], in_=x_win[r0:r0 + rn, :], inc=[("ld", 16)])
                P.wait("vector", "ld")
                P.wait("scalar", "ld")
                if i < 2:
                    P.wmark("vector", "setup", "act")
                    P.wmark("scalar", "setup", "dve")
                if i >= 1:
                    P.wait("vector", "stz")
                ln_tile(buf, rn, gen_tok)
                P.wait("sync", "dve")
                P.sync.dma_start(out=z_t_dram[r0:r0 + rn, :], in_=gen_tok[0:rn, :],
                                 inc=[("stz", 16)])
                P.mark(f"ln{i}")

            def ztrans_chunk(cc):
                P.wmark("sync", f"ln{4 * cc + 4}", "stz")
                for dl in range(8):
                    P.sync.dma_start(
                        out=z_f[dl][:, 16 + 512 * cc:16 + 512 * (cc + 1)],
                        in_=z_t_dram[1 + 512 * cc:1 + 512 * (cc + 1),
                                     128 * dl:128 * (dl + 1)],
                        transpose=True,
                        inc=[("lt" if cc % 2 == 0 else "lx", 16)])

            def zfs_chunk(cc):
                o = 512 * cc
                P.wmark("sync", f"ln{4 * cc + 4}", "dve")
                P.wait("sync", "act")
                P.sync.dma_start(out=f1[:, 512:1024], in_=IMODBC[:, o:o + 512],
                                 inc=[("lw", 16)])
                P.wait("scalar", "lw")
                P.wait("scalar", "dve")
                P.scalar.activation(f1[:, 0:512], f1[:, 512:1024], AF.Exp, scale=LG,
                                    inc=[("act", 1)])
                P.wait("vector", "act")
                P.wait("vector", "lt" if cc % 2 == 0 else "lx")
                for dl in range(8):
                    P.vector.tensor_tensor(z_fs[dl][:, 16 + o:16 + o + 512],
                                           z_f[dl][:, 16 + o:16 + o + 512],
                                           f1[:, 0:512], ALU.mult, inc=[("dve", 1)])
                P.mark(f"zfc{cc}")

            for i in range(5):
                ln_emit(i)
            P.wmark("sync", "ln0", "stz")
            for dl in range(8):
                P.sync.dma_start(out=z_fs[dl][:, 0:16],
                                 in_=z_t_dram[0:16, 128 * dl:128 * (dl + 1)],
                                 transpose=True, inc=[("lx", 16)])
            P.wait("vector", "lx")
            for dl in range(8):
                P.vector.tensor_copy(z_f[dl][:, 15:16], z_fs[dl][:, 0:1],
                                     inc=[("dve", 1)])
            ztrans_chunk(0)
            zfs_chunk(0)

            def scan_chunk(c):
                cw = C * c
                pcur, pnxt = c % 2, (c + 1) % 2
                P.wmark("gpsimd", f"ln{4 * c + 4}", "stz")
                for tt in range(4):
                    if c > 0:
                        P.wmark("gpsimd", f"Adone{c - 1}", "pe")
                    P.gpsimd.dma_start(out=wk_sc[tt][:],
                                       in_=z_t_dram[cw + 128 * tt:cw + 128 * (tt + 1), :],
                                       inc=[("gp", 16)])
                P.wait("scalar", "gp")
                for tt in range(4):
                    P.scalar.mul(wk_sc[tt][:], wk_sc[tt][:], gw4[:, tt:tt + 1], inc=[("act", 1)])
                P.mark(f"wk{c}")
                P.wmark("tensor", f"zfc{c}", "dve")
                P.wait("tensor", "lt" if c % 2 == 0 else "lx")
                if c > 0:
                    P.wmark("tensor", f"Adone{c - 1}", "dve")
                for tt in range(4):
                    psv = ps0 if tt % 2 == 0 else ps1
                    if tt >= 2:
                        P.wmark("tensor", f"vev{c}_{tt - 2}", "act")
                    for hf in range(2):
                        for kt in range(8):
                            P.tensor.matmul(
                                psv[:, 512 * hf:512 * (hf + 1)],
                                z_f[kt][:, 16 + cw + 128 * tt:16 + cw + 128 * (tt + 1)],
                                WwT_b[kt][:, 512 * hf:512 * (hf + 1)],
                                start=(kt == 0), stop=(kt == 7),
                                inc=[("pe", 1)] if kt == 7 else ())
                    P.wait("scalar", "pe")
                    P.scalar.copy(v_sb[tt][:], psv[:], inc=[("act", 1)])
                    P.mark(f"vev{c}_{tt}")
                for kt in range(4):
                    pss = ps2[:, 0:512] if kt % 2 == 0 else ps2[:, 512:1024]
                    if kt >= 2:
                        P.wmark("tensor", f"sev{c}_{kt - 2}", "dve")
                    for jt in range(8):
                        P.tensor.matmul(
                            pss,
                            z_f[jt][:, 15 + cw + 128 * kt:15 + cw + 128 * (kt + 1)],
                            z_f[jt][:, 16 + cw:16 + cw + C],
                            start=(jt == 0), stop=(jt == 7),
                            inc=[("pe", 1)] if jt == 7 else ())
                    P.wait("vector", "pe")
                    P.vector.tensor_tensor(SMt[kt][:], pss, M_T[kt][:], ALU.mult,
                                           inc=[("dve", 1)])
                    P.mark(f"sev{c}_{kt}")
                P.wmark("tensor", f"wk{c}", "act")
                P.wmark("tensor", f"sev{c}_3", "dve")
                P.wmark("tensor", f"vev{c}_3", "act")
                for ct in range(4):
                    psr = ps3 if ct % 2 == 0 else ps0
                    if ct == 1:
                        P.wmark("tensor", f"vev{c}_2", "act")
                    if ct >= 2:
                        P.wmark("tensor", f"rev{c}_{ct - 2}", "act")
                    for hf in range(2):
                        for kt in range(4):
                            P.tensor.matmul(
                                psr[:, 512 * hf:512 * (hf + 1)],
                                SMt[kt][:, 128 * ct:128 * (ct + 1)],
                                v_sb[kt][:, 512 * hf:512 * (hf + 1)],
                                start=(kt == 0), stop=False)
                        for jt in range(8):
                            P.tensor.matmul(
                                psr[:, 512 * hf:512 * (hf + 1)],
                                z_fs[jt][:, 16 + cw + 128 * ct:16 + cw + 128 * (ct + 1)],
                                Wst[pcur][jt][:, 512 * hf:512 * (hf + 1)],
                                start=False, stop=(jt == 7),
                                inc=[("pe", 1)] if jt == 7 else ())
                    P.wait("scalar", "pe")
                    bufr = rd_t[ct % 2]
                    P.wait("scalar", "str")
                    P.scalar.copy(bufr[:], psr[:], inc=[("act", 1)])
                    P.wait("gpsimd", "act")
                    P.gpsimd.dma_start(out=reads_dram[cw + 128 * ct:cw + 128 * (ct + 1), :],
                                       in_=bufr[:], inc=[("str", 16)])
                    P.mark(f"rst{c}_{ct}")
                    P.mark(f"rev{c}_{ct}")
                for et in range(8):
                    for hf in range(2):
                        idx = 2 * et + hf
                        psa = ps1[:, 0:512] if idx % 2 == 0 else ps1[:, 512:1024]
                        if idx >= 2:
                            P.wmark("tensor", f"fold{c}_{idx - 2}", "dve")
                        if idx < 2:
                            P.wmark("tensor", f"vev{c}_3", "act")
                        for tt in range(4):
                            P.tensor.matmul(
                                psa,
                                wk_sc[tt][:, 128 * et:128 * (et + 1)],
                                v_sb[tt][:, 512 * hf:512 * (hf + 1)],
                                start=(tt == 0), stop=(tt == 3),
                                inc=[("pe", 1)] if tt == 3 else ())
                        P.wait("vector", "pe")
                        P.vector.scalar_tensor_tensor(
                            Wst[pnxt][et][:, 512 * hf:512 * (hf + 1)],
                            Wst[pcur][et][:, 512 * hf:512 * (hf + 1)],
                            GC, psa, ALU.mult, ALU.add, inc=[("dve", 1)])
                        P.mark(f"fold{c}_{idx}")
                P.mark(f"fold{c}")
                P.mark(f"Adone{c}")

            scan_chunk(0)
            for c in range(1, NCH):
                for i in range(4 * c + 1, 4 * c + 5):
                    ln_emit(i)
                ztrans_chunk(c)
                zfs_chunk(c)
                scan_chunk(c)

            pfin = NCH % 2
            P.wmark("sync", f"fold{NCH - 1}", "dve")
            for et in range(8):
                P.sync.dma_start(out=aseg_bi[128 * et:128 * (et + 1), :],
                                 in_=Wst[pfin][et][:], inc=[("stw", 16)])
            P.wait("gpsimd", "stw")
            P.gpsimd.collective_compute("AllGather", ALU.bypass, replica_groups=RG,
                                        ins=[aseg_bi.ap().opt()], outs=[aseg_bo.ap().opt()],
                                        inc=[("cc", 1)])
            P.wait("gpsimd", "cc")
            pw = (NCH + 1) % 2
            for et in range(8):
                P.wait("gpsimd", "dve")
                P.wait("gpsimd", "act")
                for j in range(4):
                    P.wmark("gpsimd", f"Adone{NCH - 1}", "pe")
                    P.gpsimd.dma_start(out=v_sb[j][:],
                                       in_=aseg_bo[D * j + 128 * et:D * j + 128 * (et + 1), :],
                                       inc=[("gs", 16)])
                P.wait("vector", "gs")
                P.vector.tensor_scalar(Wst[pw][et][:], v_sb[0][:], WJ[0], None, ALU.mult,
                                       inc=[("dve", 1)])
                for j in range(1, 4):
                    dve_edge()
                    P.vector.scalar_tensor_tensor(Wst[pw][et][:], v_sb[j][:], WJ[j],
                                                  Wst[pw][et][:], ALU.mult, ALU.add,
                                                  inc=[("dve", 1)])
                P.wait("sync", "dve")
                P.sync.dma_start(out=wst_dram[128 * et:128 * (et + 1), :], in_=Wst[pw][et][:],
                                 inc=[("stw", 16)])
            P.mark("wstdone")
            P.wait("sync", "stw")
            for il in range(8):
                P.sync.dma_start(out=wstT_b[il][:], in_=wst_dram[:, 128 * il:128 * (il + 1)],
                                 transpose=True, inc=[("lw", 16)])
            P.wait("tensor", "lw")
            for jt in range(8):
                psm = ps2 if jt % 2 == 0 else ps3
                if jt >= 2:
                    P.wmark("tensor", f"m2ev{jt - 2}", "dve")
                for hf in range(2):
                    for il in range(8):
                        P.tensor.matmul(psm[:, 512 * hf:512 * (hf + 1)],
                                        wstT_b[il][:, 128 * jt:128 * (jt + 1)],
                                        WrT_b[il][:, 512 * hf:512 * (hf + 1)],
                                        start=(il == 0), stop=(il == 7),
                                        inc=[("pe", 1)] if il == 7 else ())
                P.wait("vector", "pe")
                P.vector.tensor_copy(m2t[jt][:], psm[:], inc=[("dve", 1)])
                P.mark(f"m2ev{jt}")
            P.mark("m2done")

            P.wait("sync", "str")
            for dl in range(8):
                P.sync.dma_start(out=readsf[dl][:],
                                 in_=reads_dram[0:1024, 128 * dl:128 * (dl + 1)],
                                 transpose=True, inc=[("lr", 16)])
            P.mark("rtr0")

            zfgb = [[a2(32768 + 1024 * b + 128 * dl, 128) for dl in range(8)]
                    for b in range(2)]
            fbu = [f1, a4(2560, 1024)]
            sqb = a4(3584, 1024)
            P.wmark("vector", "m2done", "dve")
            for dl in range(8):
                P.vector.tensor_tensor(zfgb[0][dl][:], z_f[dl][:, 16:144],
                                       gpatw[:, 0:128], ALU.mult, inc=[("dve", 1)])
            P.mark("zfg0")
            for i in range(NT):
                r0 = 128 * i
                hf_sc = i // 8
                if i == 8:
                    P.wmark("sync", "eg7", "pe")
                    for dl in range(8):
                        P.sync.dma_start(out=readsf[dl][:],
                                         in_=reads_dram[1024:2048, 128 * dl:128 * (dl + 1)],
                                         transpose=True, inc=[("lr", 16)])
                    P.mark("rtr1")
                col = r0 - 1024 * hf_sc
                P.wmark("sync", f"eg{i - 2}", "dve") if i >= 2 else None
                P.sync.dma_start(out=xs[i % 2][:], in_=x_win[1 + r0:1 + r0 + 128, :],
                                 inc=[("lx", 16)])
                if i >= 2:
                    P.wmark("gpsimd", f"eg{i - 2}", "dve")
                P.gpsimd.dma_start(out=zt_eg[i % 2][:], in_=z_t_dram[1 + r0:1 + r0 + 128, :],
                                   inc=[("gz" if i % 2 == 0 else "gz1", 16)])
                psr = ps0 if i % 2 == 0 else ps1
                P.wmark("tensor", "rtr0" if hf_sc == 0 else "rtr1", "lr")
                P.wmark("tensor", f"zfg{i}", "dve")
                if i >= 2:
                    P.wmark("tensor", f"y1v{i - 2}", "dve")
                if i == 0:
                    P.wmark("tensor", "m2done", "dve")
                for hf in range(2):
                    for kt in range(8):
                        P.tensor.matmul(psr[:, 512 * hf:512 * (hf + 1)],
                                        readsf[kt][:, col:col + 128],
                                        WrT_b[kt][:, 512 * hf:512 * (hf + 1)],
                                        start=(kt == 0), stop=False)
                    for jt in range(8):
                        P.tensor.matmul(psr[:, 512 * hf:512 * (hf + 1)],
                                        zfgb[i % 2][jt][:],
                                        m2t[jt][:, 512 * hf:512 * (hf + 1)],
                                        start=False, stop=(jt == 7),
                                        inc=[("pe", 1)] if jt == 7 else ())
                P.mark(f"egmm{i}")
                if i + 1 < NT:
                    r1n = 128 * (i + 1)
                    for dl in range(8):
                        if i >= 1:
                            P.wmark("vector", f"egmm{i - 1}", "pe")
                        P.vector.tensor_tensor(zfgb[(i + 1) % 2][dl][:],
                                               z_f[dl][:, 16 + r1n:16 + r1n + 128],
                                               gpatw[:, r1n:r1n + 128], ALU.mult,
                                               inc=[("dve", 1)])
                    P.mark(f"zfg{i + 1}")
                fb = fbu[i % 2]
                P.wait("vector", "pe")
                P.wait("vector", "lx")
                P.wait("vector", "ste")
                P.wait("vector", "dbg")
                P.wait("vector", "gz" if i % 2 == 0 else "gz1")
                P.vector.scalar_tensor_tensor(fb[:], psr[:], ALPHA, xs[i % 2][:],
                                              ALU.mult, ALU.add, inc=[("dve", 1)])
                dve_edge()
                P.vector.tensor_tensor(fb[:], fb[:], zt_eg[i % 2][:], ALU.add,
                                       inc=[("dve", 1)])
                P.mark(f"y1v{i}")
                P.wait("sync", "dve")
                P.sync.dma_start(out=y1_dram[r0:r0 + 128, :], in_=fb[:], inc=[("ste", 16)])
                P.sync.dma_start(out=dbg_y1[r0:r0 + 128, :], in_=fb[:], inc=[("dbg", 16)])
                dve_edge()
                P.vector.tensor_reduce(stat[:, 0:1], fb[:], AX.X, ALU.add, inc=[("dve", 1)])
                P.wait("scalar", "dve")
                P.wait("scalar", "act")
                P.scalar.activation(sqb[:], fb[:], AF.Square, inc=[("act", 1)])
                P.wait("vector", "act")
                P.vector.tensor_reduce(stat[:, 1:2], sqb[:], AX.X, ALU.add, inc=[("dve", 1)])
                dve_edge()
                P.vector.tensor_scalar(stat[:, 2:3], stat[:, 0:1], 1.0 / D, None, ALU.mult,
                                       inc=[("dve", 1)])
                dve_edge()
                P.vector.tensor_scalar(stat[:, 3:4], stat[:, 1:2], 1.0 / D, None, ALU.mult,
                                       inc=[("dve", 1)])
                dve_edge()
                P.vector.tensor_tensor(stat[:, 4:5], stat[:, 2:3], stat[:, 2:3], ALU.mult,
                                       inc=[("dve", 1)])
                dve_edge()
                P.vector.tensor_tensor(stat[:, 5:6], stat[:, 3:4], stat[:, 4:5],
                                       ALU.subtract, inc=[("dve", 1)])
                P.wait("scalar", "dve")
                P.scalar.activation(stat[:, 6:7], stat[:, 5:6], AF.Ln, bias=scal[:, 8:9],
                                    inc=[("act", 1)])
                P.wait("scalar", "act")
                P.scalar.activation(stat[:, 7:8], stat[:, 6:7], AF.Exp, scale=-0.5,
                                    inc=[("act", 1)])
                P.wait("vector", "act")
                P.vector.tensor_scalar(gen_tok[:], fb[:], stat[:, 2:3], stat[:, 7:8],
                                       ALU.subtract, op1=ALU.mult, inc=[("dve", 1)])
                dve_edge()
                P.vector.tensor_scalar(sqb[:], fb[:], stat[:, 2:3], stat[:, 7:8],
                                       ALU.subtract, op1=ALU.mult, inc=[("dve", 1)])
                dve_edge()
                P.vector.tensor_tensor(rd_t[0][:], sqb[:], gen_tok[:], ALU.subtract,
                                       inc=[("dve", 1)])
                P.wait("sync", "dve")
                P.sync.dma_start(out=z2aug_bi[r0:r0 + 128, 0:D], in_=gen_tok[:],
                                 inc=[("ste", 16)])
                P.sync.dma_start(out=z2lo_dram[r0:r0 + 128, :], in_=rd_t[0][:],
                                 inc=[("ste", 16)])
                P.mark(f"eg{i}")
            if PHASE == 1:
                P.gpsimd.dma_start(out=dbg_z[:], in_=z_t_dram[:], inc=[("dbg2", 16)])
                P.gpsimd.dma_start(out=dbg_reads[:], in_=reads_dram[:], inc=[("dbg2", 16)])
                P.gpsimd.dma_start(out=dbg_wst[:], in_=wst_dram[:], inc=[("dbg2", 16)])
                P.gpsimd.dma_start(out=dbg_aseg[:], in_=aseg_bi[:], inc=[("dbg2", 16)])
                P.wait("sync", "ste")
                P.wait("sync", "str")
                P.wait("sync", "stz")
                P.wait("sync", "stw")
                P.wait("gpsimd", "gp")
                P.wait("gpsimd", "ld")
                P.wait("gpsimd", "lt")
                P.wait("gpsimd", "lx")
                P.wait("gpsimd", "lr")
                P.wait("gpsimd", "lw")
                P.wait("gpsimd", "gs")
                P.wait("gpsimd", "gz")
                P.wait("gpsimd", "gz1")
                P.wait("gpsimd", "dbg")
                P.wait("gpsimd", "dbg2")
                P.wait("scalar", "act")
                P.wait("vector", "dve")
                P.wait("tensor", "pe")
                return


            P.wait("sync", "ste")
            P.wmark("sync", "eg15", "dve", "pe")
            for hf in range(2):
                for dl in range(8):
                    P.sync.dma_start(out=z_f[dl][:, 1024 * hf:1024 * (hf + 1)],
                                     in_=z2aug_bi[1024 * hf:1024 * (hf + 1),
                                                  128 * dl:128 * (dl + 1)],
                                     transpose=True, inc=[("lt", 16)])
                    P.sync.dma_start(out=z_fs[dl][:, 1024 * hf:1024 * (hf + 1)],
                                     in_=z2lo_dram[1024 * hf:1024 * (hf + 1),
                                                   128 * dl:128 * (dl + 1)],
                                     transpose=True, inc=[("lt", 16)])
            P.wait("tensor", "lt")
            for hf in range(2):
                if hf == 1:
                    P.wmark("tensor", "lgev0", "act")
                combos = [(w8hi, z_f), (w8hi, z_fs), (w8lo, z_f)]
                for ci, (wmat, zmat) in enumerate(combos):
                    for kt in range(8):
                        for nh in range(2):
                            P.tensor.matmul(ps2[0:8, 512 * nh:512 * (nh + 1)],
                                            wmat[:, 8 * kt:8 * (kt + 1)],
                                            zmat[kt][:, 1024 * hf + 512 * nh:
                                                      1024 * hf + 512 * (nh + 1)],
                                            start=(ci == 0 and kt == 0),
                                            stop=(ci == 2 and kt == 7),
                                            inc=[("pe", 1)] if (ci == 2 and kt == 7
                                                                and nh == 1) else ())
                P.wait("scalar", "pe")
                P.scalar.copy(logits_sb[0:8, 1024 * hf:1024 * (hf + 1)], ps2[0:8, 0:1024],
                              inc=[("act", 1)])
                P.mark(f"lgev{hf}")
            P.wait("sync", "act")
            P.sync.dma_start(out=logit_bi[:], in_=logits_sb[:], inc=[("ste", 16)])
            P.wait("tensor", "act")
            for i in range(NT):
                psl = ps0 if i % 2 == 0 else ps1
                if i >= 2:
                    P.wmark("tensor", f"ltok{i - 2}", "dve")
                P.tensor.matmul(psl[:, 0:8], logits_sb[0:8, 128 * i:128 * (i + 1)],
                                ident8[:], start=True, stop=True, inc=[("pe", 1)])
                P.wait("vector", "pe")
                P.wait("vector", "ste")
                P.vector.tensor_copy(logit_tok[:, 8 * i:8 * (i + 1)], psl[:, 0:8],
                                     inc=[("dve", 1)])
                dve_edge()
                P.vector.tensor_copy(gen_tok[:, 0:8], psl[:, 0:8], inc=[("dve", 1)])
                P.wait("sync", "dve")
                P.sync.dma_start(out=z2aug_bi[128 * i:128 * (i + 1), 1024:1032],
                                 in_=gen_tok[:, 0:8], inc=[("ste", 16)])
                P.mark(f"ltok{i}")
            P.wait("gpsimd", "ste")
            P.gpsimd.collective_compute("AllGather", ALU.bypass, replica_groups=RG,
                                        ins=[logit_bi.ap().opt()], outs=[logit_bo.ap().opt()],
                                        inc=[("cc", 1)])
            P.gpsimd.collective_compute("AllGather", ALU.bypass, replica_groups=RG,
                                        ins=[z2aug_bi.ap().opt()], outs=[z2aug_bo.ap().opt()],
                                        inc=[("cc", 1)])

            P.wait("sync", "cc", 2)
            for inst in range(8):
                for w in range(4):
                    P.sync.dma_start(
                        out=bis[16 * inst + 4 * w:16 * inst + 4 * w + 4, :],
                        in_=logit_bo[8 * w + inst:8 * w + inst + 1, :].rearrange(
                            "1 (a f) -> a f", a=4),
                        inc=[("ld", 16)])
            P.wait("tensor", "ld")
            P.tensor.matmul(ps2[:, 384:512], sel16t[:], sel16t[:], start=True, stop=True,
                            inc=[("pe", 1)])
            P.wait("vector", "pe")
            P.wait("vector", "dbg")
            P.wait("vector", "ste")
            P.vector.tensor_copy(blk16[:], ps2[:, 384:512], inc=[("dve", 1)])
            P.wait("vector", "ld")
            P.vector.memset(bw[:, 0:1], -40.0, inc=[("dve", 1)])
            P.vector.memset(bw[:, 1:2], 40.0, inc=[("dve", 1)])
            P.vector.memset(bw[:, 2:3], 0.0, inc=[("dve", 1)])
            LO, HI, TAU = bw[:, 0:1], bw[:, 1:2], bw[:, 2:3]
            for it in range(BIS_IT):
                dve_edge()
                P.vector.tensor_scalar(f1[:, 0:512], bis[:], TAU, None, ALU.is_ge,
                                       inc=[("dve", 1)])
                dve_edge()
                P.vector.tensor_reduce(bw[:, 3:4], f1[:, 0:512], AX.X, ALU.add,
                                       inc=[("dve", 1)])
                P.wait("tensor", "dve")
                P.tensor.matmul(ps2[:, 2:3], blk16[:], bw[:, 3:4], start=True, stop=True,
                                inc=[("pe", 1)])
                P.wait("vector", "pe")
                P.vector.tensor_copy(bw[:, 5:6], ps2[:, 2:3], inc=[("dve", 1)])
                dve_edge()
                P.vector.tensor_tensor(bw[:, 6:7], bw[:, 5:6], tgt[:], ALU.is_ge,
                                       inc=[("dve", 1)])
                dve_edge()
                P.vector.tensor_tensor(bw[:, 7:8], TAU, LO, ALU.subtract, inc=[("dve", 1)])
                dve_edge()
                P.vector.scalar_tensor_tensor(LO, bw[:, 7:8], bw[:, 6:7], LO,
                                              ALU.mult, ALU.add, inc=[("dve", 1)])
                dve_edge()
                P.vector.tensor_tensor(bw[:, 9:10], HI, TAU, ALU.subtract, inc=[("dve", 1)])
                dve_edge()
                P.vector.scalar_tensor_tensor(HI, bw[:, 9:10], bw[:, 6:7], TAU,
                                              ALU.mult, ALU.add, inc=[("dve", 1)])
                dve_edge()
                P.vector.tensor_scalar(TAU, LO, HI, 0.5, ALU.add, op1=ALU.mult,
                                       inc=[("dve", 1)])
            dve_edge()
            P.vector.tensor_scalar(f1[:, 0:512], bis[:], LO, None, ALU.is_ge,
                                   inc=[("dve", 1)])
            dve_edge()
            P.vector.tensor_scalar(f1[:, 512:1024], bis[:], HI, None, ALU.is_ge,
                                   inc=[("dve", 1)])
            dve_edge()
            P.vector.tensor_tensor(f1[:, 0:512], f1[:, 0:512], f1[:, 512:1024],
                                   ALU.subtract, inc=[("dve", 1)])
            dve_edge()
            P.vector.tensor_scalar(f1[:, 512:1024], f1[:, 0:512], 1e30, -1e30, ALU.mult,
                                   op1=ALU.add, inc=[("dve", 1)])
            dve_edge()
            P.vector.tensor_tensor(f1[:, 0:512], f1[:, 0:512], bis[:], ALU.mult,
                                   inc=[("dve", 1)])
            dve_edge()
            P.vector.tensor_tensor(f1[:, 0:512], f1[:, 0:512], f1[:, 512:1024],
                                   ALU.add, inc=[("dve", 1)])
            dve_edge()
            P.vector.tensor_reduce(bw[:, 4:5], f1[:, 0:512], AX.X, ALU.max,
                                   inc=[("dve", 1)])
            dve_edge()
            P.vector.tensor_copy(f1[:, 640:768], iblk128[:, 0:128], inc=[("dve", 1)])
            dve_edge()
            P.vector.tensor_scalar(f1[:, 512:640], f1[:, 640:768], iota_c[:], None,
                                   ALU.is_equal, inc=[("dve", 1)])
            P.wait("tensor", "dve")
            P.tensor.matmul(ps2[0:1, 128:256], bw[:, 4:5], f1[:, 512:640], start=True,
                            stop=True, inc=[("pe", 1)])
            P.wait("vector", "pe")
            for si in range(8):
                dve_edge()
                P.vector.tensor_reduce(snapw[0:1, si:si + 1],
                                       ps2[0:1, 128 + 16 * si:144 + 16 * si], AX.X,
                                       ALU.max, inc=[("dve", 1)])
            P.wait("tensor", "dve")
            P.tensor.matmul(ps2[0:8, 256:257], snapw[0:1, 0:8], onescf[0:1, :],
                            start=True, stop=True, inc=[("pe", 1)])
            P.wait("vector", "pe")
            P.vector.tensor_copy(bw[0:8, 12:13], ps2[0:8, 256:257], inc=[("dve", 1)])
            P.wait("tensor", "dve")
            P.tensor.matmul(ps2[:, 2:3], sel16t[:], bw[0:8, 12:13], start=True, stop=True,
                            inc=[("pe", 1)])
            P.wait("vector", "pe")
            P.vector.tensor_copy(TAU, ps2[:, 2:3], inc=[("dve", 1)])
            dve_edge()
            P.vector.tensor_scalar(f1[:, 0:512], bis[:], TAU, None, ALU.is_ge,
                                   inc=[("dve", 1)])
            dve_edge()
            P.vector.tensor_reduce(bw[:, 3:4], f1[:, 0:512], AX.X, ALU.add,
                                   inc=[("dve", 1)])
            P.wait("tensor", "dve")
            P.tensor.matmul(ps2[0:32, 4:5], selw32[:], bw[:, 3:4], start=True, stop=True,
                            inc=[("pe", 1)])
            P.wait("vector", "pe")
            P.vector.tensor_copy(cnt32[:], ps2[0:32, 4:5], inc=[("dve", 1)])
            P.wait("tensor", "dve")
            P.tensor.matmul(ps2[0:32, 5:6], prefixw[:], cnt32[:], start=True, stop=True,
                            inc=[("pe", 1)])
            P.wait("vector", "pe")
            P.vector.tensor_copy(bw[0:32, 10:11], ps2[0:32, 5:6], inc=[("dve", 1)])
            dve_edge()
            P.vector.tensor_tensor(bw[0:32, 11:12], nq32[:], bw[0:32, 10:11],
                                   ALU.subtract, inc=[("dve", 1)])
            dve_edge()
            P.vector.tensor_scalar(bw[0:32, 11:12], bw[0:32, 11:12], 0.0, None, ALU.max,
                                   inc=[("dve", 1)])
            dve_edge()
            P.vector.tensor_tensor(cnt32[:], cnt32[:], bw[0:32, 11:12], ALU.min,
                                   inc=[("dve", 1)])
            P.wait("tensor", "dve")
            P.tensor.matmul(ps2[0:1, 8:16], bw[0:8, 12:13], ident8[:], start=True,
                            stop=True, inc=[("pe", 1)])
            P.wait("scalar", "pe")
            P.scalar.copy(rowb[0:1, 0:8], ps2[0:1, 8:16], inc=[("act", 1)])
            P.wait("tensor", "act")
            P.tensor.matmul(ps2[:, 8:16], onesr[:], rowb[0:1, 0:8], start=True, stop=True,
                            inc=[("pe", 1)])
            P.wait("vector", "pe")
            P.vector.tensor_copy(taub[:], ps2[:, 8:16], inc=[("dve", 1)])
            P.wait("tensor", "dve")
            P.tensor.matmul(ps2[0:1, 16:24], cnt32[:], prevw[:], start=True, stop=True,
                            inc=[("pe", 1)])
            P.wait("scalar", "pe")
            P.scalar.copy(offrow[:], ps2[0:1, 16:24], inc=[("act", 1)])
            P.wait("tensor", "act")
            P.tensor.matmul(ps2[:, 16:24], onesr[:], offrow[:], start=True, stop=True,
                            inc=[("pe", 1)])
            P.wait("vector", "pe")
            P.vector.tensor_copy(offib[:], ps2[:, 16:24], inc=[("dve", 1)])
            P.wait("tensor", "dve")
            P.tensor.matmul(ps2[0:20, 24:25], cntsel[:], cnt32[:], start=True, stop=True,
                            inc=[("pe", 1)])
            P.wait("scalar", "pe")
            P.scalar.copy(stat[0:20, 12:13], ps2[0:20, 24:25], inc=[("act", 1)])
            P.wait("tensor", "act")
            P.tensor.matmul(ps2[0:1, 32:52], stat[0:20, 12:13], ident32[0:20, 0:20],
                            start=True, stop=True, inc=[("pe", 1)])
            P.wait("scalar", "pe")
            P.scalar.copy(rowb[0:1, 8:28], ps2[0:1, 32:52], inc=[("act", 1)])
            P.wait("tensor", "act")
            P.tensor.matmul(ps2[:, 32:52], onesr[:], rowb[0:1, 8:28], start=True, stop=True,
                            inc=[("pe", 1)])
            P.wait("vector", "pe")
            P.vector.tensor_copy(cnt20b[:], ps2[:, 32:52], inc=[("dve", 1)])
            dve_edge()
            P.vector.tensor_scalar(stat[:, 13:14], iota_c[:], kbase20[:, 0:1], None,
                                   ALU.add, inc=[("dve", 1)]) if False else None
            P.vector.tensor_tensor(valid20[:], kbase20[:], cnt20b[:], ALU.is_lt,
                                   inc=[("dve", 1)]) if False else None
            P.vector.tensor_scalar(valid20[:], kbase20[:], iota_c[:], None, ALU.add,
                                   inc=[("dve", 1)])
            dve_edge()
            P.vector.tensor_tensor(valid20[:], valid20[:], cnt20b[:], ALU.is_lt,
                                   inc=[("dve", 1)])
            P.mark("bisdone")

            P.wmark("vector", "bisdone", "dve")
            for i in range(NT):
                dve_edge()
                P.vector.tensor_tensor(pose[:, 0:8], logit_tok[:, 8 * i:8 * (i + 1)],
                                       taub[:], ALU.is_ge, inc=[("dve", 1)])
                psl = ps0 if i % 2 == 0 else ps1
                P.wait("tensor", "dve")
                P.wmark("tensor", "bisdone", "act")
                P.tensor.matmul(psl[:, 0:8], tris[:], pose[:, 0:8], start=True, stop=False)
                P.tensor.matmul(psl[:, 0:8], onesr[:], offrow[:], start=False, stop=True,
                                inc=[("pe", 1)])
                P.tensor.matmul(psl[0:1, 8:16], onescf[:], pose[:, 0:8], start=True,
                                stop=True, inc=[("pe", 1)])
                P.wait("vector", "pe")
                P.vector.tensor_copy(f1[:, 0:8], psl[:, 0:8], inc=[("dve", 1)])
                P.wait("scalar", "pe")
                P.scalar.copy(rowb[0:1, 32:40], psl[0:1, 8:16], inc=[("act", 1)])
                P.wait("vector", "act")
                dve_edge()
                P.vector.tensor_tensor(offrow[:], offrow[:], rowb[0:1, 32:40], ALU.add,
                                       inc=[("dve", 1)])
                dve_edge()
                P.vector.tensor_tensor(f1[:, 8:16], f1[:, 0:8], nqcap[:], ALU.is_lt,
                                       inc=[("dve", 1)])
                dve_edge()
                P.vector.tensor_tensor(selr[:, 16 * i:16 * i + 8], pose[:, 0:8],
                                       f1[:, 8:16], ALU.mult, inc=[("dve", 1)])
                dve_edge()
                P.vector.tensor_tensor(selr[:, 16 * i + 8:16 * i + 16], f1[:, 0:8],
                                       offib[:], ALU.subtract, inc=[("dve", 1)])
                P.mark(f"sel{i}")

            for i in range(NT):
                if i >= 1:
                    P.wmark("vector", f"pos{i - 1}", "pe")
                dve_edge()
                P.vector.tensor_scalar(pose[:, 24:32], selr[:, 16 * i + 8:16 * i + 16],
                                       128.0, None, ALU.is_ge, inc=[("dve", 1)])
                dve_edge()
                P.vector.tensor_scalar(pose[:, 16:24], selr[:, 16 * i + 8:16 * i + 16],
                                       256.0, None, ALU.is_ge, inc=[("dve", 1)])
                dve_edge()
                P.vector.tensor_tensor(pose[:, 24:32], pose[:, 24:32], pose[:, 16:24],
                                       ALU.add, inc=[("dve", 1)])
                dve_edge()
                P.vector.tensor_scalar(pose[:, 16:24], selr[:, 16 * i + 8:16 * i + 16],
                                       384.0, None, ALU.is_ge, inc=[("dve", 1)])
                dve_edge()
                P.vector.tensor_tensor(pose[:, 24:32], pose[:, 24:32], pose[:, 16:24],
                                       ALU.add, inc=[("dve", 1)])
                dve_edge()
                P.vector.scalar_tensor_tensor(pose[:, 8:16], pose[:, 24:32], -128.0,
                                              selr[:, 16 * i + 8:16 * i + 16],
                                              ALU.mult, ALU.add, inc=[("dve", 1)])
                dve_edge()
                P.vector.tensor_tensor(
                    ehi[:], iblk4[:],
                    pose.ap()[:, 24:32, None].broadcast_to([128, 8, 4]),
                    ALU.is_equal, inc=[("dve", 1)])
                dve_edge()
                P.vector.tensor_tensor(
                    ehi[:], ehi[:],
                    selr.ap()[:, 16 * i:16 * i + 8, None].broadcast_to([128, 8, 4]),
                    ALU.mult, inc=[("dve", 1)])
                dve_edge()
                P.vector.tensor_scalar(pose[:, 32:33], iota_c[:], float(128 * i), WOFFB,
                                       ALU.add, op1=ALU.add, inc=[("dve", 1)])
                dve_edge()
                P.vector.tensor_scalar(ehi[:], ehi[:], pose[:, 32:33], None, ALU.mult,
                                       inc=[("dve", 1)])
                dve_edge()
                P.vector.tensor_tensor(
                    elo[:], iblk128[:],
                    pose.ap()[:, 8:16, None].broadcast_to([128, 8, 128]),
                    ALU.is_equal, inc=[("dve", 1)])
                P.wait("tensor", "dve")
                P.tensor.matmul(ps2[0:32, 0:512], ehi[:], elo[:, 0:512],
                                start=(i == 0), stop=(i == NT - 1),
                                inc=[("pe", 1)] if i == NT - 1 else ())
                P.tensor.matmul(ps2[0:32, 512:1024], ehi[:], elo[:, 512:1024],
                                start=(i == 0), stop=(i == NT - 1),
                                inc=[("pe", 2)] if i == NT - 1 else [("pe", 1)])
                P.mark(f"pos{i}")
            P.wait("vector", "pe")
            P.vector.tensor_copy(possb[:], ps2[0:32, 0:1024], inc=[("dve", 1)])
            P.wait("sync", "dve")
            for inst in range(8):
                P.sync.dma_start(
                    out=pos_bi[inst:inst + 1, :].rearrange("1 (h l) -> h l", h=4),
                    in_=possb[4 * inst:4 * inst + 4, 128 * inst:128 * (inst + 1)],
                    inc=[("stw", 16)])
            P.wait("gpsimd", "stw")
            P.gpsimd.collective_compute("AllGather", ALU.bypass, replica_groups=RG,
                                        ins=[pos_bi.ap().opt()], outs=[pos_bo.ap().opt()],
                                        inc=[("cc", 1)])
            P.mark("posdone")

            P.wait("sync", "cc", 4)
            P.sync.dma_start(out=possb[0:32, 0:512], in_=pos_bo[:], inc=[("ld", 16)])
            P.wait("tensor", "ld")
            for j in range(20):
                k = [0, 1, 2] * 4 + [0, 1] * 4
                psl = ps0 if j % 2 == 0 else ps1
                if j >= 2:
                    P.wmark("tensor", f"idx{j - 2}", "dve")
                P.tensor.matmul(psl[0:1, 0:128], possel[:, j:j + 1],
                                possb[0:32, 128 * k[j]:128 * (k[j] + 1)],
                                start=True, stop=True, inc=[("pe", 1)])
                P.wait("scalar", "pe")
                P.scalar.copy(rowb[0:1, 0:0].ap() if False else stat[0:1, 14:15],
                              psl[0:1, 0:1], inc=[("act", 1)]) if False else None
                P.scalar.copy(logits_sb[0:1, 128 * (j % 8):128 * (j % 8) + 128],
                              psl[0:1, 0:128], inc=[("act", 1)])
                P.wait("tensor", "act")
                P.tensor.matmul(psl[:, 128:129],
                                logits_sb[0:1, 128 * (j % 8):128 * (j % 8) + 128],
                                onescf[0:1, 0:1], start=True, stop=True, inc=[("pe", 1)])
                P.wait("vector", "pe")
                P.vector.tensor_copy(idx_f[:, j:j + 1], psl[:, 128:129], inc=[("dve", 1)])
                P.mark(f"idx{j}")
            dve_edge()
            P.vector.tensor_tensor(f1[:, 0:20], idx_f[:, 0:20], valid20[:], ALU.mult,
                                   inc=[("dve", 1)])
            dve_edge()
            P.vector.tensor_scalar(f1[:, 20:40], valid20[:], 1.0, -BIGIDX, ALU.subtract,
                                   op1=ALU.mult, inc=[("dve", 1)])
            dve_edge()
            P.vector.tensor_tensor(f1[:, 40:60], f1[:, 0:20], f1[:, 20:40], ALU.add,
                                   inc=[("dve", 1)])
            dve_edge()
            P.vector.tensor_copy(idxq_i[:, 0:20], f1[:, 40:60], inc=[("dve", 1)])
            P.mark("idxdone")
            gbuf = [a2(1032 * j, AUGW) for j in range(20)]
            P.wmark("gpsimd", "eg15", "pe")
            P.wmark("gpsimd", "m2done", "pe")
            for j in range(20):
                P.gpsimd.memset(gbuf[j][:], 0.0, inc=[("pool", 1)])
            P.wmark("gpsimd", "idxdone", "dve")
            P.wait("gpsimd", "pool")
            for j in range(20):
                gqs = ["gq", "gqb", "gqc"][j % 3]
                P.gpsimd.indirect_dma_start(
                    out=gbuf[j][:], out_offset=None,
                    in_=z2aug_bo[:],
                    in_offset=IndirectOffsetOnAxis(idxq_i[:, j:j + 1], 0),
                    bounds_check=4 * WIN - 1, oob_is_err=False, inc=[(gqs, 16)])
            for j in range(20):
                gqs = ["gq", "gqb", "gqc"][j % 3]
                buf = gbuf[j]
                P.wait("vector", gqs)
                dve_edge()
                if j >= 3:
                    P.wait("vector", "ste")
                gsel = gselq if j < 12 else gselk
                P.vector.tensor_tensor(f1[:, 60 + 8 * (j % 4):68 + 8 * (j % 4)],
                                       buf[:, 1024:1032], gsel[:], ALU.mult,
                                       inc=[("dve", 1)])
                dve_edge()
                if j < 12:
                    P.vector.tensor_reduce(stat[:, 13 + j % 3:14 + j % 3],
                                           f1[:, 60 + 8 * (j % 4):68 + 8 * (j % 4)],
                                           AX.X, ALU.add, inc=[("dve", 1)])
                else:
                    P.vector.tensor_reduce(sig_kv[:, j - 12:j - 11],
                                           f1[:, 60 + 8 * (j % 4):68 + 8 * (j % 4)],
                                           AX.X, ALU.add, inc=[("dve", 1)])
                P.wait("scalar", "dve")
                P.wait("scalar", gqs)
                if j < 12:
                    wq_, kq_ = j // 3, j % 3
                    nrows = 96 if kq_ == 2 else 128
                    r0q = QCAP * wq_ + 128 * kq_
                    P.scalar.dma_start(out=qcomp_dram[r0q:r0q + nrows, :],
                                       in_=buf[0:nrows, 0:1024], inc=[("ste", 16)])
                    P.scalar.dma_start(out=qlog_dram[r0q:r0q + nrows, :],
                                       in_=stat[0:nrows, 13 + j % 3:14 + j % 3],
                                       inc=[("ste", 16)])
                else:
                    wk_, kk_ = (j - 12) // 2, (j - 12) % 2
                    r0k = 256 * wk_ + 128 * kk_
                    P.scalar.dma_start(out=kvcomp_dram[r0k:r0k + 128, :],
                                       in_=buf[:, 0:1024], inc=[("ste", 16)])
                P.mark(f"gst{j}")
            P.wait("sync", "ste")
            for i in range(8):
                P.wait("sync", "dve")
                P.sync.dma_start(out=xs[0][:, 0:DI], in_=wqT[128 * i:128 * (i + 1), :],
                                 inc=[("ld", 16)])
                P.wait("vector", "ld")
                P.wait("vector", "ste")
                P.vector.tensor_copy(wq_b[i][:], xs[0][:, 0:DI], inc=[("dve", 1)])
                P.wait("sync", "dve")
                P.sync.dma_start(out=xs[0][:], in_=wkvT[128 * i:128 * (i + 1), :],
                                 inc=[("ld", 16)])
                P.wait("vector", "ld")
                P.vector.tensor_copy(wkv_b[i][:], xs[0][:], inc=[("dve", 1)])
            for i in range(4):
                P.wait("sync", "dve")
                P.sync.dma_start(out=xs[0][:], in_=woutT[128 * i:128 * (i + 1), :],
                                 inc=[("ld", 16)])
                P.wait("vector", "ld")
                P.vector.tensor_copy(wout_b[i][:], xs[0][:], inc=[("dve", 1)])
            P.wait("sync", "dve")
            P.sync.dma_start(out=xs[0][:, 0:4], in_=nk_in[:], inc=[("ld", 16)])
            P.sync.dma_start(out=xs[1][0:1, 0:8 * 65], in_=nv_in[:], inc=[("ld", 16)])
            P.wait("vector", "ld")
            P.vector.tensor_copy(nkb[:], xs[0][:, 0:4], inc=[("dve", 1)])
            dve_edge()
            P.vector.tensor_copy(nvb[:], xs[1][0:1, 0:8 * 65], inc=[("dve", 1)])
            P.mark("fwts")

            dve_edge()
            P.vector.tensor_scalar(sig_kv[:], sig_kv[:], -1.0, None, ALU.mult,
                                   inc=[("dve", 1)])
            P.wait("scalar", "dve")
            P.scalar.activation(sig_kv[:], sig_kv[:], AF.Exp, inc=[("act", 1)])
            P.wait("vector", "act")
            P.vector.tensor_scalar(sig_kv[:], sig_kv[:], 1.0, None, ALU.add,
                                   inc=[("dve", 1)])
            dve_edge()
            P.vector.reciprocal(sig_kv[:], sig_kv[:], inc=[("dve", 1)])
            dve_edge()
            P.vector.tensor_scalar(kbias[:], valid20[:, 12:20], 1.0, 1e9, ALU.subtract,
                                   op1=ALU.mult, inc=[("dve", 1)])
            dve_edge()
            P.vector.tensor_scalar(kbias[:], kbias[:], SHIFT, None, ALU.add,
                                   inc=[("dve", 1)])
            P.mark("gdone")

            P.wmark("sync", "gdone", "ste")
            for dl in range(8):
                P.sync.dma_start(out=z_f[dl][:, 0:QS],
                                 in_=qcomp_dram[:, 128 * dl:128 * (dl + 1)],
                                 transpose=True, inc=[("lr", 16)])
            P.wait("tensor", "lr")
            P.wmark("tensor", "fwts", "dve")
            for et in range(4):
                for nch in range(3):
                    n0, n1 = 512 * nch, min(512 * (nch + 1), QS)
                    psl = ps0 if (et * 3 + nch) % 2 == 0 else ps1
                    if et * 3 + nch >= 2:
                        P.wmark("tensor", f"qev{et * 3 + nch - 2}", "dve")
                    for kt in range(8):
                        P.tensor.matmul(psl[:, 0:n1 - n0],
                                        wq_b[kt][:, 128 * et:128 * (et + 1)],
                                        z_f[kt][:, n0:n1],
                                        start=(kt == 0), stop=(kt == 7),
                                        inc=[("pe", 1)] if kt == 7 else ())
                    P.wait("vector", "pe")
                    P.vector.tensor_copy(qh[et][:, n0:n1], psl[:, 0:n1 - n0],
                                         inc=[("dve", 1)])
                    P.mark(f"qev{et * 3 + nch}")
            for wv in range(2):
                for t in range(4):
                    P.wait("sync", "dve") if wv == 1 else None
                    P.sync.dma_start(out=kvf[t][:],
                                     in_=kvcomp_dram[:, 128 * (4 * wv + t):
                                                     128 * (4 * wv + t + 1)],
                                     transpose=True, inc=[("lw", 16)])
                P.wait("tensor", "lw")
                if wv == 0:
                    P.wmark("tensor", "qev11", "dve")
                for et in range(4):
                    for nch in range(2):
                        idx2 = et * 2 + nch
                        psl = ps0 if idx2 % 2 == 0 else ps1
                        if wv == 1 or idx2 >= 2:
                            P.wait("tensor", "dve")
                        for kt in range(4):
                            P.tensor.matmul(psl[:, 0:512],
                                            wkv_b[4 * wv + kt][:, 128 * et:128 * (et + 1)],
                                            kvf[kt][:, 512 * nch:512 * (nch + 1)],
                                            start=(kt == 0), stop=(kt == 3),
                                            inc=[("pe", 1)] if kt == 3 else ())
                        P.wait("vector", "pe")
                        if wv == 0:
                            P.vector.tensor_copy(kf[et][:, 512 * nch:512 * (nch + 1)],
                                                 psl[:, 0:512], inc=[("dve", 1)])
                        else:
                            P.vector.tensor_tensor(kf[et][:, 512 * nch:512 * (nch + 1)],
                                                   kf[et][:, 512 * nch:512 * (nch + 1)],
                                                   psl[:, 0:512], ALU.add, inc=[("dve", 1)])
                for st in range(8):
                    psl = ps2 if st % 2 == 0 else ps3
                    if wv == 1 or st >= 2:
                        P.wait("tensor", "dve")
                    for kt in range(4):
                        P.tensor.matmul(
                            psl[:, 0:512],
                            kvf[kt][:, 128 * st:128 * (st + 1)],
                            wkv_b[4 * wv + kt][:, DI:2 * DI],
                            start=(kt == 0), stop=(kt == 3),
                            inc=[("pe", 1)] if kt == 3 else ())
                    P.wait("vector", "pe")
                    vview = vtok[st].rearrange("p (h e) -> p h e", h=8)
                    pview = psl.ap()[:, 0:512].rearrange("p (h e) -> p h e", h=8)
                    if wv == 0:
                        P.vector.tensor_copy(vview[:, :, 0:64], pview, inc=[("dve", 1)])
                    else:
                        P.vector.tensor_tensor(vview[:, :, 0:64], vview[:, :, 0:64], pview,
                                               ALU.add, inc=[("dve", 1)])
                        dve_edge()
                        P.vector.tensor_scalar(vview[:, :, 0:64], vview[:, :, 0:64],
                                               sig_kv[:, st:st + 1], None, ALU.mult,
                                               inc=[("dve", 1)])
                        dve_edge()
                        P.vector.memset(vview[:, :, 64:65], 1.0, inc=[("dve", 1)])
            P.wait("vector", "act")
            for et in range(4):
                dve_edge()
                P.vector.tensor_copy(kf[et][:, KS:KS + 1], nkb[:, et:et + 1],
                                     inc=[("dve", 1)])
            P.mark("projs")

            for h in range(8):
                et, oo = h // 2, 64 * (h % 2)
                slots6 = [ps0.ap()[:, 0:512], ps0.ap()[:, 512:1024],
                          ps1.ap()[:, 0:512], ps1.ap()[:, 512:1024],
                          ps2.ap()[:, 0:512], ps3.ap()[:, 512:1024]]
                for jt in range(8):
                    for nch in range(3):
                        n0 = 512 * nch
                        n1 = min(512 * (nch + 1), QS)
                        pss = slots6[(3 * jt + nch) % 6]
                        if jt == 1 and nch == 0:
                            if h > 0:
                                P.wmark("tensor", f"oev{h - 1}_2", "dve", "act")
                            else:
                                P.wait("tensor", "dve")
                                P.wait("tensor", "act")
                        if jt >= 2:
                            P.wmark("tensor", f"exp{h}_{jt - 2}_{nch}", "act")
                        elif h > 0:
                            P.wmark("tensor", f"exp{h - 1}_{jt + 6}_{nch}", "act")
                        P.wmark("tensor", "projs", "dve") if (h == 0 and jt == 0
                                                              and nch == 0) else None
                        P.tensor.matmul(pss[:, 0:n1 - n0],
                                        kf[et][64 * (h % 2):64 * (h % 2) + 64,
                                               128 * jt:128 * (jt + 1)],
                                        qh[et][64 * (h % 2):64 * (h % 2) + 64, n0:n1],
                                        start=True, stop=True, inc=[("pe", 1)])
                        P.wait("scalar", "pe")
                        P.scalar.activation(expT[jt][:, n0:n1], pss[:, 0:n1 - n0],
                                            AF.Exp, scale=0.125,
                                            bias=kbias[:, jt:jt + 1], inc=[("act", 1)])
                        P.mark(f"exp{h}_{jt}_{nch}")
                for nch in range(3):
                    n0 = 512 * nch
                    n1 = min(512 * (nch + 1), QS)
                    pss = slots6[nch]
                    P.wmark("tensor", f"exp{h}_6_{nch}", "act")
                    P.tensor.matmul(pss[0:1, 0:n1 - n0],
                                    kf[et][64 * (h % 2):64 * (h % 2) + 64, KS:KS + 1],
                                    qh[et][64 * (h % 2):64 * (h % 2) + 64, n0:n1],
                                    start=True, stop=True, inc=[("pe", 1)])
                    P.wait("scalar", "pe")
                    P.scalar.activation(expN[0:1, n0:n1], pss[0:1, 0:n1 - n0], AF.Exp,
                                        scale=0.125, bias=scal[0:1, 10:11],
                                        inc=[("act", 1)])
                    P.mark(f"expN{h}_{nch}")
                P.wait("tensor", "act")
                P.wait("tensor", "dve")
                for jt in range(8):
                    for nch in range(3):
                        n0 = 512 * nch
                        n1 = min(512 * (nch + 1), QS)
                        psO = [ps2.ap()[0:65, 0:512], ps2.ap()[0:65, 512:1024],
                               ps3.ap()[0:65, 0:512]][nch]
                        P.tensor.matmul(psO[:, 0:n1 - n0],
                                        vtok[jt][:, 65 * h:65 * (h + 1)],
                                        expT[jt][:, n0:n1],
                                        start=(jt == 0), stop=False)
                for nch in range(3):
                    n0 = 512 * nch
                    n1 = min(512 * (nch + 1), QS)
                    psO = [ps2.ap()[0:65, 0:512], ps2.ap()[0:65, 512:1024],
                           ps3.ap()[0:65, 0:512]][nch]
                    P.tensor.matmul(psO[:, 0:n1 - n0], nvb[0:1, 65 * h:65 * (h + 1)],
                                    expN[0:1, n0:n1], start=False, stop=True,
                                    inc=[("pe", 1)])
                P.wait("scalar", "pe")
                for nch in range(3):
                    n0 = 512 * nch
                    n1 = min(512 * (nch + 1), QS)
                    psO = [ps2.ap()[0:65, 0:512], ps2.ap()[0:65, 512:1024],
                           ps3.ap()[0:65, 0:512]][nch]
                    P.scalar.copy(sumrow[0:1, n0:n1], psO[64:65, 0:n1 - n0],
                                  inc=[("act", 1)])
                P.wait("vector", "act")
                P.vector.reciprocal(sumrow[0:1, 0:QS], sumrow[0:1, 0:QS], inc=[("dve", 1)])
                P.wait("tensor", "dve")
                for nch in range(3):
                    n0 = 512 * nch
                    n1 = min(512 * (nch + 1), QS)
                    P.wait("tensor", "act")
                    P.tensor.matmul(ps1.ap()[0:64, 512:512 + n1 - n0],
                                    onesr[0:1, 0:64], sumrow[0:1, n0:n1],
                                    start=True, stop=True, inc=[("pe", 1)])
                    P.wait("scalar", "pe")
                    P.scalar.copy(recb[:, n0:n1], ps1.ap()[0:64, 512:512 + n1 - n0],
                                  inc=[("act", 1)])
                    P.wait("vector", "act")
                    psO = [ps2.ap()[0:65, 0:512], ps2.ap()[0:65, 512:1024],
                           ps3.ap()[0:65, 0:512]][nch]
                    P.vector.tensor_tensor(oT[et][oo:oo + 64, n0:n1], psO[0:64, 0:n1 - n0],
                                           recb[:, n0:n1], ALU.mult, inc=[("dve", 1)])
                    P.mark(f"oev{h}_{nch}")
            P.mark("attn")

            P.wmark("tensor", "attn", "dve")
            for st in range(11):
                n0 = 128 * st
                psl = ps0 if st % 2 == 0 else ps1
                if st >= 2:
                    P.wmark("tensor", f"oproj{st - 2}", "dve")
                for hf in range(2):
                    for kt in range(4):
                        P.tensor.matmul(psl[:, 512 * hf:512 * (hf + 1)],
                                        oT[kt][:, n0:n0 + 128],
                                        wout_b[kt][:, 512 * hf:512 * (hf + 1)],
                                        start=(kt == 0), stop=(kt == 3),
                                        inc=[("pe", 1)] if kt == 3 else ())
                P.wait("gpsimd", "ste")
                P.gpsimd.dma_start(out=stat[:, 15:16], in_=qlog_dram[n0:n0 + 128, :],
                                   inc=[("gz", 16)]) if False else None
                P.sync.dma_start(out=stat[:, 15:16], in_=qlog_dram[n0:n0 + 128, :],
                                 inc=[("ld", 16)])
                P.wait("scalar", "ld")
                P.scalar.activation(stat[:, 14:15], stat[:, 15:16], AF.Exp, scale=-1.0,
                                    inc=[("act", 1)])
                P.wait("vector", "act")
                P.vector.tensor_scalar(stat[:, 13:14], stat[:, 14:15], 1.0, None, ALU.add,
                                       inc=[("dve", 1)])
                dve_edge()
                P.vector.reciprocal(stat[:, 12:13], stat[:, 13:14], inc=[("dve", 1)])
                P.wait("vector", "pe")
                dve_edge()
                buf = gen_tok if st % 2 == 0 else a2(38848, 1024)
                if st >= 2:
                    P.wait("vector", "so0" if st % 2 == 0 else "so1")
                P.vector.tensor_scalar(buf[:], psl[:], stat[:, 12:13], None, ALU.mult,
                                       inc=[("dve", 1)])
                P.wait("sync", "dve")
                P.sync.dma_start(out=o_bi[n0:n0 + 128, :], in_=buf[:],
                                 inc=[("so0" if st % 2 == 0 else "so1", 16)])
                P.mark(f"ost{st}")
                P.mark(f"oproj{st}")
            P.wait("gpsimd", "so0")
            P.wait("gpsimd", "so1")
            P.gpsimd.collective_compute("AllGather", ALU.bypass, replica_groups=RG,
                                        ins=[o_bi.ap().opt()], outs=[o_bo.ap().opt()],
                                        inc=[("cc", 1)])

            dve_edge()
            P.vector.tensor_scalar(stat[:, 11:12], WOFFB, float(QCAP) / 2048.0, None,
                                   ALU.mult, inc=[("dve", 1)])
            P.wait("tensor", "dve")
            P.tensor.matmul(ps2[0:1, 52:64], cnt32[:], combsel[:], start=True, stop=True,
                            inc=[("pe", 1)])
            P.wait("scalar", "pe")
            P.scalar.copy(rowb[0:1, 40:52], ps2[0:1, 52:64], inc=[("act", 1)])
            P.wait("tensor", "act")
            P.tensor.matmul(ps2[:, 52:64], onesr[:], rowb[0:1, 40:52], start=True,
                            stop=True, inc=[("pe", 1)])
            P.wait("vector", "pe")
            P.vector.tensor_copy(cm_f[:], ps2[:, 52:64], inc=[("dve", 1)])
            dve_edge()
            P.vector.tensor_scalar(f1[:, 0:12], kbase12[:], iota_c[:], None, ALU.add,
                                   inc=[("dve", 1)])
            dve_edge()
            P.vector.tensor_tensor(f1[:, 12:24], f1[:, 0:12], cm_f[:], ALU.is_lt,
                                   inc=[("dve", 1)])
            dve_edge()
            P.vector.tensor_scalar(f1[:, 24:36], kbase12[:], iota_c[:], stat[:, 11:12],
                                   ALU.add, op1=ALU.add, inc=[("dve", 1)])
            for gg in range(4):
                dve_edge()
                P.vector.tensor_scalar(f1[:, 36 + 3 * gg:39 + 3 * gg],
                                       f1[:, 24 + 3 * gg:27 + 3 * gg],
                                       float(QS * gg), None, ALU.add, inc=[("dve", 1)])
            dve_edge()
            P.vector.tensor_tensor(f1[:, 48:60], f1[:, 36:48], f1[:, 12:24], ALU.mult,
                                   inc=[("dve", 1)])
            dve_edge()
            P.vector.tensor_scalar(f1[:, 60:72], f1[:, 12:24], 1.0, -BIGIDX,
                                   ALU.subtract, op1=ALU.mult, inc=[("dve", 1)])
            dve_edge()
            P.vector.tensor_tensor(f1[:, 48:60], f1[:, 48:60], f1[:, 60:72], ALU.add,
                                   inc=[("dve", 1)])
            dve_edge()
            P.vector.tensor_copy(cm_i[:], f1[:, 48:60], inc=[("dve", 1)])
            P.wait("tensor", "dve")
            for j in range(12):
                k3 = j % 3
                psl = ps0 if j % 2 == 0 else ps1
                if j >= 2:
                    P.wmark("tensor", f"cidx{j - 2}", "dve")
                P.tensor.matmul(psl[0:1, 0:128], combsel2[:, j:j + 1],
                                possb[0:32, 128 * k3:128 * (k3 + 1)],
                                start=True, stop=True, inc=[("pe", 1)])
                P.wait("scalar", "pe")
                P.scalar.copy(logits_sb[0:1, 128 * (j % 8):128 * (j % 8) + 128],
                              psl[0:1, 0:128], inc=[("act", 1)])
                P.wait("tensor", "act")
                P.tensor.matmul(psl[:, 128:129],
                                logits_sb[0:1, 128 * (j % 8):128 * (j % 8) + 128],
                                onescf[0:1, 0:1], start=True, stop=True, inc=[("pe", 1)])
                P.wait("vector", "pe")
                P.vector.tensor_scalar(f1[:, 72 + j:73 + j], psl[:, 128:129], WOFFB, None,
                                       ALU.subtract, inc=[("dve", 1)])
                P.mark(f"cidx{j}")
            dve_edge()
            P.vector.tensor_tensor(f1[:, 84:96], f1[:, 72:84], f1[:, 12:24], ALU.mult,
                                   inc=[("dve", 1)])
            dve_edge()
            P.vector.tensor_tensor(f1[:, 84:96], f1[:, 84:96], f1[:, 60:72], ALU.add,
                                   inc=[("dve", 1)])
            dve_edge()
            P.vector.tensor_scalar(f1[:, 96:108], f1[:, 84:96], 0.0, None, ALU.is_lt,
                                   inc=[("dve", 1)])
            dve_edge()
            P.vector.scalar_tensor_tensor(f1[:, 84:96], f1[:, 96:108], 2.0 * BIGIDX,
                                          f1[:, 84:96], ALU.mult, ALU.add,
                                          inc=[("dve", 1)])
            dve_edge()
            P.vector.tensor_copy(cm2_i[:], f1[:, 84:96], inc=[("dve", 1)])
            P.mark("cmidx")
            P.wait("gpsimd", "cc", 5)
            P.wmark("gpsimd", "cmidx", "dve")
            ob = [a2(20544 + 1024 * j, 1024) for j in range(12)]
            for j in range(12):
                P.gpsimd.memset(ob[j][:], 0.0, inc=[("pool", 1)])
            P.wait("gpsimd", "pool")
            for j in range(12):
                gqs = ["gq", "gqb", "gqc"][j % 3]
                P.gpsimd.indirect_dma_start(
                    out=ob[j][:], out_offset=None,
                    in_=o_bo[:],
                    in_offset=IndirectOffsetOnAxis(cm_i[:, j:j + 1], 0),
                    bounds_check=4 * QS - 1, oob_is_err=False, inc=[(gqs, 16)])
            P.wait("sync", "pe")
            P.wait("sync", "dve")
            iwb = a4(2560, 2048)
            P.sync.dma_start(out=iwb[:], in_=IWINBC[:], inc=[("lw", 16)])
            P.wait("vector", "pe")
            P.wait("vector", "lw")
            P.wait("vector", "so0")
            P.wait("vector", "so1")
            Ebuf = [a2(2048 * j, 2048) for j in range(10)] + [a2(38848, 2048),
                                                             a2(40896, 2048)]
            for j in range(12):
                dve_edge()
                P.vector.tensor_scalar(Ebuf[j][:], iwb[:], f1[:, 84 + j:85 + j], None,
                                       ALU.is_equal, inc=[("dve", 1)])
            P.mark("ebld")
            P.wait("sync", "dve")
            P.wait("sync", "act")
            P.sync.dma_start(out=nullb[:], in_=null_in[:, 0:1024], inc=[("ld", 16)])
            P.wmark("tensor", "ebld", "dve")
            P.wait("tensor", "act")
            for sm in ["gq", "gqb", "gqc"]:
                P.wait("tensor", sm)
            fbuf = [f1, ocomb[0]]
            for i in range(NT):
                r0 = 128 * i
                psl = ps0 if i % 2 == 0 else ps1
                fb = fbuf[i % 2]
                if i < 2:
                    P.wait("sync", "dve")
                else:
                    P.wmark("sync", f"cdve{i - 2}", "dve")
                P.sync.dma_start(out=xs[i % 2][:], in_=y1_dram[r0:r0 + 128, :],
                                 inc=[("lx", 16)])
                if i >= 2:
                    P.wmark("tensor", f"cdve{i - 2}", "dve")
                for j in range(12):
                    for h in range(2):
                        P.tensor.matmul(psl[:, 512 * h:512 * (h + 1)],
                                        Ebuf[j][:, r0:r0 + 128],
                                        ob[j][:, 512 * h:512 * (h + 1)],
                                        start=(j == 0), stop=(j == 11),
                                        inc=[("pe", 1)] if (j == 11 and h == 1) else ())
                P.mark(f"cmm{i}")
                P.wmark("vector", f"cmm{i}", "pe")
                P.wait("vector", "lx")
                P.wait("vector", "ld") if i == 0 else None
                if i >= 2:
                    P.wmark("vector", f"cmb{i - 2}", "so0" if i % 2 == 0 else "so1")
                dve_edge()
                P.vector.tensor_reduce(stat[:, 0:1], selr[:, 16 * i:16 * i + 4], AX.X,
                                       ALU.add, inc=[("dve", 1)])
                dve_edge()
                P.vector.tensor_scalar(stat[:, 1:2], stat[:, 0:1], 1.0, None, ALU.max,
                                       inc=[("dve", 1)])
                dve_edge()
                P.vector.reciprocal(stat[:, 2:3], stat[:, 1:2], inc=[("dve", 1)])
                dve_edge()
                P.vector.tensor_scalar(stat[:, 3:4], stat[:, 0:1], 0.0, None, ALU.is_le,
                                       inc=[("dve", 1)])
                dve_edge()
                P.vector.scalar_tensor_tensor(fb[:], psl[:], stat[:, 2:3],
                                              xs[i % 2][:], ALU.mult, ALU.add,
                                              inc=[("dve", 1)])
                dve_edge()
                P.vector.scalar_tensor_tensor(fb[:], nullb[:], stat[:, 3:4], fb[:],
                                              ALU.mult, ALU.add, inc=[("dve", 1)])
                P.mark(f"cdve{i}")
                P.wait("sync", "dve")
                P.sync.dma_start(out=out_y[r0:r0 + 128, :], in_=fb[:],
                                 inc=[("so0" if i % 2 == 0 else "so1", 16)])
                P.mark(f"cmb{i}")
            for sm in ["ste", "str", "stz", "stw"]:
                P.wait("sync", sm)
            for sm in ["gp", "gs", "gz", "gz1", "gq", "gqb", "gqc", "sw", "pool", "so0",
                       "so1", "dbg2", "ld", "lt", "lx", "lr", "lw", "dbg"]:
                P.wait("gpsimd", sm)
            P.wait("scalar", "act")
            P.wait("vector", "dve")
            P.wait("tensor", "pe")
            return


        with nc.Block() as block:
            for eng in Prog.ENGINES:
                def make(e):
                    def f(_):
                        P.active = e
                        P.reset()
                        program()
                    return f
                getattr(block, eng)(make(eng))

    return nc


def _consts():
    c = {}
    c["DIFFS_T"] = np.maximum(np.arange(C)[None, :] - 1 - np.arange(C)[:, None], 0) \
        .astype(np.float32)
    c["CAUSAL_T"] = (np.arange(C)[None, :] > np.arange(C)[:, None]).astype(np.float32)
    c["IMODBC"] = np.repeat((np.arange(WIN) % C)[None, :].astype(np.float32),
                             128, axis=0)
    c["IWINBC"] = np.repeat(np.arange(WIN, dtype=np.float32)[None, :], 128, axis=0)
    p = np.arange(128, dtype=np.float32)
    c["REV_COL"] = np.stack([C - 1 - 128 * tl - p for tl in range(4)], axis=1) \
        .astype(np.float32)
    c["TRIS_F"] = (np.arange(128)[:, None] < np.arange(128)[None, :]).astype(np.float32)
    c["ONES_ROW_F"] = np.ones((1, 128), np.float32)
    c["ONES_COL_B"] = np.ones((128, 1), ml_dtypes.bfloat16)
    c["ONES_COL_F"] = np.ones((128, 1), np.float32)
    c["SEL16"] = (np.arange(128)[:, None] // 16 == np.arange(8)[None, :]).astype(np.float32)
    c["SEL16T"] = c["SEL16"].T.copy()
    iw = (np.arange(128) % 16) // 4
    ii = np.arange(128) // 16
    c["SELW32"] = (ii[:, None] * 4 + iw[:, None] == np.arange(32)[None, :]).astype(np.float32)
    c["NQCAPC"] = np.repeat(np.array([[1024.0] * 4 + [512.0] * 4], np.float32), 128, axis=0)
    c["TGT"] = np.where(np.arange(128)[:, None] < 64, 1024.0, 512.0).astype(np.float32)
    c["IBLK4"] = np.repeat((np.arange(32) % 4)[None, :].astype(np.float32), 128, axis=0)
    c["IBLK128"] = np.repeat((np.arange(1024) % 128)[None, :].astype(ml_dtypes.bfloat16),
                             128, axis=0)
    c["IOTA_COL"] = np.arange(128, dtype=np.float32)[:, None]
    c["IDENT8"] = np.eye(8, dtype=np.float32)
    c["IDENT32"] = np.eye(32, dtype=np.float32)
    return c


_CONSTS = _consts()


def build_in_maps(L0_attn_out, L0_attn_W_write, L0_attn_W_read, L0_attn_decay, L0_attn_log_alpha,
           L0_ffn_w_qr, L0_ffn_w_kvr, L0_ffn_w_q, L0_ffn_w_kv, L0_ffn_w_out,
           L0_ffn_null_kv, L0_ffn_null_token):
    x = np.asarray(L0_attn_out, np.float32)
    Ww = np.asarray(L0_attn_W_write, np.float32)
    Wr = np.asarray(L0_attn_W_read, np.float32)
    w8 = np.concatenate([np.asarray(L0_ffn_w_qr, np.float32).T,
                         np.asarray(L0_ffn_w_kvr, np.float32).T], axis=1)
    wq = np.asarray(L0_ffn_w_q, np.float32)
    wkv = np.asarray(L0_ffn_w_kv, np.float32)
    wout = np.asarray(L0_ffn_w_out, np.float32)
    nkv = np.asarray(L0_ffn_null_kv, np.float32)
    ntok = np.asarray(L0_ffn_null_token, np.float32).reshape(1, D)

    in_maps = []
    for cid in range(8):
        b, w = cid // 4, cid % 4
        g = w
        xw = np.zeros((WIN + 1, D), np.float32)
        lo = 2048 * w
        if w == 0:
            xw[1:] = x[b, 0:2048]
        else:
            xw[:] = x[b, lo - 1:lo + 2048]
        expw = np.array([[max(2048.0 * (w - 1 - j), 0.0) for j in range(4)]], np.float32)
        maskw = np.array([[1.0 if j < w else 0.0 for j in range(4)]], np.float32)
        pv = np.zeros((32, 8), np.float32)
        for i in range(8):
            for wp in range(w):
                pv[i * 4 + wp, i] = 1.0
        nv = np.concatenate([nkv[1, g].reshape(8, 64), np.ones((8, 1), np.float32)],
                            axis=1).reshape(1, 8 * 65)
        possel = np.zeros((32, 20), np.float32)
        for j in range(12):
            possel[8 * (j // 3) % 32, 0] += 0
            possel[(8 * (j // 3) + g) % 32, j] = 1.0
        for j in range(12, 20):
            possel[(8 * ((j - 12) // 2) + 4 + g) % 32, j] = 1.0
        gselq_a = np.zeros((128, 8), np.float32)
        gselq_a[:, g] = 1.0
        gselk_a = np.zeros((128, 8), np.float32)
        gselk_a[:, 4 + g] = 1.0
        kbase20 = np.zeros((128, 20), np.float32)
        for j in range(12):
            kbase20[:, j] = 128.0 * (j % 3)
        for j in range(12, 20):
            kbase20[:, j] = 128.0 * ((j - 12) % 2)
        combsel_a = np.zeros((32, 12), np.float32)
        combsel2_a = np.zeros((32, 12), np.float32)
        for j in range(12):
            gp_ = j // 3
            combsel_a[4 * gp_ + w, j] = 1.0
            combsel2_a[8 * w + gp_, j] = 1.0
        kbase12 = np.repeat((128.0 * (np.arange(12) % 3))[None, :].astype(np.float32),
                            128, axis=0)
        prefixw_a = np.zeros((32, 32), np.float32)
        for i in range(8):
            for wnum in range(4):
                for wp in range(wnum):
                    prefixw_a[4 * i + wp, 4 * i + wnum] = 1.0
        nq32_a = np.array([[1024.0] * 16 + [512.0] * 16], np.float32).reshape(32, 1)
        cntsel_a = np.zeros((32, 20), np.float32)
        for j in range(12):
            cntsel_a[4 * g + j // 3, j] = 1.0
        for j in range(12, 20):
            cntsel_a[16 + 4 * g + (j - 12) // 2, j] = 1.0
        m = dict(
            x_win=xw, WwT=Ww.T.copy(), WrT=Wr.T.copy(),
            decay=np.asarray(L0_attn_decay, np.float32).reshape(1, 1),
            log_alpha=np.asarray(L0_attn_log_alpha, np.float32).reshape(1, 1),
            w8in=w8.copy(), wqT=wq[g].T.copy(), wkvT=wkv[g].T.copy(),
            woutT=wout[g].T.copy(),
            nk_in=nkv[0, g].reshape(4, 128).T.copy(),
            nv_in=nv,
            null_in=np.repeat(ntok, 128, axis=0),
            EXPW=expw, MASKW=maskw, PREVW=pv,
            WOFF=np.array([[2048.0 * w]], np.float32),
            POSSEL=possel, GSELQ=gselq_a, GSELK=gselk_a, KBASE20=kbase20,
            COMBSEL=combsel_a, COMBSEL2=combsel2_a, KBASE12=kbase12,
            PREFIXW=prefixw_a, NQ32=nq32_a, CNTSEL=cntsel_a,
            **_CONSTS,
        )
        in_maps.append(m)

    return in_maps


def kernel(**inputs):
    nc = build_nc()
    in_maps = build_in_maps(**inputs)
    res = run_bass_kernel_spmd(nc, in_maps, core_ids=list(range(8)))
    global _LAST_RES
    _LAST_RES = res
    out = np.zeros((B, T, D), np.float32)
    for cid in range(8):
        b, w = cid // 4, cid % 4
        out[b, 2048 * w:2048 * (w + 1)] = res.results[cid]["out_y"]
    return out
